# version 02
# baseline (speedup 1.0000x reference)
import math
import numpy as np
import ml_dtypes
import concourse.bass as bass
import concourse.mybir as mybir

F32 = mybir.dt.float32
BF16 = mybir.dt.bfloat16
I32 = mybir.dt.int32
AF = mybir.ActivationFunctionType
ALU = mybir.AluOpType
AX = mybir.AxisListType
P = 128


class _Inst:
    __slots__ = ("rec",)
    def __init__(self, rec):
        self.rec = rec
    def then_inc(self, sem, v=1):
        self.rec["incs"].append((sem, v))
        return self

class EngRec:
    def __init__(self, name):
        self.name = name
        self.ops = []
    def op(self, method, *a, **kw):
        rec = {"m": method, "a": a, "kw": kw, "incs": []}
        self.ops.append(rec)
        return _Inst(rec)
    def raw(self, fn):
        self.ops.append({"m": None, "fn": fn})
    def __getattr__(self, name):
        if name.startswith("_"):
            raise AttributeError(name)
        def call(*a, **kw):
            return self.op(name, *a, **kw)
        return call
    def replay(self, eng):
        for rec in self.ops:
            if rec["m"] is None:
                rec["fn"](eng)
                continue
            ins = getattr(eng, rec["m"])(*rec["a"], **rec["kw"])
            for sem, v in rec["incs"]:
                ins = ins.then_inc(sem, v)

class Sem:
    def __init__(self, handle, name=""):
        self.h = handle
        self.n = 0
        self.name = name
    def inc(self, k=1):
        self.n += k
        return self.n

def inc(inst, sem, k=1):
    inst.then_inc(sem.h, k)
    sem.inc(k)
    return sem.n

class Ctx:
    def __init__(self, nc):
        self.nc = nc
        self.T = EngRec("tensor")
        self.V = EngRec("vector")
        self.S = EngRec("scalar")
        self.G = EngRec("gpsimd")
        self.Y = EngRec("sync")
        self.sems = {}
        self._es = None
    def sem(self, name):
        if name not in self.sems:
            self.sems[name] = Sem(self._es.enter_context(self.nc.semaphore(name)), name)
        return self.sems[name]
    def sb(self, name, shape, dt):
        return self._es.enter_context(self.nc.sbuf_tensor(name, shape, dt))
    def asb(self, name, shape, dt):
        ar = getattr(self, "arena", None)
        if ar is None:
            return self.sb(name, shape, dt)
        return ar.alloc(name, shape, dt)
    def ps(self, name, shape, dt):
        return self._es.enter_context(self.nc.psum_tensor(name, shape, dt))

def finalize(ctx):
    nc = ctx.nc
    with nc.Block() as block:
        @block.sync
        def _(eng):
            ctx.Y.replay(eng)
        @block.tensor
        def _(eng):
            ctx.T.replay(eng)
        @block.vector
        def _(eng):
            ctx.V.replay(eng)
        @block.scalar
        def _(eng):
            ctx.S.replay(eng)
        @block.gpsimd
        def _(eng):
            ctx.G.replay(eng)


def host_blk_lhsT(wT):
    Din, Dout = wT.shape
    n_kt, n_m = Din // P, Dout // P
    w4 = wT.reshape(n_kt, P, n_m, P)
    return np.ascontiguousarray(w4.transpose(2, 1, 0, 3).reshape(n_m, P, n_kt * P))

def ceil_div(a, b):
    return (a + b - 1) // b


def _branched_dma(E, sem_h, out=None, in_=None, pair=None, store=False):
    a0, a1 = pair
    def fn(eng):
        with eng.If_eq(eng._slot_reg, 0):
            if store:
                eng.dma_start(out=a0, in_=in_).then_inc(sem_h, 16)
            else:
                eng.dma_start(out=out, in_=a0).then_inc(sem_h, 16)
        with eng.Else():
            if store:
                eng.dma_start(out=a1, in_=in_).then_inc(sem_h, 16)
            else:
                eng.dma_start(out=out, in_=a1).then_inc(sem_h, 16)
    E.raw(fn)


class XV:
    def __init__(self, ap=None, shared_ap=None, pad=512, ext=False, row0=0):
        self.ap = ap
        self.sh = shared_ap
        self.pad = pad
        self.ext = ext
        self.row0 = row0
    def sl(self, r0, r1, c0=0, c1=2048):
        if self.ap is not None:
            return self.ap[self.row0 + r0:self.row0 + r1, c0:c1]
        if self.ext:
            return (self.sh[r0:r1, c0:c1], self.sh[2048 + r0:2048 + r1, c0:c1])
        return (self.sh[self.pad + r0:self.pad + r1, c0:c1],
                self.sh[self.pad + 2048 + r0:self.pad + 2048 + r1, c0:c1])


class Slot:
    def __init__(self, ctx, name):
        self.ld = ctx.sem(name + "_ld")
        self.free_sem = None
        self.free_val = 0
        self.nload = 0
    def load(self, E, pairs):
        if self.free_sem is not None:
            E.wait_ge(self.free_sem.h, self.free_val)
        self.nload += 1
        for out, in_ in pairs:
            if isinstance(in_, tuple):
                _branched_dma(E, self.ld.h, out=out, pair=in_)
                self.ld.inc(16)
            else:
                inc(E.dma_start(out=out, in_=in_), self.ld, 16)
    def wait(self, E):
        E.wait_ge(self.ld.h, self.ld.n)
    def set_free(self, sem, val):
        self.free_sem, self.free_val = sem, val


def store_x(E, sem, out_sel, in_sb):
    if isinstance(out_sel, tuple):
        _branched_dma(E, sem.h, in_=in_sb, pair=out_sel, store=True)
        sem.inc(16)
    else:
        inc(E.dma_start(out=out_sel, in_=in_sb), sem, 16)


def emit_join(ctx, tag=""):
    engs = [ctx.T, ctx.V, ctx.S, ctx.G, ctx.Y]
    names = ["jT", "jV", "jS", "jG", "jY"]
    vals = {}
    for E, nm in zip(engs, names):
        s = ctx.sem(nm)
        vals[nm] = inc(E.nop(), s)
    for E in engs:
        for nm in names:
            E.wait_ge(ctx.sems[nm].h, vals[nm])


def emit_slot_regs(ctx, slot_in_ap):
    sl_sb = ctx.sb("slot_sb", [1, 1], I32)
    s = ctx.sem("slotld")
    inc(ctx.Y.dma_start(out=sl_sb[:], in_=slot_in_ap), s, 16)
    for E in (ctx.Y, ctx.G):
        E.wait_ge(s.h, s.n)
        def fn(eng):
            eng._slot_reg = eng.alloc_register()
            eng.reg_load(eng._slot_reg, sl_sb[0:1, 0:1])
        E.raw(fn)


def flag_magics(k):
    m = (k * 2654435761) % (2 ** 31 - 7) + 3
    c = (m ^ 0x5A5A5A5A) & 0x7FFFFFFF
    return m, c


def flag_const_array():
    a = np.zeros((1, 16), np.int32)
    for k in range(1, 9):
        m, c = flag_magics(k)
        a[0, k - 1] = m
        a[0, 8 + k - 1] = c
    return a


def emit_flag_write(ctx, flags_ap, kconst_sb, k):
    G = ctx.G
    s = ctx.sem("barw")
    _branched_dma(G, s.h, in_=kconst_sb[0:1, k - 1:k],
                  pair=(flags_ap[0:1, k - 1:k], flags_ap[1:2, k - 1:k]), store=True)
    s.inc(16)
    _branched_dma(G, s.h, in_=kconst_sb[0:1, 7 + k:8 + k],
                  pair=(flags_ap[0:1, 7 + k:8 + k], flags_ap[1:2, 7 + k:8 + k]), store=True)
    s.inc(16)
    G.wait_ge(s.h, s.n)


def emit_flag_poll(ctx, flags_ap, k, pollbuf, tag=""):
    G = ctx.G
    bar_ok = ctx.sem("barok")
    pl = ctx.sem("barpl" + tag)
    m, c = flag_magics(k)

    def fn(eng):
        rf1 = eng.alloc_register()
        rf2 = eng.alloc_register()
        rcnt = eng.alloc_register()
        rnd = eng.alloc_register()
        eng.reg_mov(rcnt, 0)
        eng.reg_mov(rnd, 1)
        with eng.While(rnd):
            eng.reg_add(rcnt, rcnt, 32)
            with eng.If_eq(eng._slot_reg, 0):
                eng.dma_start(out=pollbuf[0:1, 0:1], in_=flags_ap[1:2, k - 1:k]).then_inc(pl.h, 16)
                eng.dma_start(out=pollbuf[0:1, 1:2], in_=flags_ap[1:2, 7 + k:8 + k]).then_inc(pl.h, 16)
            with eng.Else():
                eng.dma_start(out=pollbuf[0:1, 0:1], in_=flags_ap[0:1, k - 1:k]).then_inc(pl.h, 16)
                eng.dma_start(out=pollbuf[0:1, 1:2], in_=flags_ap[0:1, 7 + k:8 + k]).then_inc(pl.h, 16)
            eng.wait_ge(pl.h, rcnt)
            eng.reg_load(rf1, pollbuf[0:1, 0:1])
            eng.reg_load(rf2, pollbuf[0:1, 1:2])
            with eng.If_eq(rf1, m):
                with eng.If_eq(rf2, c):
                    eng.reg_mov(rnd, 0)
    G.raw(fn)
    return inc(G.nop(), bar_ok)


DT_SIZE_MAP = {F32: 4, BF16: 2, I32: 4}


class Arena:
    def __init__(self, ctx, nbytes):
        self.ctx = ctx
        self.t = ctx.sb("arena", [P, nbytes], mybir.dt.uint8)
        self.base = ctx.nc.lookup_mloc(self.t).addr
        self.nbytes = nbytes
        self.off = 0
        self.peak = 0
    def reset(self):
        self.off = 0
    def alloc(self, name, shape, dt):
        self.off = (self.off + 31) // 32 * 32
        h = self.ctx.nc.alloc_sbuf_tensor_at(name, shape, dt, offset=self.base + self.off)
        self.off += shape[1] * DT_SIZE_MAP[dt]
        self.peak = max(self.peak, self.off)
        assert self.off <= self.nbytes, "arena overflow at " + name
        return h



def emit_ln_transpose(ctx, bufs, xv, n_tiles, dst_sb, dst_tc, dst_col0=0,
                      eps=1e-5, row_base=0):
    T, V, S, Y = ctx.T, ctx.V, ctx.S, ctx.Y
    ve = ctx.sem("ve"); se = ctx.sem("se"); mm = ctx.sem("mm"); frp = ctx.sem("frp")
    D = 2048
    for i in range(n_tiles):
        par = i % len(bufs["xa"])
        xa = bufs["xa"][par]; xn = bufs["xn"][par]
        sl = bufs["xa_slot"][par]
        r0 = row_base + i * P
        sl.load(Y, [(xa[:], xv.sl(r0, r0 + P))])
        sl.wait(V)
        nV = inc(V.reduce_sum(bufs["s1"][:], xa[:], axis=AX.X), ve)
        sl.wait(S)
        nS = inc(S.activation(bufs["sq"][:], xa[:], AF.Square, accum_out=bufs["ss"][:]), se)
        V.wait_ge(se.h, nS)
        V.wait_ge(ve.h, nV)
        nV = inc(V.tensor_scalar(bufs["va"][:], bufs["ss"][:], 1.0 / D, None, ALU.mult), ve)
        nV = inc(V.tensor_scalar(bufs["vb"][:], bufs["s1"][:], 1.0 / D, None, ALU.mult), ve)
        V.wait_ge(ve.h, nV)
        nV = inc(V.tensor_tensor(bufs["vc"][:], bufs["vb"][:], bufs["vb"][:], ALU.mult), ve)
        V.wait_ge(ve.h, nV)
        nV = inc(V.tensor_tensor(bufs["vc"][:], bufs["va"][:], bufs["vc"][:], ALU.subtract), ve)
        V.wait_ge(ve.h, nV)
        nV = inc(V.tensor_scalar(bufs["vc"][:], bufs["vc"][:], eps, None, ALU.add), ve)
        S.wait_ge(ve.h, nV)
        nS = inc(S.activation(bufs["lnv"][:], bufs["vc"][:], AF.Ln), se)
        S.wait_ge(se.h, nS)
        nS = inc(S.activation(bufs["rstd"][:], bufs["lnv"][:], AF.Exp, scale=-0.5), se)
        V.wait_ge(se.h, nS)
        nV = inc(V.tensor_tensor(bufs["nmr"][:], bufs["vb"][:], bufs["rstd"][:], ALU.mult), ve)
        V.wait_ge(ve.h, nV)
        nV = inc(V.tensor_scalar(bufs["nmr"][:], bufs["nmr"][:], -1.0, None, ALU.mult), ve)
        S.wait_ge(ve.h, nV)
        nS = inc(S.activation(xn[:], xa[:], AF.Identity, bias=bufs["nmr"][:], scale=bufs["rstd"][:]), se)
        sl.set_free(se, nS)
        T.wait_ge(se.h, nS)
        if bufs["psbf_busy"]:
            T.wait_ge(frp.h, bufs["psbf_busy"])
        for kt in range(16):
            ins = T.transpose(bufs["psbf"][:, kt * P:(kt + 1) * P],
                              xn[:, kt * P:(kt + 1) * P], bufs["ident_bf"][:])
        nT = inc(ins, mm)
        V.wait_ge(mm.h, nT)
        for kt in range(16):
            dsl = dst_sb[:, kt * dst_tc + dst_col0 + i * P: kt * dst_tc + dst_col0 + (i + 1) * P]
            ins = V.tensor_copy(dsl, bufs["psbf"][:, kt * P:(kt + 1) * P])
        inc(ins, frp)
        bufs["psbf_busy"] = frp.n
    return ctx.sem("frp").n


def alloc_ln_bufs(ctx, tag="", nx=2):
    b = {}
    b["xa"] = [ctx.sb(f"ln_xa{tag}{j}", [P, 2048], F32) for j in range(nx)]
    b["xn"] = [ctx.sb(f"ln_xn{tag}{j}", [P, 2048], BF16) for j in range(nx)]
    b["xa_slot"] = [Slot(ctx, f"lnxa{tag}{j}") for j in range(nx)]
    b["sq"] = ctx.sb(f"ln_sq{tag}", [P, 2048], F32)
    for nm in ["s1", "ss", "va", "vb", "vc", "rstd", "lnv", "nmr"]:
        b[nm] = ctx.sb(f"ln_{nm}{tag}", [P, 1], F32)
    b["psbf_busy"] = 0
    return b


def emit_ffn0_body(ctx, xv_in, xv_out, wgt, wut, wdt, n_rows, Tc, residual_scale, lnb, b=None):
    T, V, S, Y, G = ctx.T, ctx.V, ctx.S, ctx.Y, ctx.G
    mm = ctx.sem("mm"); ve = ctx.sem("ve"); se = ctx.sem("se")
    evf = ctx.sem("evf"); frp = ctx.sem("frp")
    n_chunks = n_rows // Tc
    nt = Tc // P
    NOT = 43
    assert Tc == 512
    if b is None:
        b = ctx.bufs
    xnT, h_sb = b["xnT"], b["h_sb"]

    for c in range(n_chunks):
        ln_done = emit_ln_transpose(ctx, lnb, xv_in, nt, xnT, Tc, row_base=c * Tc)
        evf_base = evf.n
        for m in range(NOT):
            par = m % 2
            wsl = b["wgu_slot"][par]
            wsl.load(Y, [(b["wgbuf"][par][:], wgt[m, :, :]),
                         (b["wubuf"][par][:], wut[m, :, :])])
            wsl.wait(T)
            if m == 0:
                T.wait_ge(frp.h, ln_done)
            if m >= 2:
                T.wait_ge(evf.h, evf_base + (m - 2) + 1)
            pg = b["pgu"][2 * par]; pu = b["pgu"][2 * par + 1]
            for kt in range(16):
                T.matmul(pg[:], b["wgbuf"][par][:, kt * P:(kt + 1) * P],
                         xnT[:, kt * Tc:(kt + 1) * Tc], start=(kt == 0), stop=(kt == 15))
            for kt in range(16):
                ins = T.matmul(pu[:], b["wubuf"][par][:, kt * P:(kt + 1) * P],
                               xnT[:, kt * Tc:(kt + 1) * Tc], start=(kt == 0), stop=(kt == 15))
            nT = inc(ins, mm)
            wsl.set_free(mm, nT)
            S.wait_ge(mm.h, nT)
            nS = inc(S.activation(b["sgbuf"][par][:], pg[:], AF.Silu), se)
            V.wait_ge(se.h, nS)
            ins = V.tensor_tensor(h_sb[:, m * Tc:(m + 1) * Tc], b["sgbuf"][par][:], pu[:], ALU.mult)
            inc(ins, evf)
        h_done = evf.n
        dnev_vals = []
        for dc in range(8):
            par = dc % 2
            wsl = b["wd_slot"][par]
            wsl.load(Y, [(b["wdbuf"][par][:],
                          wdt[:, :, dc * 256:(dc + 1) * 256].rearrange("o p f -> p o f"))])
            wsl.wait(T)
            if dc == 0:
                T.wait_ge(evf.h, h_done)
            if dc >= 1:
                T.wait_ge(ve.h, dnev_vals[dc - 1])
            for ot in range(NOT):
                for tt in range(nt):
                    psl = b["pdn"][tt // 2][:, (tt % 2) * 256:(tt % 2) * 256 + 256]
                    ins = T.matmul(psl, h_sb[:, ot * Tc + tt * P: ot * Tc + (tt + 1) * P],
                                   b["wdbuf"][par][:, ot * 256:(ot + 1) * 256],
                                   start=(ot == 0 and tt % 2 == 0),
                                   stop=(ot == NOT - 1 and tt % 2 == 1))
            nT = inc(ins, mm)
            wsl.set_free(mm, nT)
            xsl = b["xrm_slot"][par]
            xsl.load(Y, [(b["xrm"][par][tt][:],
                          xv_in.sl(c * Tc + tt * P, c * Tc + (tt + 1) * P, dc * 256, (dc + 1) * 256))
                         for tt in range(nt)])
            V.wait_ge(mm.h, nT)
            xsl.wait(V)
            stx_p = ctx.sem("stx" + str(par))
            if stx_p.n > 0:
                V.wait_ge(stx_p.h, stx_p.n)
            for tt in range(nt):
                psl = b["pdn"][tt // 2][:, (tt % 2) * 256:(tt % 2) * 256 + 256]
                ins = V.scalar_tensor_tensor(b["xrw"][par][tt][:], psl, residual_scale,
                                             b["xrm"][par][tt][:], ALU.mult, ALU.add)
            nV = inc(ins, ve)
            xsl.set_free(ve, nV)
            dnev_vals.append(nV)
            G.wait_ge(ve.h, nV)
            for tt in range(nt):
                store_x(G, stx_p, xv_out.sl(c * Tc + tt * P, c * Tc + (tt + 1) * P,
                                            dc * 256, (dc + 1) * 256), b["xrw"][par][tt][:])


def alloc_ffn0_bufs(ctx, Tc=512, pool=None):
    NOT = 43
    b = {}
    b["xnT"] = ctx.asb("xnT", [P, 16 * Tc], BF16)
    b["h_sb"] = ctx.asb("h_sb", [P, NOT * Tc], BF16)
    b["wgbuf"] = [ctx.asb(f"wgbuf{j}", [P, 2048], BF16) for j in range(2)]
    b["wubuf"] = [ctx.asb(f"wubuf{j}", [P, 2048], BF16) for j in range(2)]
    b["wdbuf"] = [ctx.asb(f"wdbuf{j}", [P, NOT * 256], BF16) for j in range(2)]
    b["sgbuf"] = [ctx.asb(f"sgbuf{j}", [P, Tc], F32) for j in range(2)]
    b["xrm"] = [[ctx.asb(f"xrm{j}_{tt}", [P, 256], F32) for tt in range(4)] for j in range(2)]
    b["xrw"] = [[ctx.asb(f"xrw{j}_{tt}", [P, 256], F32) for tt in range(4)] for j in range(2)]
    b["wgu_slot"] = [Slot(ctx, f"wgu{j}") for j in range(2)]
    b["wd_slot"] = [Slot(ctx, f"wd{j}") for j in range(2)]
    b["xrm_slot"] = [Slot(ctx, f"xrm{j}") for j in range(2)]
    b["xrw_slot"] = [Slot(ctx, f"xrw{j}") for j in range(2)]
    if pool is None:
        b["pgu"] = [ctx.ps(f"pgu{j}", [P, 512], F32) for j in range(4)]
        b["pdn"] = [ctx.ps(f"pdn{j}", [P, 512], F32) for j in range(2)]
    else:
        b["pgu"] = pool[0:4]
        b["pdn"] = pool[4:6]
    return b


def build_ffn0(nc, n_rows=2048, Tc=512):
    import contextlib
    ctx = Ctx(nc)
    ctx._es = contextlib.ExitStack()
    x = nc.dram_tensor("x", [n_rows, 2048], F32, kind="ExternalInput").ap()
    wgt = nc.dram_tensor("wgt", [43, P, 2048], BF16, kind="ExternalInput").ap()
    wut = nc.dram_tensor("wut", [43, P, 2048], BF16, kind="ExternalInput").ap()
    wdt = nc.dram_tensor("wdt", [43, P, 2048], BF16, kind="ExternalInput").ap()
    identb = nc.dram_tensor("identb", [P, P], BF16, kind="ExternalInput").ap()
    xo = nc.dram_tensor("xo", [n_rows, 2048], F32, kind="ExternalOutput").ap()

    lnb = alloc_ln_bufs(ctx)
    lnb["ident_bf"] = ctx.sb("identb_sb", [P, P], BF16)
    lnb["psbf"] = ctx.ps("psbf", [P, 2048], BF16)
    ctx.bufs = alloc_ffn0_bufs(ctx, Tc)

    idl = Slot(ctx, "idl")
    idl.load(ctx.Y, [(lnb["ident_bf"][:], identb)])
    idl.wait(ctx.T)

    emit_ffn0_body(ctx, XV(ap=x), XV(ap=xo), wgt, wut, wdt, n_rows, Tc, 0.5, lnb)

    for pn in ("stx0", "stx1"):
        s = ctx.sem(pn)
        ctx.G.wait_ge(s.h, s.n)
    finalize(ctx)
    ctx._es.close()
    return nc


def emit_lut_bcast(ctx, lut_in_ap, lut_sb, ones_sb, pbank):
    T, V, Y = ctx.T, ctx.V, ctx.Y
    mm = ctx.sem("mm"); ve = ctx.sem("ve")
    lrow = ctx.asb("lut_row", [1, 16], F32)
    sl = Slot(ctx, "lutld")
    sl.load(Y, [(lrow[:], lut_in_ap.rearrange("(a b) -> a b", a=1))])
    sl.wait(T)
    nT = inc(T.matmul(pbank[:, 0:16], ones_sb[0:1, :], lrow[:], start=True, stop=True), mm)
    V.wait_ge(mm.h, nT)
    nV = inc(V.tensor_copy(lut_sb[:], pbank[:, 0:16]), ve)
    return nV

def emit_gather_tile(ctx, idxb, acc, tmp, lut_sb, width):
    V = ctx.V
    ve = ctx.sem("ve")
    n = inc(V.tensor_scalar(acc[:, 0:width], idxb[:, 0:width], 0.0, lut_sb[:, 0:1],
                            ALU.is_equal, ALU.mult), ve)
    for j in range(1, 16):
        n = inc(V.tensor_scalar(tmp[:, 0:width], idxb[:, 0:width], float(j), lut_sb[:, j:j + 1],
                                ALU.is_equal, ALU.mult), ve)
        V.wait_ge(ve.h, n)
        n = inc(V.tensor_tensor(acc[:, 0:width], acc[:, 0:width], tmp[:, 0:width], ALU.add), ve)
        V.wait_ge(ve.h, n)

def emit_stage_ghost_lhsT(ctx, idx_dram, out_blk_dram, lut_sb, b, ot_range):
    T, V, Y, G = ctx.T, ctx.V, ctx.Y, ctx.G
    ve = ctx.sem("ve"); mm = ctx.sem("mm"); frp = ctx.sem("frp")
    for i, ot in enumerate(ot_range):
        par = i % 2
        stg = b["stg"][par]
        isl = b["idx_slot"][par]
        isl.load(Y, [(b["idx_sb"][par][:, 0:2048], idx_dram[ot * P:(ot + 1) * P, :])])
        isl.wait(V)
        nV = inc(V.tensor_copy(b["idxb"][:, 0:2048], b["idx_sb"][par][:, 0:2048]), ve)
        isl.set_free(ve, nV)
        V.wait_ge(ve.h, nV)
        emit_gather_tile(ctx, b["idxb"], b["acc"], b["tmp"], lut_sb, 2048)
        nV = inc(V.engine_nop(), ve)
        T.wait_ge(ve.h, nV)
        if b["psbf_busy"]:
            T.wait_ge(frp.h, b["psbf_busy"])
        for kt in range(16):
            ins = T.transpose(b["psbf"][:, kt * P:(kt + 1) * P],
                              b["acc"][:, kt * P:(kt + 1) * P], b["ident_bf"][:])
        nT = inc(ins, mm)
        V.wait_ge(mm.h, nT)
        wsl = b["wt_slot"][par]
        if wsl.free_sem is not None:
            V.wait_ge(wsl.free_sem.h, wsl.free_val)
        for kt in range(16):
            ins = V.tensor_copy(b["wt_sb"][par][:, kt * P:(kt + 1) * P],
                                b["psbf"][:, kt * P:(kt + 1) * P])
        nV = inc(ins, frp)
        b["psbf_busy"] = frp.n
        G.wait_ge(frp.h, nV)
        n_st = inc(G.dma_start(out=out_blk_dram[ot, :, :], in_=b["wt_sb"][par][:]), stg, 16)
        wsl.free_sem, wsl.free_val = stg, n_st

def emit_stage_ghost_rhs(ctx, idx_dram, out_dram, lut_sb, b, dt_range):
    T, V, Y, G = ctx.T, ctx.V, ctx.Y, ctx.G
    ve = ctx.sem("ve"); mm = ctx.sem("mm"); frp = ctx.sem("frp")
    W = 5504
    for i, dt in enumerate(dt_range):
        par = i % 2
        stg = b["stg"][par]
        isl = b["idx_slot"][par]
        isl.load(Y, [(b["idx_sb"][par][:, 0:W], idx_dram[dt * P:(dt + 1) * P, :])])
        isl.wait(V)
        nV = inc(V.tensor_copy(b["idxb"][:, 0:W], b["idx_sb"][par][:, 0:W]), ve)
        isl.set_free(ve, nV)
        V.wait_ge(ve.h, nV)
        emit_gather_tile(ctx, b["idxb"], b["acc"], b["tmp"], lut_sb, W)
        nV = inc(V.engine_nop(), ve)
        T.wait_ge(ve.h, nV)
        done = 0
        for batch0 in range(0, 43, 16):
            nb = min(16, 43 - batch0)
            if b["psbf_busy"]:
                T.wait_ge(frp.h, b["psbf_busy"])
            for j in range(nb):
                ins = T.transpose(b["psbf"][:, j * P:(j + 1) * P],
                                  b["acc"][:, (batch0 + j) * P:(batch0 + j + 1) * P],
                                  b["ident_bf"][:])
            nT = inc(ins, mm)
            V.wait_ge(mm.h, nT)
            wsl = b["wt_slot"][par]
            if wsl.free_sem is not None:
                V.wait_ge(wsl.free_sem.h, wsl.free_val)
            for j in range(nb):
                ins = V.tensor_copy(b["wt_sb"][par][:, j * P:(j + 1) * P],
                                    b["psbf"][:, j * P:(j + 1) * P])
            nV = inc(ins, frp)
            b["psbf_busy"] = frp.n
            G.wait_ge(frp.h, nV)
            for j in range(nb):
                ins = G.dma_start(out=out_dram[batch0 + j, :, dt * P:(dt + 1) * P],
                                  in_=b["wt_sb"][par][:, j * P:(j + 1) * P])
                n_st = inc(ins, stg, 16)
            wsl.free_sem, wsl.free_val = stg, n_st

def emit_stage_transpose(ctx, src_dram, O, I, out_dram, layout, b, slab_range=None,
                         src_row0=0):
    T, V, Y, G = ctx.T, ctx.V, ctx.Y, ctx.G
    ve = ctx.sem("ve"); mm = ctx.sem("mm"); frp = ctx.sem("frp")
    n_slab = O // P
    n_i = I // P
    rng = slab_range if slab_range is not None else range(n_slab)
    for ii, s in enumerate(rng):
        par = ii % 2
        stg = b["stg"][par]
        fsl = b["f32_slot"][par]
        fsl.load(Y, [(b["f32_sb"][par][:, 0:I], src_dram[src_row0 + s * P: src_row0 + (s + 1) * P, :])])
        fsl.wait(V)
        nV = inc(V.tensor_copy(b["acc"][:, 0:I], b["f32_sb"][par][:, 0:I]), ve)
        fsl.set_free(ve, nV)
        T.wait_ge(ve.h, nV)
        for batch0 in range(0, n_i, 16):
            nb = min(16, n_i - batch0)
            if b["psbf_busy"]:
                T.wait_ge(frp.h, b["psbf_busy"])
            for j in range(nb):
                ins = T.transpose(b["psbf"][:, j * P:(j + 1) * P],
                                  b["acc"][:, (batch0 + j) * P:(batch0 + j + 1) * P],
                                  b["ident_bf"][:])
            nT = inc(ins, mm)
            V.wait_ge(mm.h, nT)
            wsl = b["wt_slot"][par]
            if wsl.free_sem is not None:
                V.wait_ge(wsl.free_sem.h, wsl.free_val)
            for j in range(nb):
                ins = V.tensor_copy(b["wt_sb"][par][:, j * P:(j + 1) * P],
                                    b["psbf"][:, j * P:(j + 1) * P])
            nV = inc(ins, frp)
            b["psbf_busy"] = frp.n
            G.wait_ge(frp.h, nV)
            for j in range(nb):
                it = batch0 + j
                if layout == "blk":
                    dst = out_dram[s, :, it * P:(it + 1) * P]
                else:
                    dst = out_dram[it, :, s * P:(s + 1) * P]
                ins = G.dma_start(out=dst, in_=b["wt_sb"][par][:, j * P:(j + 1) * P])
                n_st = inc(ins, stg, 16)
            wsl.free_sem, wsl.free_val = stg, n_st

def emit_stage_cast(ctx, src_dram, R, C, out_dram, b, slab_range=None):
    V, Y, G = ctx.V, ctx.Y, ctx.G
    ve = ctx.sem("ve")
    rng = slab_range if slab_range is not None else range(R // P)
    for ii, s in enumerate(rng):
        par = ii % 2
        stg = b["stgc"][par]
        fsl = b["f32_slot"][par]
        fsl.load(Y, [(b["f32_sb"][par][:, 0:C], src_dram[s * P:(s + 1) * P, :])])
        fsl.wait(V)
        csl = b["cast_slot"][par]
        if csl.free_sem is not None:
            V.wait_ge(csl.free_sem.h, csl.free_val)
        nV = inc(V.tensor_copy(b["cast_sb"][par][:, 0:C], b["f32_sb"][par][:, 0:C]), ve)
        fsl.set_free(ve, nV)
        G.wait_ge(ve.h, nV)
        n_st = inc(G.dma_start(out=out_dram[s * P:(s + 1) * P, :],
                               in_=b["cast_sb"][par][:, 0:C]), stg, 16)
        csl.free_sem, csl.free_val = stg, n_st

def alloc_stage_bufs(ctx, pool=None):
    b = {}
    b["idx_sb"] = [ctx.asb(f"st_idx{j}", [P, 5504], I32) for j in range(2)]
    b["idx_slot"] = [Slot(ctx, f"stidx{j}") for j in range(2)]
    b["idxb"] = ctx.asb("st_idxb", [P, 5504], BF16)
    b["acc"] = ctx.asb("st_acc", [P, 5504], BF16)
    b["tmp"] = ctx.asb("st_tmp", [P, 5504], BF16)
    b["wt_sb"] = [ctx.asb(f"st_wt{j}", [P, 2048], BF16) for j in range(2)]
    b["wt_slot"] = [Slot(ctx, f"stwt{j}") for j in range(2)]
    b["f32_sb"] = [ctx.asb(f"st_f32{j}", [P, 2816], F32) for j in range(2)]
    b["f32_slot"] = [Slot(ctx, f"stf32{j}") for j in range(2)]
    b["cast_sb"] = [ctx.asb(f"st_cast{j}", [P, 2816], BF16) for j in range(2)]
    b["cast_slot"] = [Slot(ctx, f"stcast{j}") for j in range(2)]
    b["stg"] = [ctx.sem("stgA"), ctx.sem("stgB")]
    b["stgc"] = [ctx.sem("stgcA"), ctx.sem("stgcB")]
    b["psbf_busy"] = 0
    return b

def build_stage(nc):
    import contextlib
    ctx = Ctx(nc)
    ctx._es = contextlib.ExitStack()
    dr = lambda nm, sh, dt, kind: nc.dram_tensor(nm, sh, dt, kind=kind).ap()
    I, O = "ExternalInput", "ExternalOutput"
    gidx = dr("gidx", [5504, 2048], I32, I)
    uidx = dr("uidx", [5504, 2048], I32, I)
    didx = dr("didx", [2048, 5504], I32, I)
    lut = dr("lut", [16], F32, I)
    wk = dr("wk", [2048, 2048], F32, I)
    wv = dr("wv", [2048, 2048], F32, I)
    wr = dr("wr", [2048, 2048], F32, I)
    wo = dr("wo", [2048, 2048], F32, I)
    qw = dr("qw", [2048, 2048], F32, I)
    kva = dr("kva", [576, 2048], F32, I)
    kvb = dr("kvb", [4096, 512], F32, I)
    ow = dr("ow", [2048, 2048], F32, I)
    mg = dr("mg", [8, 2048, 1408], F32, I)
    mu = dr("mu", [8, 2048, 1408], F32, I)
    md = dr("md", [8, 1408, 2048], F32, I)
    swg = dr("swg", [2048, 2816], F32, I)
    swu = dr("swu", [2048, 2816], F32, I)
    swd = dr("swd", [2816, 2048], F32, I)
    identb = dr("identb", [P, P], BF16, I)
    onesf = dr("onesf", [1, P], F32, I)

    wgt = dr("wgt", [43, P, 2048], BF16, O)
    wut = dr("wut", [43, P, 2048], BF16, O)
    wdt = dr("wdt", [43, P, 2048], BF16, O)
    wkT = dr("wkT", [16, P, 2048], BF16, O)
    wvT = dr("wvT", [16, P, 2048], BF16, O)
    wrT = dr("wrT", [16, P, 2048], BF16, O)
    woT = dr("woT", [16, P, 2048], BF16, O)
    qwT = dr("qwT", [16, P, 2048], BF16, O)
    kvaT = dr("kvaT", [16, P, 512], BF16, O)
    kvbT = dr("kvbT", [32, P, 512], BF16, O)
    owT = dr("owT", [16, P, 2048], BF16, O)
    mgb = dr("mgb", [8, 2048, 1408], BF16, O)
    mub = dr("mub", [8, 2048, 1408], BF16, O)
    mdb = dr("mdb", [8, 1408, 2048], BF16, O)
    swgb = dr("swgb", [2048, 2816], BF16, O)
    swub = dr("swub", [2048, 2816], BF16, O)
    swdb = dr("swdb", [2816, 2048], BF16, O)

    b = alloc_stage_bufs(ctx)
    b["ident_bf"] = ctx.sb("identb_sb", [P, P], BF16)
    b["psbf"] = ctx.ps("psbf", [P, 2048], BF16)
    pbank = ctx.ps("pb0", [P, 512], F32)
    ones_sb = ctx.sb("ones_sb", [1, P], F32)
    lut_sb = ctx.sb("lut_sb", [P, 16], F32)

    idl = Slot(ctx, "idl")
    idl.load(ctx.Y, [(b["ident_bf"][:], identb), (ones_sb[:], onesf)])
    idl.wait(ctx.T)
    nV = emit_lut_bcast(ctx, lut, lut_sb, ones_sb, pbank)
    ctx.V.wait_ge(ctx.sem("ve").h, nV)

    emit_stage_ghost_lhsT(ctx, gidx, wgt, lut_sb, b, range(43))
    emit_stage_ghost_lhsT(ctx, uidx, wut, lut_sb, b, range(43))
    emit_stage_ghost_rhs(ctx, didx, wdt, lut_sb, b, range(16))
    emit_stage_transpose(ctx, wk, 2048, 2048, wkT, "blk", b)
    emit_stage_transpose(ctx, wv, 2048, 2048, wvT, "blk", b)
    emit_stage_transpose(ctx, wr, 2048, 2048, wrT, "blk", b)
    emit_stage_transpose(ctx, wo, 2048, 2048, woT, "rowmajor", b)
    emit_stage_transpose(ctx, qw, 2048, 2048, qwT, "blk", b)
    emit_stage_transpose(ctx, kva, 512, 2048, kvaT, "rowmajor", b, src_row0=64)
    emit_stage_transpose(ctx, kvb, 4096, 512, kvbT, "blk", b)
    emit_stage_transpose(ctx, ow, 2048, 2048, owT, "rowmajor", b)
    for e in range(8):
        emit_stage_cast(ctx, mg[e], 2048, 1408, mgb[e], b)
        emit_stage_cast(ctx, mu[e], 2048, 1408, mub[e], b)
        emit_stage_cast(ctx, md[e], 1408, 2048, mdb[e], b)
    emit_stage_cast(ctx, swg, 2048, 2816, swgb, b)
    emit_stage_cast(ctx, swu, 2048, 2816, swub, b)
    emit_stage_cast(ctx, swd, 2816, 2048, swdb, b)

    for s in b["stg"] + b["stgc"]:
        ctx.G.wait_ge(s.h, s.n)
    finalize(ctx)
    ctx._es.close()
    return nc


def emit_rwkv_body(ctx, xv_ext, xv_out, wkT, wvT, wrT, woT, lnb, b, after_chunk0_ln=None):
    T, V, S, Y, G = ctx.T, ctx.V, ctx.S, ctx.Y, ctx.G
    ve = ctx.sem("ve"); se = ctx.sem("se"); mm = ctx.sem("mm"); frp = ctx.sem("frp")
    Tc = 512
    nt = 4

    def vch(op):
        n = inc(op, ve)
        V.wait_ge(ve.h, n)
        return n

    for c in range(5):
        warm = (c == 0)
        ln_done = emit_ln_transpose(ctx, lnb, xv_ext, nt, b["xnT"], Tc, row_base=c * Tc)
        if c == 0 and after_chunk0_ln is not None:
            after_chunk0_ln()
        V.wait_ge(frp.h, ln_done)
        for kt in range(16):
            xns = b["xnT"][:, kt * Tc:(kt + 1) * Tc]
            xsh = b["xsh"]
            vch(V.tensor_copy(xsh[:, 1:Tc], xns[:, 0:Tc - 1]))
            vch(V.tensor_copy(xsh[:, 0:1], b["pcol"][:, kt:kt + 1]))
            vch(V.tensor_tensor(b["tdiff"][:], xns, xsh[:], ALU.subtract))
            vch(V.scalar_tensor_tensor(b["xkT"][:, kt * Tc:(kt + 1) * Tc], b["tdiff"][:],
                                       b["mk_sb"][:, kt:kt + 1], xsh[:], ALU.mult, ALU.add))
            vch(V.scalar_tensor_tensor(b["xvT"][:, kt * Tc:(kt + 1) * Tc], b["tdiff"][:],
                                       b["mv_sb"][:, kt:kt + 1], xsh[:], ALU.mult, ALU.add))
            if not warm:
                vch(V.scalar_tensor_tensor(b["xrT"][:, kt * Tc:(kt + 1) * Tc], b["tdiff"][:],
                                           b["mr_sb"][:, kt:kt + 1], xsh[:], ALU.mult, ALU.add))
            vch(V.tensor_copy(b["pcol"][:, kt:kt + 1], xns[:, Tc - 1:Tc]))
        mix_done = ve.n
        free_at = b.setdefault("pkv_free", {})
        for m in range(16):
            par = m % 2
            wsl = b["wkvr_slot"][par]
            pairs = [(b["wkb"][par][:], wkT[m, :, :]), (b["wvb"][par][:], wvT[m, :, :])]
            if not warm:
                pairs.append((b["wrb"][par][:], wrT[m, :, :]))
            wsl.load(Y, pairs)
            wsl.wait(T)
            if m == 0:
                T.wait_ge(ve.h, mix_done)
            if par in free_at:
                for s_, v_ in free_at[par]:
                    T.wait_ge(s_.h, v_)
            pk = b["pk"][par]; pv = b["pv"][par]; pr = b["pr"][par]
            for kt in range(16):
                T.matmul(pk[:], b["wkb"][par][:, kt * P:(kt + 1) * P],
                         b["xkT"][:, kt * Tc:(kt + 1) * Tc], start=(kt == 0), stop=(kt == 15))
            for kt in range(16):
                ins = T.matmul(pv[:], b["wvb"][par][:, kt * P:(kt + 1) * P],
                               b["xvT"][:, kt * Tc:(kt + 1) * Tc], start=(kt == 0), stop=(kt == 15))
            if not warm:
                for kt in range(16):
                    ins = T.matmul(pr[:], b["wrb"][par][:, kt * P:(kt + 1) * P],
                                   b["xrT"][:, kt * Tc:(kt + 1) * Tc], start=(kt == 0), stop=(kt == 15))
            nT = inc(ins, mm)
            wsl.set_free(mm, nT)
            S.wait_ge(mm.h, nT)
            if b.get("scan_ve_prev"):
                S.wait_ge(ve.h, b["scan_ve_prev"])
            nS = inc(S.activation(b["e_sb"][:], pk[:], AF.Exp,
                                  bias=(b["warmb"][:] if warm else 0.0)), se)
            if not warm:
                nS2 = inc(S.activation(b["sr_sb"][:], pr[:], AF.Sigmoid), se)
            else:
                nS2 = nS
            V.wait_ge(se.h, nS)
            vch(V.tensor_tensor(b["p_sb"][:], b["e_sb"][:], pv[:], ALU.mult))
            lam_bc = b["lam_sb"][:, m:m + 1].broadcast_to((P, Tc))
            vch(V.tensor_tensor_scan(b["s_sb"][:], lam_bc, b["p_sb"][:],
                                     b["stS"][:, m:m + 1], ALU.mult, ALU.add))
            vch(V.tensor_tensor_scan(b["z_sb"][:], lam_bc, b["e_sb"][:],
                                     b["stZ"][:, m:m + 1], ALU.mult, ALU.add))
            nfree_v = ve.n
            if not warm:
                vch(V.tensor_copy(b["sh_sb"][:, 1:Tc], b["s_sb"][:, 0:Tc - 1]))
                vch(V.tensor_copy(b["sh_sb"][:, 0:1], b["stS"][:, m:m + 1]))
                vch(V.tensor_copy(b["zh_sb"][:, 1:Tc], b["z_sb"][:, 0:Tc - 1]))
                vch(V.tensor_copy(b["zh_sb"][:, 0:1], b["stZ"][:, m:m + 1]))
            vch(V.tensor_copy(b["stS"][:, m:m + 1], b["s_sb"][:, Tc - 1:Tc]))
            vch(V.tensor_copy(b["stZ"][:, m:m + 1], b["z_sb"][:, Tc - 1:Tc]))
            if not warm:
                vch(V.scalar_tensor_tensor(b["num_sb"][:], b["p_sb"][:], b["eu_sb"][:, m:m + 1],
                                           b["sh_sb"][:], ALU.mult, ALU.add))
                vch(V.scalar_tensor_tensor(b["den_sb"][:], b["e_sb"][:], b["eu_sb"][:, m:m + 1],
                                           b["zh_sb"][:], ALU.mult, ALU.add))
                vch(V.reciprocal(b["den_sb"][:], b["den_sb"][:]))
                V.wait_ge(se.h, nS2)
                vch(V.tensor_tensor(b["num_sb"][:], b["num_sb"][:], b["den_sb"][:], ALU.mult))
                vch(V.tensor_tensor(b["wkvsr"][:, m * Tc:(m + 1) * Tc], b["num_sb"][:],
                                    b["sr_sb"][:], ALU.mult))
            free_at[par] = [(ctx.sem("ve"), ve.n), (ctx.sem("se"), nS2)]
            b["scan_ve_prev"] = ve.n
        scan_done = ve.n
        if warm:
            continue
        for dq in range(4):
            par = 0
            wsl = b["wo_slot"][par]
            wsl.load(Y, [(b["wob"][par][:],
                          woT[:, :, dq * 512:(dq + 1) * 512].rearrange("k p f -> p k f"))])
            wsl.wait(T)
            if dq == 0:
                T.wait_ge(ve.h, scan_done)
            if dq >= 1:
                T.wait_ge(ve.h, b["wo_ev"][dq - 1])
            for kt in range(16):
                for tt in range(nt):
                    ins = T.matmul(b["po"][tt][:], b["wkvsr"][:, kt * Tc + tt * P: kt * Tc + (tt + 1) * P],
                                   b["wob"][par][:, kt * 512:(kt + 1) * 512],
                                   start=(kt == 0), stop=(kt == 15))
            nT = inc(ins, mm)
            wsl.set_free(mm, nT)
            xsl = b["xr_slot"][par]
            xsl.load(Y, [(b["xrm"][par][tt][:],
                          xv_ext.sl(c * Tc + tt * P, c * Tc + (tt + 1) * P, dq * 512, (dq + 1) * 512))
                         for tt in range(nt)])
            V.wait_ge(mm.h, nT)
            xsl.wait(V)
            stx_p = ctx.sem("rwst0")
            if stx_p.n > 0:
                V.wait_ge(stx_p.h, stx_p.n)
            for tt in range(nt):
                ins = V.tensor_tensor(b["xrw"][par][tt][:], b["po"][tt][:], b["xrm"][par][tt][:],
                                      ALU.add)
            nV = inc(ins, ve)
            xsl.set_free(ve, nV)
            b["wo_ev"][dq] = nV
            G.wait_ge(ve.h, nV)
            for tt in range(nt):
                store_x(G, stx_p, xv_out.sl((c - 1) * Tc + tt * P, (c - 1) * Tc + (tt + 1) * P,
                                            dq * 512, (dq + 1) * 512), b["xrw"][par][tt][:])
        b["pkv_free"] = {0: [(ve, b["wo_ev"][3])], 1: [(ve, b["wo_ev"][3])]}


def alloc_rwkv_bufs(ctx, pool=None):
    Tc = 512
    b = {}
    b["xnT"] = ctx.asb("rw_xnT", [P, 16 * Tc], BF16)
    b["xkT"] = ctx.asb("rw_xkT", [P, 16 * Tc], BF16)
    b["xvT"] = ctx.asb("rw_xvT", [P, 16 * Tc], BF16)
    b["xrT"] = ctx.asb("rw_xrT", [P, 16 * Tc], BF16)
    b["wkvsr"] = ctx.asb("rw_wkvsr", [P, 16 * Tc], BF16)
    b["xsh"] = ctx.asb("rw_xsh", [P, Tc], BF16)
    b["tdiff"] = ctx.asb("rw_tdiff", [P, Tc], BF16)
    b["pcol"] = ctx.asb("rw_pcol", [P, 16], BF16)
    for nm in ["e_sb", "p_sb", "s_sb", "z_sb", "sh_sb", "zh_sb", "num_sb", "den_sb", "sr_sb"]:
        b[nm] = ctx.asb("rw_" + nm, [P, Tc], F32)
    b["stS"] = ctx.asb("rw_stS", [P, 16], F32)
    b["stZ"] = ctx.asb("rw_stZ", [P, 16], F32)
    for nm in ["lam_sb", "eu_sb", "mk_sb", "mv_sb", "mr_sb"]:
        b[nm] = ctx.asb("rw_" + nm, [P, 16], F32)
    b["warmb"] = ctx.asb("rw_warmb", [P, 1], F32)
    b["td_sb"] = ctx.asb("rw_td", [P, 16], F32)
    b["tf_sb"] = ctx.asb("rw_tf", [P, 16], F32)
    b["tmp16"] = ctx.asb("rw_tmp16", [P, 16], F32)
    b["wkb"] = [ctx.asb(f"rw_wkb{j}", [P, 2048], BF16) for j in range(2)]
    b["wvb"] = [ctx.asb(f"rw_wvb{j}", [P, 2048], BF16) for j in range(2)]
    b["wrb"] = [ctx.asb(f"rw_wrb{j}", [P, 2048], BF16) for j in range(2)]
    b["wob"] = [ctx.asb("rw_wob0", [P, 16 * 512], BF16)]
    b["wkvr_slot"] = [Slot(ctx, f"rwkvr{j}") for j in range(2)]
    b["wo_slot"] = [Slot(ctx, "rwwo0")]
    b["xr_slot"] = [Slot(ctx, "rwxr0")]
    b["xrm"] = [[ctx.asb(f"rw_xrm0_{t}", [P, 512], F32) for t in range(4)]]
    b["xrw"] = [[ctx.asb(f"rw_xrw0_{t}", [P, 512], F32) for t in range(4)]]
    if pool is None:
        b["pk"] = [ctx.ps(f"rw_pk{j}", [P, 512], F32) for j in range(2)]
        b["pv"] = [ctx.ps(f"rw_pv{j}", [P, 512], F32) for j in range(2)]
        b["pr"] = [ctx.ps(f"rw_pr{j}", [P, 512], F32) for j in range(2)]
    else:
        b["pk"] = pool[0:2]
        b["pv"] = pool[2:4]
        b["pr"] = pool[4:6]
    b["po"] = b["pk"] + b["pv"]
    b["wo_ev"] = {}
    return b


def emit_rwkv_consts(ctx, b, td_in, tf_in, mk_in, mv_in, mr_in, warmb_in):
    V, S, Y = ctx.V, ctx.S, ctx.Y
    ve = ctx.sem("ve"); se = ctx.sem("se")
    sl = Slot(ctx, "rwconst")
    sl.load(Y, [(b["td_sb"][:], td_in), (b["tf_sb"][:], tf_in), (b["mk_sb"][:], mk_in),
                (b["mv_sb"][:], mv_in), (b["mr_sb"][:], mr_in), (b["warmb"][:], warmb_in)])
    sl.wait(S)
    nS = inc(S.activation(b["tmp16"][:], b["td_sb"][:], AF.Exp), se)
    S.wait_ge(se.h, nS)
    nS = inc(S.activation(b["lam_sb"][:], b["tmp16"][:], AF.Exp, scale=-1.0), se)
    S.wait_ge(se.h, nS)
    nS = inc(S.activation(b["eu_sb"][:], b["tf_sb"][:], AF.Exp), se)
    V.wait_ge(se.h, nS)
    nV = inc(V.memset(b["pcol"][:], 0.0), ve)
    nV = inc(V.memset(b["stS"][:], 0.0), ve)
    nV = inc(V.memset(b["stZ"][:], 0.0), ve)
    V.wait_ge(ve.h, nV)
    return nS, nV


def build_rwkv(nc):
    import contextlib
    ctx = Ctx(nc)
    ctx._es = contextlib.ExitStack()
    dr = lambda nm, sh, dt, kind: nc.dram_tensor(nm, sh, dt, kind=kind).ap()
    I, O = "ExternalInput", "ExternalOutput"
    x_ext = dr("x_ext", [2560, 2048], F32, I)
    wkT = dr("wkT", [16, P, 2048], BF16, I)
    wvT = dr("wvT", [16, P, 2048], BF16, I)
    wrT = dr("wrT", [16, P, 2048], BF16, I)
    woT = dr("woT", [16, P, 2048], BF16, I)
    td = dr("td", [P, 16], F32, I)
    tf = dr("tf", [P, 16], F32, I)
    mkc = dr("mkc", [P, 16], F32, I)
    mvc = dr("mvc", [P, 16], F32, I)
    mrc = dr("mrc", [P, 16], F32, I)
    warmb = dr("warmb", [P, 1], F32, I)
    identb = dr("identb", [P, P], BF16, I)
    xo = dr("xo", [2048, 2048], F32, O)

    lnb = alloc_ln_bufs(ctx)
    lnb["ident_bf"] = ctx.sb("identb_sb", [P, P], BF16)
    lnb["psbf"] = ctx.ps("psbf", [P, 2048], BF16)
    b = alloc_rwkv_bufs(ctx)

    idl = Slot(ctx, "idl")
    idl.load(ctx.Y, [(lnb["ident_bf"][:], identb)])
    idl.wait(ctx.T)
    emit_rwkv_consts(ctx, b, td, tf, mkc, mvc, mrc, warmb)
    emit_rwkv_body(ctx, XV(ap=x_ext), XV(ap=xo), wkT, wvT, wrT, woT, lnb, b)
    s = ctx.sem("rwst0")
    ctx.G.wait_ge(s.h, s.n)
    finalize(ctx)
    ctx._es.close()
    return nc



def emit_moe_ln_gating(ctx, lnb, b, xv, row_base, gwb, gwlo, n_tiles, Tc):
    T, V, S, Y = ctx.T, ctx.V, ctx.S, ctx.Y
    ve = ctx.sem("ve"); se = ctx.sem("se"); mm = ctx.sem("mm"); frp = ctx.sem("frp")
    D = 2048

    def vch(op):
        n = inc(op, ve)
        V.wait_ge(ve.h, n)
        return n

    for i in range(n_tiles):
        par = i % len(lnb["xa"])
        xa = lnb["xa"][par]; xn = lnb["xn"][par]
        sl = lnb["xa_slot"][par]
        sl.load(Y, [(xa[:], xv.sl(row_base + i * P, row_base + (i + 1) * P))])
        sl.wait(V)
        nV = inc(V.reduce_sum(lnb["s1"][:], xa[:], axis=AX.X), ve)
        sl.wait(S)
        nS = inc(S.activation(lnb["sq"][:], xa[:], AF.Square, accum_out=lnb["ss"][:]), se)
        V.wait_ge(se.h, nS)
        V.wait_ge(ve.h, nV)
        vch(V.tensor_scalar(lnb["va"][:], lnb["ss"][:], 1.0 / D, None, ALU.mult))
        vch(V.tensor_scalar(lnb["vb"][:], lnb["s1"][:], 1.0 / D, None, ALU.mult))
        vch(V.tensor_tensor(lnb["vc"][:], lnb["vb"][:], lnb["vb"][:], ALU.mult))
        vch(V.tensor_tensor(lnb["vc"][:], lnb["va"][:], lnb["vc"][:], ALU.subtract))
        nV = vch(V.tensor_scalar(lnb["vc"][:], lnb["vc"][:], 1e-5, None, ALU.add))
        S.wait_ge(ve.h, nV)
        nS = inc(S.activation(lnb["lnv"][:], lnb["vc"][:], AF.Ln), se)
        S.wait_ge(se.h, nS)
        nS = inc(S.activation(lnb["rstd"][:], lnb["lnv"][:], AF.Exp, scale=-0.5), se)
        V.wait_ge(se.h, nS)
        vch(V.tensor_tensor(lnb["nmr"][:], lnb["vb"][:], lnb["rstd"][:], ALU.mult))
        nV = vch(V.tensor_scalar(lnb["nmr"][:], lnb["nmr"][:], -1.0, None, ALU.mult))
        S.wait_ge(ve.h, nV)
        if b.get("sq_free"):
            S.wait_ge(ve.h, b["sq_free"])
        nS = inc(S.activation(lnb["sq"][:], xa[:], AF.Identity, bias=lnb["nmr"][:],
                              scale=lnb["rstd"][:]), se)
        sl.set_free(se, nS)
        V.wait_ge(se.h, nS)
        nV = vch(V.tensor_copy(xn[:], lnb["sq"][:]))
        nV = vch(V.tensor_tensor(b["lo_t"][:], lnb["sq"][:], xn[:], ALU.subtract))
        b["sq_free"] = nV
        T.wait_ge(ve.h, nV)
        if lnb["psbf_busy"]:
            T.wait_ge(frp.h, lnb["psbf_busy"])
        for kt in range(16):
            ins = T.transpose(lnb["psbf"][:, kt * P:(kt + 1) * P], xn[:, kt * P:(kt + 1) * P],
                              lnb["ident_bf"][:])
        nT = inc(ins, mm)
        V.wait_ge(mm.h, nT)
        for kt in range(16):
            ins = V.tensor_copy(b["xnT"][:, kt * Tc + i * P: kt * Tc + (i + 1) * P],
                                lnb["psbf"][:, kt * P:(kt + 1) * P])
        nfrp = inc(ins, frp)
        lnb["psbf_busy"] = nfrp
        T.wait_ge(frp.h, nfrp)
        for kt in range(16):
            ins = T.transpose(lnb["psbf"][:, kt * P:(kt + 1) * P], b["lo_t"][:, kt * P:(kt + 1) * P],
                              lnb["ident_bf"][:])
        nT = inc(ins, mm)
        V.wait_ge(mm.h, nT)
        for kt in range(16):
            ins = V.tensor_copy(b["loT_t"][:, kt * P:(kt + 1) * P], lnb["psbf"][:, kt * P:(kt + 1) * P])
        nfrp = inc(ins, frp)
        lnb["psbf_busy"] = nfrp
        nloT = inc(V.engine_nop(), ve)
        T.wait_ge(ve.h, nloT)
        if b.get("gt_ev"):
            T.wait_ge(ve.h, b["gt_ev"])
        first = True
        for kt in range(16):
            T.matmul(b["pgt"][:, 0:8], b["xnT"][:, kt * Tc + i * P: kt * Tc + (i + 1) * P],
                     gwb[:, kt * 8:(kt + 1) * 8], start=first, stop=False)
            first = False
            T.matmul(b["pgt"][:, 0:8], b["xnT"][:, kt * Tc + i * P: kt * Tc + (i + 1) * P],
                     gwlo[:, kt * 8:(kt + 1) * 8], start=False, stop=False)
            ins = T.matmul(b["pgt"][:, 0:8], b["loT_t"][:, kt * P:(kt + 1) * P],
                           gwb[:, kt * 8:(kt + 1) * 8], start=False, stop=(kt == 15))
        nT = inc(ins, mm)
        V.wait_ge(mm.h, nT)
        tt = i
        lg = b["lg"]; mx = b["mx"]; es_ = b["es"]
        vch(V.tensor_copy(lg[:], b["pgt"][:, 0:8]))
        b["gt_ev"] = ve.n
        vch(V.reduce_max(mx[:], lg[:], axis=AX.X))
        nV = vch(V.tensor_scalar(mx[:], mx[:], -1.0, None, ALU.mult))
        S.wait_ge(ve.h, nV)
        nS = inc(S.activation(es_[:], lg[:], AF.Exp, bias=mx[:], accum_out=b["den8"][:]), se)
        V.wait_ge(se.h, nS)
        vch(V.reciprocal(b["rden8"][:], b["den8"][:]))
        sc = b["sc_sb"][:, tt * 8:(tt + 1) * 8]
        vch(V.tensor_scalar(sc, es_[:], b["rden8"][:], None, ALU.mult))
        vch(V.reduce_max(b["mx1"][:], sc, axis=AX.X))
        vch(V.tensor_scalar(b["is1"][:], sc, b["mx1"][:], None, ALU.is_equal))
        vch(V.scalar_tensor_tensor(b["tmsk"][:], b["is1"][:], -1e30, sc, ALU.mult, ALU.add))
        vch(V.reduce_max(b["mx2"][:], b["tmsk"][:], axis=AX.X))
        vch(V.tensor_scalar(b["is2"][:], b["tmsk"][:], b["mx2"][:], None, ALU.is_equal))
        vch(V.tensor_tensor(b["is1"][:], b["is1"][:], b["is2"][:], ALU.add))
        vch(V.tensor_tensor(b["comb"][:, tt * 8:(tt + 1) * 8], sc, b["is1"][:], ALU.mult))
    return ve.n

def emit_moe_body(ctx, xv_in, xv_out, mgb, mub, mdb, swgb, swub, swdb, gwb_pair, lnb, b):
    import os
    """x, xo [2048, 2048] f32. mgb/mub [8, 2048, 1408] bf16, mdb [8, 1408, 2048] bf16,
    swgb/swub [2048, 2816] bf16, swdb [2816, 2048] bf16. gwb_sb [128, 16*8] bf16 resident."""
    T, V, S, Y, G = ctx.T, ctx.V, ctx.S, ctx.Y, ctx.G
    ve = ctx.sem("ve"); se = ctx.sem("se"); mm = ctx.sem("mm"); frp = ctx.sem("frp")
    Tc = 1024
    nt = 8

    def vch(op):
        n = inc(op, ve)
        V.wait_ge(ve.h, n)
        return n

    gwb_sb, gwlo_sb = gwb_pair
    for c in range(int(os.environ.get('MOE_NCHUNK', 2))):
        comb_done = emit_moe_ln_gating(ctx, lnb, b, xv_in, c * Tc, gwb_sb, gwlo_sb, nt, Tc)
        if "dbg_comb" in b:
            G.wait_ge(ve.h, comb_done)
            dbs = ctx.sem("dbs")
            inc(G.dma_start(out=b["dbg_comb"][c], in_=b["comb"][:]), dbs, 16)
            inc(G.dma_start(out=b["dbg_sc"][c], in_=b["sc_sb"][:]), dbs, 16)
            if c == 1:
                inc(G.dma_start(out=b["dbg_xnT"], in_=b["xnT"][:]), dbs, 16)
            G.wait_ge(dbs.h, dbs.n)
        T.wait_ge(ve.h, comb_done)
        for tt in range(nt):
            ins = T.transpose(b["pct"][tt // 4][0:8, (tt % 4) * P:(tt % 4 + 1) * P],
                              b["comb"][:, tt * 8:(tt + 1) * 8], b["ident_f"][:])
        nT = inc(ins, mm)
        V.wait_ge(mm.h, nT)
        vch(V.tensor_copy(b["combT"][0:8, 0:512], b["pct"][0][0:8, :]))
        vch(V.tensor_copy(b["combT"][0:8, 512:1024], b["pct"][1][0:8, :]))
        ct_done = ve.n

        glist = [4] if os.environ.get('MOE_ONLYSHARED') else list(range(int(os.environ.get('MOE_MAXG', 5))))
        for g in glist:
            shared = (g == 4)
            n_m_half = [(0, 6), (6, 5)] if not shared else [(0, 6), (6, 5), (11, 6), (17, 5)]
            NKT_D = 22
            nexp = 1 if shared else 2
            src_xv = xv_in if (g == 0 or (os.environ.get('MOE_ONLYSHARED') and g == 4)) else xv_out
            h_cols = 0
            for ei in range(nexp):
                e = 2 * g + ei if not shared else 0
                if not shared:
                    T.wait_ge(ve.h, ct_done)
                    if b.get("cbc_ev"):
                        T.wait_ge(ve.h, b["cbc_ev"])
                    if b.get("gu_ev"):
                        T.wait_ge(ve.h, b["gu_ev"])
                    for nb in range(2):
                        if nb == 1:
                            T.wait_ge(ve.h, ve.n)
                        ins = T.matmul(b["pcb"][:], b["sel8"][:, e * P:(e + 1) * P],
                                       b["combT"][0:8, nb * 512:(nb + 1) * 512],
                                       start=True, stop=True)
                        nT = inc(ins, mm)
                        V.wait_ge(mm.h, nT)
                        vch(V.tensor_copy(b["cbc"][:, nb * 512:(nb + 1) * 512], b["pcb"][:]))
                    b["cbc_ev"] = ve.n
                    b["gu_ev"] = ve.n
                for (o0, no) in n_m_half:
                    wsl = b["wgu_slot"]
                    if shared:
                        gsrc, usrc = swgb, swub
                        pairs = []
                        for kt in range(16):
                            pairs.append((b["wg_sb"][:, kt * no * P:(kt + 1) * no * P],
                                          gsrc[kt * P:(kt + 1) * P, o0 * P:(o0 + no) * P]))
                            pairs.append((b["wu_sb"][:, kt * no * P:(kt + 1) * no * P],
                                          usrc[kt * P:(kt + 1) * P, o0 * P:(o0 + no) * P]))
                    else:
                        pairs = []
                        for kt in range(16):
                            pairs.append((b["wg_sb"][:, kt * no * P:(kt + 1) * no * P],
                                          mgb[e, kt * P:(kt + 1) * P, o0 * P:(o0 + no) * P]))
                            pairs.append((b["wu_sb"][:, kt * no * P:(kt + 1) * no * P],
                                          mub[e, kt * P:(kt + 1) * P, o0 * P:(o0 + no) * P]))
                    wsl.load(Y, pairs)
                    wsl.wait(T)
                    for m in range(no):
                        hc = (ei * 11 + o0 + m) * Tc
                        for th in range(2):
                            if b.get("gu_ev"):
                                T.wait_ge(ve.h, b["gu_ev"])
                            pg = b["pgu"][0]; pu = b["pgu"][1]
                            for kt in range(16):
                                T.matmul(pg[:], b["wg_sb"][:, (kt * no + m) * P:(kt * no + m + 1) * P],
                                         b["xnT"][:, kt * Tc + th * 512: kt * Tc + th * 512 + 512],
                                         start=(kt == 0), stop=(kt == 15))
                            for kt in range(16):
                                ins = T.matmul(pu[:], b["wu_sb"][:, (kt * no + m) * P:(kt * no + m + 1) * P],
                                               b["xnT"][:, kt * Tc + th * 512: kt * Tc + th * 512 + 512],
                                               start=(kt == 0), stop=(kt == 15))
                            nT = inc(ins, mm)
                            if m == no - 1 and th == 1:
                                wsl.set_free(mm, nT)
                            S.wait_ge(mm.h, nT)
                            if b.get("sg_ev"):
                                S.wait_ge(ve.h, b["sg_ev"])
                            nS = inc(S.activation(b["sg"][:, 0:512], pg[:], AF.Silu), se)
                            V.wait_ge(se.h, nS)
                            vch(V.tensor_tensor(b["hf"][:, 0:512], b["sg"][:, 0:512], pu[:], ALU.mult))
                            b["sg_ev"] = ve.n
                            if shared:
                                ins = V.tensor_copy(b["h_sb"][:, hc + th * 512: hc + (th + 1) * 512],
                                                    b["hf"][:, 0:512])
                            else:
                                ins = V.tensor_tensor(b["h_sb"][:, hc + th * 512: hc + (th + 1) * 512],
                                                      b["hf"][:, 0:512],
                                                      b["cbc"][:, th * 512:(th + 1) * 512], ALU.mult)
                            nV = inc(ins, ve)
                            V.wait_ge(ve.h, nV)
                            b["gu_ev"] = nV
            h_done = ve.n
            if "dbg_h" in b and c == 1 and shared:
                dbs = ctx.sem("dbs")
                G.wait_ge(ve.h, h_done)
                inc(G.dma_start(out=b["dbg_h"], in_=b["h_sb"][:]), dbs, 16)
                G.wait_ge(dbs.h, dbs.n)
            for dc in range(8):
                par = 0
                wsl = b["wd_slot"][par]
                if shared:
                    wd_pairs = [(b["wd_sb"][par][:, 0:22 * 256],
                                 swdb[:, dc * 256:(dc + 1) * 256].rearrange("(k p) f -> p k f", p=P))]
                else:
                    wd_pairs = [(b["wd_sb"][par][:, ei * 11 * 256:(ei + 1) * 11 * 256],
                                 mdb[2 * g + ei][:, dc * 256:(dc + 1) * 256].rearrange("(k p) f -> p k f", p=P))
                                for ei in range(nexp)]
                wsl.load(Y, wd_pairs)
                wsl.wait(T)
                if dc == 0:
                    T.wait_ge(ve.h, h_done)
                    if 7 in b["dn_ev"]:
                        T.wait_ge(ve.h, b["dn_ev"][7])
                if dc >= 1:
                    T.wait_ge(ve.h, b["dn_ev"][dc - 1])
                for ks in range(NKT_D):
                    for tt in range(nt):
                        psl = b["pdn"][tt // 2][:, (tt % 2) * 256:(tt % 2) * 256 + 256]
                        ins = T.matmul(psl, b["h_sb"][:, ks * Tc + tt * P: ks * Tc + (tt + 1) * P],
                                       b["wd_sb"][par][:, ks * 256:(ks + 1) * 256],
                                       start=(ks == 0 and tt % 2 == 0),
                                       stop=(ks == NKT_D - 1 and tt % 2 == 1))
                nT = inc(ins, mm)
                wsl.set_free(mm, nT)
                xsl = b["xrm_slot"][0]
                stq = ctx.sem("moest0")
                if g > 0 and stq.n > 0:
                    Y.wait_ge(stq.h, stq.n)
                xsl.load(Y, [(b["xrm"][0][tt][:],
                              src_xv.sl(c * Tc + tt * P, c * Tc + (tt + 1) * P, dc * 256, (dc + 1) * 256))
                             for tt in range(nt)])
                V.wait_ge(mm.h, nT)
                xsl.wait(V)
                stx_p = ctx.sem("moest0")
                if stx_p.n > 0:
                    V.wait_ge(stx_p.h, stx_p.n)
                for tt in range(nt):
                    psl = b["pdn"][tt // 2][:, (tt % 2) * 256:(tt % 2) * 256 + 256]
                    ins = V.scalar_tensor_tensor(b["xrw"][0][tt][:], psl, 0.5,
                                                 b["xrm"][0][tt][:], ALU.mult, ALU.add)
                nV = inc(ins, ve)
                xsl.set_free(ve, nV)
                b["dn_ev"][dc] = nV
                G.wait_ge(ve.h, nV)
                for tt in range(nt):
                    store_x(G, stx_p, xv_out.sl(c * Tc + tt * P, c * Tc + (tt + 1) * P,
                                                dc * 256, (dc + 1) * 256), b["xrw"][0][tt][:])
                b["st_mark"][(c, dc)] = (stx_p, stx_p.n)


def alloc_moe_bufs(ctx, pool=None):
    Tc = 1024
    b = {}
    b["xnT"] = ctx.asb("mo_xnT", [P, 16 * Tc], BF16)
    b["h_sb"] = ctx.asb("mo_h", [P, 22 * Tc], BF16)
    b["wg_sb"] = ctx.asb("mo_wg", [P, 16 * 768], BF16)
    b["wu_sb"] = ctx.asb("mo_wu", [P, 16 * 768], BF16)
    b["wd_sb"] = [ctx.asb("mo_wd0", [P, 22 * 256], BF16)]
    b["sg"] = ctx.asb("mo_sg", [P, 512], F32)
    b["lo_t"] = ctx.asb("mo_lo_t", [P, 2048], BF16)
    b["loT_t"] = ctx.asb("mo_loT_t", [P, 2048], BF16)
    b["hf"] = ctx.asb("mo_hf", [P, 512], F32)
    b["cbc"] = ctx.asb("mo_cbc", [P, Tc], F32)
    b["combT"] = ctx.asb("mo_combT", [8, Tc], F32)
    b["sc_sb"] = ctx.asb("mo_sc", [P, 64], F32)
    b["comb"] = ctx.asb("mo_comb", [P, 64], F32)
    for nm in ["lg", "es", "is1", "is2", "tmsk"]:
        b[nm] = ctx.asb("mo_" + nm, [P, 8], F32)
    for nm in ["mx", "mx1", "mx2", "den8", "rden8"]:
        b[nm] = ctx.asb("mo_" + nm, [P, 1], F32)
    b["ones_f"] = ctx.asb("mo_ones", [1, P], F32)
    b["sel8"] = ctx.asb("mo_sel8", [8, 8 * P], F32)
    b["ident_f"] = ctx.asb("mo_identf", [P, P], F32)
    b["xrm"] = [[ctx.asb(f"mo_xrm0_{t}", [P, 256], F32) for t in range(8)]]
    b["xrw"] = [[ctx.asb(f"mo_xrw0_{t}", [P, 256], F32) for t in range(8)]]
    b["wgu_slot"] = Slot(ctx, "mowgu")
    b["wd_slot"] = [Slot(ctx, "mowd0")]
    b["xrm_slot"] = [Slot(ctx, "moxrm0")]
    if pool is None:
        b["pgu"] = [ctx.ps(f"mo_pgu{j}", [P, 512], F32) for j in range(2)]
        b["pdn"] = [ctx.ps(f"mo_pdn{j}", [P, 512], F32) for j in range(4)]
    else:
        b["pgu"] = pool[0:2]
        b["pdn"] = pool[2:6]
    b["pgt"] = b["pgu"][0]
    b["pcb"] = b["pgu"][1]
    b["pct"] = [b["pdn"][0], b["pdn"][1]]
    b["st_mark"] = {}
    b["dn_ev"] = {}
    return b


def build_moe(nc):
    import contextlib
    ctx = Ctx(nc)
    ctx._es = contextlib.ExitStack()
    dr = lambda nm, sh, dt, kind: nc.dram_tensor(nm, sh, dt, kind=kind).ap()
    I, O = "ExternalInput", "ExternalOutput"
    x = dr("x", [2048, 2048], F32, I)
    mgb = dr("mgb", [8, 2048, 1408], BF16, I)
    mub = dr("mub", [8, 2048, 1408], BF16, I)
    mdb = dr("mdb", [8, 1408, 2048], BF16, I)
    swgb = dr("swgb", [2048, 2816], BF16, I)
    swub = dr("swub", [2048, 2816], BF16, I)
    swdb = dr("swdb", [2816, 2048], BF16, I)
    gw = dr("gw", [2048, 8], F32, I)
    identb = dr("identb", [P, P], BF16, I)
    identf = dr("identf", [P, P], F32, I)
    onesf = dr("onesf", [1, P], F32, I)
    sel8 = dr("sel8", [8, 8 * P], F32, I)
    xo = dr("xo", [2048, 2048], F32, O)

    lnb = alloc_ln_bufs(ctx, nx=1)
    lnb["ident_bf"] = ctx.sb("identb_sb", [P, P], BF16)
    lnb["psbf"] = ctx.ps("psbf", [P, 2048], BF16)
    b = alloc_moe_bufs(ctx)

    gw_f = ctx.sb("mo_gwf", [P, 16 * 8], F32)
    gwb = ctx.sb("mo_gwb", [P, 16 * 8], BF16)
    gwlo = ctx.sb("mo_gwlo", [P, 16 * 8], BF16)
    idl = Slot(ctx, "idl")
    idl.load(ctx.Y, [(lnb["ident_bf"][:], identb), (b["ident_f"][:], identf),
                     (b["ones_f"][:], onesf), (b["sel8"][:], sel8),
                     (gw_f[:], gw.rearrange("(k p) e -> p k e", p=P))])
    idl.wait(ctx.V)
    ve = ctx.sem("ve")
    nV = inc(ctx.V.tensor_copy(gwb[:], gw_f[:]), ve)
    ctx.V.wait_ge(ve.h, nV)
    nV = inc(ctx.V.tensor_tensor(gwlo[:], gw_f[:], gwb[:], ALU.subtract), ve)
    ctx.V.wait_ge(ve.h, nV)
    idl.wait(ctx.T)

    import os
    if os.environ.get("MOE_DEBUG"):
        b["dbg_comb"] = dr("dbg_comb", [2, P, 64], F32, O)
        b["dbg_sc"] = dr("dbg_sc", [2, P, 64], F32, O)
        b["dbg_xnT"] = dr("dbg_xnT", [P, 16 * 1024], BF16, O)
        b["dbg_h"] = dr("dbg_h", [P, 22 * 1024], BF16, O)
    emit_moe_body(ctx, XV(ap=x), XV(ap=xo), mgb, mub, mdb, swgb, swub, swdb, (gwb, gwlo), lnb, b)
    s = ctx.sem("moest0")
    ctx.G.wait_ge(s.h, s.n)
    finalize(ctx)
    ctx._es.close()
    return nc


def emit_mla_body(ctx, xv_full, xv_own, xv_out, kvaT, kvbT, qwT, owT, lnb, b):
    T, V, S, Y, G = ctx.T, ctx.V, ctx.S, ctx.Y, ctx.G
    ve = ctx.sem("ve"); se = ctx.sem("se"); mm = ctx.sem("mm"); frp = ctx.sem("frp")
    EPS = 1e-6
    SCALE = 1.0 / math.sqrt(128.0)

    def vch(op):
        n = inc(op, ve)
        V.wait_ge(ve.h, n)
        return n

    for i in range(32):
        par = i % len(lnb["xa"])
        xa = lnb["xa"][par]; xn = lnb["xn"][par]
        sl = lnb["xa_slot"][par]
        sl.load(Y, [(xa[:], xv_full.sl(i * P, (i + 1) * P))])
        sl.wait(V)
        nV = inc(V.reduce_sum(lnb["s1"][:], xa[:], axis=AX.X), ve)
        sl.wait(S)
        nS = inc(S.activation(lnb["sq"][:], xa[:], AF.Square, accum_out=lnb["ss"][:]), se)
        V.wait_ge(se.h, nS)
        V.wait_ge(ve.h, nV)
        vch(V.tensor_scalar(lnb["va"][:], lnb["ss"][:], 1.0 / 2048, None, ALU.mult))
        vch(V.tensor_scalar(lnb["vb"][:], lnb["s1"][:], 1.0 / 2048, None, ALU.mult))
        vch(V.tensor_tensor(lnb["vc"][:], lnb["vb"][:], lnb["vb"][:], ALU.mult))
        vch(V.tensor_tensor(lnb["vc"][:], lnb["va"][:], lnb["vc"][:], ALU.subtract))
        nV = vch(V.tensor_scalar(lnb["vc"][:], lnb["vc"][:], 1e-5, None, ALU.add))
        S.wait_ge(ve.h, nV)
        nS = inc(S.activation(lnb["lnv"][:], lnb["vc"][:], AF.Ln), se)
        S.wait_ge(se.h, nS)
        nS = inc(S.activation(lnb["rstd"][:], lnb["lnv"][:], AF.Exp, scale=-0.5), se)
        V.wait_ge(se.h, nS)
        vch(V.tensor_tensor(lnb["nmr"][:], lnb["vb"][:], lnb["rstd"][:], ALU.mult))
        nV = vch(V.tensor_scalar(lnb["nmr"][:], lnb["nmr"][:], -1.0, None, ALU.mult))
        S.wait_ge(ve.h, nV)
        nS = inc(S.activation(xn[:], xa[:], AF.Identity, bias=lnb["nmr"][:], scale=lnb["rstd"][:]), se)
        sl.set_free(se, nS)
        T.wait_ge(se.h, nS)
        if lnb["psbf_busy"]:
            T.wait_ge(frp.h, lnb["psbf_busy"])
        for kt in range(16):
            ins = T.transpose(lnb["psbf"][:, kt * P:(kt + 1) * P], xn[:, kt * P:(kt + 1) * P],
                              lnb["ident_bf"][:])
        nT = inc(ins, mm)
        V.wait_ge(mm.h, nT)
        if b.get("xnt_ev"):
            V.wait_ge(mm.h, b["xnt_ev"])
        for kt in range(16):
            ins = V.tensor_copy(b["xnT_t"][:, kt * P:(kt + 1) * P], lnb["psbf"][:, kt * P:(kt + 1) * P])
        nfrp = inc(ins, frp)
        lnb["psbf_busy"] = nfrp
        T.wait_ge(frp.h, nfrp)
        if b.get("kvc_ev"):
            T.wait_ge(ve.h, b["kvc_ev"])
        for kt in range(16):
            ins = T.matmul(b["pkv"][:], b["xnT_t"][:, kt * P:(kt + 1) * P],
                           b["kvaT_sb"][:, kt * 512:(kt + 1) * 512], start=(kt == 0), stop=(kt == 15))
        nT = inc(ins, mm)
        b["xnt_ev"] = nT
        S.wait_ge(mm.h, nT)
        if b.get("lat_sq_ev"):
            S.wait_ge(ve.h, b["lat_sq_ev"])
        nS = inc(S.activation(b["sq512"][:], b["pkv"][:], AF.Square, accum_out=b["ss2"][:]), se)
        V.wait_ge(se.h, nS)
        vch(V.tensor_scalar(b["va2"][:], b["ss2"][:], 1.0 / 512, None, ALU.mult))
        nV = vch(V.tensor_scalar(b["va2"][:], b["va2"][:], EPS, None, ALU.add))
        S.wait_ge(ve.h, nV)
        nS = inc(S.activation(b["lnv2"][:], b["va2"][:], AF.Ln), se)
        S.wait_ge(se.h, nS)
        nS = inc(S.activation(b["rstd2"][:], b["lnv2"][:], AF.Exp, scale=-0.5), se)
        V.wait_ge(se.h, nS)
        vch(V.tensor_scalar(b["lat32"][:], b["pkv"][:], b["rstd2"][:], None, ALU.mult))
        b["kvc_ev"] = ve.n
        nV = vch(V.tensor_tensor(b["lat_t"][:], b["lat32"][:], b["wkv_bc"][:], ALU.mult))
        b["lat_sq_ev"] = nV
        T.wait_ge(ve.h, nV)
        T.wait_ge(frp.h, lnb["psbf_busy"])
        for kt in range(4):
            ins = T.transpose(lnb["psbf"][:, kt * P:(kt + 1) * P], b["lat_t"][:, kt * P:(kt + 1) * P],
                              lnb["ident_bf"][:])
        nT = inc(ins, mm)
        V.wait_ge(mm.h, nT)
        for kt in range(4):
            ins = V.tensor_copy(b["latT"][:, kt * 4096 + i * P: kt * 4096 + (i + 1) * P],
                                lnb["psbf"][:, kt * P:(kt + 1) * P])
        nfrp = inc(ins, frp)
        lnb["psbf_busy"] = nfrp
    latT_done = ctx.sem("frp").n

    def rope_inplace(buf, cosb, sinb, F):
        rsem = ctx.sem("rope")
        G.wait_ge(ve.h, ve.n)
        if b.get("rope_free"):
            G.wait_ge(ve.h, b["rope_free"])
        inc(G.dma_start(out=b["rot"][0:32, 0:F], in_=buf[32:64, 0:F]), rsem, 16)
        inc(G.dma_start(out=b["rot"][32:64, 0:F], in_=buf[0:32, 0:F]), rsem, 16)
        V.wait_ge(rsem.h, rsem.n)
        vch(V.tensor_scalar(b["rot"][0:32, 0:F], b["rot"][0:32, 0:F], -1.0, None, ALU.mult))
        vch(V.tensor_tensor(b["rt1"][0:64, 0:F], buf[0:64, 0:F], cosb[0:64, 0:F], ALU.mult))
        vch(V.tensor_tensor(b["rot"][0:64, 0:F], b["rot"][0:64, 0:F], sinb[0:64, 0:F], ALU.mult))
        nV = vch(V.tensor_tensor(buf[0:64, 0:F], b["rt1"][0:64, 0:F], b["rot"][0:64, 0:F], ALU.add))
        b["rope_free"] = nV
        return nV

    for toh in range(2):
        own_done = emit_ln_transpose(ctx, lnb, xv_own, 8, b["xnT_own"], 1024,
                                      row_base=toh * 1024)
        for h in range(16):
            qsl = b["qw_slot"]
            qsl.load(Y, [(b["qwb"][:], qwT[h, :, :])])
            ksl = b["kvb_slot"]
            ksl.load(Y, [(b["kvbb"][:, 0:512], kvbT[2 * h, :, :]),
                         (b["kvbb"][:, 512:1024], kvbT[2 * h + 1, :, :])])
            ksl.wait(T)
            if toh == 0 and h == 0:
                T.wait_ge(frp.h, latT_done)
            T.wait_ge(frp.h, own_done)
            for nb in range(8):
                pk = b["psc"][nb % 2]
                if b.get("sc_ev") and (nb % 2) in b["sc_ev"]:
                    T.wait_ge(se.h, b["sc_ev"][nb % 2])
                if nb >= 2:
                    T.wait_ge(ve.h, b["kevac"][nb - 2])
                for kt in range(4):
                    ins = T.matmul(pk[:], b["kvbb"][:, kt * P:(kt + 1) * P],
                                   b["latT"][:, kt * 4096 + nb * 512: kt * 4096 + (nb + 1) * 512],
                                   start=(kt == 0), stop=(kt == 3))
                nT = inc(ins, mm)
                V.wait_ge(mm.h, nT)
                if nb == 0 and b.get("kb_war"):
                    V.wait_ge(mm.h, b["kb_war"])
                ins = V.tensor_copy(b["kb"][:, nb * 512:(nb + 1) * 512], pk[:])
                b["kevac"][nb] = inc(ins, ve)
            rope_inplace(b["kb"], b["kcos"], b["ksin"], 4096)
            kb_done = ctx.sem("ve").n
            for tk in range(32):
                pv = b["pav"]
                if b.get("v_ev"):
                    T.wait_ge(ve.h, b["v_ev"])
                if tk == 0 and b.get("av_ev"):
                    T.wait_ge(ve.h, b["av_ev"])
                for kt in range(4):
                    ins = T.matmul(pv[:, 0:P], b["latT"][:, kt * 4096 + tk * P: kt * 4096 + (tk + 1) * P],
                                   b["kvbb"][:, 512 + kt * P: 512 + (kt + 1) * P],
                                   start=(kt == 0), stop=(kt == 3))
                nT = inc(ins, mm)
                ksl.set_free(mm, nT)
                V.wait_ge(mm.h, nT)
                ins = V.tensor_copy(b["v_sb"][:, tk * P:(tk + 1) * P], pv[:, 0:P])
                b["v_ev"] = inc(ins, ve)
            v_done = ctx.sem("ve").n
            qsl.wait(T)
            for nb in range(2):
                pq = b["psc"][nb % 2]
                if b.get("sc_ev") and (nb % 2) in b["sc_ev"]:
                    T.wait_ge(se.h, b["sc_ev"][nb % 2])
                if b.get("q_ev"):
                    T.wait_ge(ve.h, b["q_ev"])
                for kt in range(16):
                    ins = T.matmul(pq[:], b["qwb"][:, kt * P:(kt + 1) * P],
                                   b["xnT_own"][:, kt * 1024 + nb * 512: kt * 1024 + (nb + 1) * 512],
                                   start=(kt == 0), stop=(kt == 15))
                nT = inc(ins, mm)
                qsl.set_free(mm, nT)
                V.wait_ge(mm.h, nT)
                ins = V.tensor_copy(b["qb"][:, nb * 512:(nb + 1) * 512], pq[:])
                inc(ins, ve)
            q_done = rope_inplace(b["qb"], b["qcs"][toh][0], b["qcs"][toh][1], 1024)
            b["q_ev"] = q_done
            for tq in range(8):
                for nb in range(8):
                    ps = b["psc"][nb % 2]
                    if b.get("sc_ev") and (nb % 2) in b["sc_ev"]:
                        T.wait_ge(se.h, b["sc_ev"][nb % 2])
                    if nb == 0 and tq == 0:
                        T.wait_ge(ve.h, max(kb_done, q_done, v_done))
                    ins = T.matmul(ps[:], b["qb"][:, tq * P:(tq + 1) * P],
                                   b["kb"][:, nb * 512:(nb + 1) * 512], start=True, stop=True)
                    nT = inc(ins, mm)
                    S.wait_ge(mm.h, nT)
                    if nb == 0 and b.get("sm_done"):
                        S.wait_ge(ve.h, b["sm_done"])
                    ins = S.activation(b["sc_sb"][:, nb * 512:(nb + 1) * 512], ps[:], AF.Identity,
                                       scale=SCALE)
                    nS = inc(ins, se)
                    b.setdefault("sc_ev", {})[nb % 2] = nS
                V.wait_ge(se.h, nS)
                nV = inc(V.reduce_max(b["smx"][:], b["sc_sb"][:], axis=AX.X), ve)
                V.wait_ge(ve.h, nV)
                nV = vch(V.tensor_scalar(b["smx"][:], b["smx"][:], -1.0, None, ALU.mult))
                S.wait_ge(ve.h, nV)
                if b.get("ab_free"):
                    S.wait_ge(mm.h, b["ab_free"])
                nS = inc(S.activation(b["ab"][:], b["sc_sb"][:], AF.Exp, bias=b["smx"][:],
                                      accum_out=b["sden"][:]), se)
                V.wait_ge(se.h, nS)
                nV = vch(V.reciprocal(b["srden"][:], b["sden"][:]))
                b["sm_done"] = nV
                T.wait_ge(se.h, nS)
                for half in range(2):
                    T.wait_ge(frp.h, lnb["psbf_busy"])
                    for j in range(16):
                        ins = T.transpose(lnb["psbf"][:, j * P:(j + 1) * P],
                                          b["ab"][:, (half * 16 + j) * P:(half * 16 + j + 1) * P],
                                          lnb["ident_bf"][:])
                    nT = inc(ins, mm)
                    V.wait_ge(mm.h, nT)
                    if b.get("abT_war"):
                        V.wait_ge(mm.h, b["abT_war"])
                    for j in range(16):
                        ins = V.tensor_copy(b["abT"][:, (half * 16 + j) * P:(half * 16 + j + 1) * P],
                                            lnb["psbf"][:, j * P:(j + 1) * P])
                    nfrp = inc(ins, frp)
                    lnb["psbf_busy"] = nfrp
                abT_done = inc(V.engine_nop(), ve)
                b["ab_free"] = 0
                T.wait_ge(ve.h, abT_done)
                if b.get("av_ev"):
                    T.wait_ge(ve.h, b["av_ev"])
                for kt in range(32):
                    ins = T.matmul(b["pav"][:, 0:P], b["abT"][:, kt * P:(kt + 1) * P],
                                   b["v_sb"][:, kt * P:(kt + 1) * P], start=(kt == 0), stop=(kt == 31))
                nT = inc(ins, mm)
                b["ab_free"] = nT
                b["abT_war"] = nT
                V.wait_ge(mm.h, nT)
                if b.get("ao_st"):
                    V.wait_ge(ctx.sem("aost").h, b["ao_st"])
                ins = V.tensor_scalar(b["ao_t"][:], b["pav"][:, 0:P], b["srden"][:], None, ALU.mult)
                nV = inc(ins, ve)
                b["av_ev"] = nV
                G.wait_ge(ve.h, nV)
                aost = ctx.sem("aost")
                b["ao_st"] = inc(G.dma_start(out=b["attn_out"][h, toh * 1024 + tq * P: toh * 1024 + (tq + 1) * P, :],
                                             in_=b["ao_t"][:]), aost, 16)
            b["kb_war"] = ctx.sem("mm").n
    head_done_st = ctx.sem("aost").n

    for tt in range(16):
        asl = b["ao_slot"]
        if tt == 0:
            Y.wait_ge(ctx.sem("aost").h, head_done_st)
        asl.load(Y, [(b["aon"][:, hh * P:(hh + 1) * P], b["attn_out"][hh, tt * P:(tt + 1) * P, :])
                     for hh in range(16)])
        asl.wait(T)
        if lnb["psbf_busy"]:
            T.wait_ge(frp.h, lnb["psbf_busy"])
        for hh in range(16):
            ins = T.transpose(lnb["psbf"][:, hh * P:(hh + 1) * P], b["aon"][:, hh * P:(hh + 1) * P],
                              lnb["ident_bf"][:])
        nT = inc(ins, mm)
        asl.set_free(mm, nT)
        V.wait_ge(mm.h, nT)
        if b.get("aoT_ev"):
            V.wait_ge(mm.h, b["aoT_ev"])
        for hh in range(16):
            ins = V.tensor_copy(b["aoT"][:, hh * P:(hh + 1) * P], lnb["psbf"][:, hh * P:(hh + 1) * P])
        nfrp = inc(ins, frp)
        lnb["psbf_busy"] = nfrp
        aoT_done = inc(V.engine_nop(), ve)
        T.wait_ge(ve.h, aoT_done)
        if b.get("o_ev"):
            T.wait_ge(ve.h, b["o_ev"])
        for kt in range(16):
            osl = b["ow_slot"]
            osl.load(Y, [(b["owb"][:], owT[kt, :, :])])
            osl.wait(T)
            for dc in range(4):
                ins = T.matmul(b["po"][dc][:], b["aoT"][:, kt * P:(kt + 1) * P],
                               b["owb"][:, dc * 512:(dc + 1) * 512],
                               start=(kt == 0), stop=(kt == 15))
            nT = inc(ins, mm)
            osl.set_free(mm, nT)
        b["aoT_ev"] = nT
        xsl = b["xr_slot"]
        xsl.load(Y, [(b["xrm"][dc][:], xv_own.sl(tt * P, (tt + 1) * P, dc * 512, (dc + 1) * 512))
                     for dc in range(4)])
        V.wait_ge(mm.h, nT)
        xsl.wait(V)
        stq = ctx.sem("mlast")
        if stq.n > 0:
            V.wait_ge(stq.h, stq.n)
        for dc in range(4):
            ins = V.tensor_tensor(b["xrw"][dc][:], b["po"][dc][:], b["xrm"][dc][:], ALU.add)
        nV = inc(ins, ve)
        xsl.set_free(ve, nV)
        b["o_ev"] = nV
        G.wait_ge(ve.h, nV)
        for dc in range(4):
            store_x(G, stq, xv_out.sl(tt * P, (tt + 1) * P, dc * 512, (dc + 1) * 512),
                    b["xrw"][dc][:])


def alloc_mla_bufs(ctx, attn_out_dram, pool=None):
    b = {}
    b["kvaT_sb"] = ctx.asb("ml_kvaT", [P, 16 * 512], BF16)
    b["latT"] = ctx.asb("ml_latT", [P, 4 * 4096], BF16)
    b["xnT_own"] = ctx.asb("ml_xnTo", [P, 16 * 1024], BF16)
    b["xnT_t"] = ctx.nc.alloc_sbuf_tensor_at("ml_xnTt", [P, 16 * P], BF16,
                                             offset=ctx.nc.lookup_mloc(b["xnT_own"]).addr)
    b["sq512"] = ctx.asb("ml_sq512", [P, 512], F32)
    b["lat32"] = ctx.asb("ml_lat32", [P, 512], F32)
    b["lat_t"] = ctx.asb("ml_latt", [P, 512], BF16)
    b["wkv_bc"] = ctx.asb("ml_wkvbc", [P, 512], F32)
    for nm in ["ss2", "va2", "lnv2", "rstd2", "smx", "sden", "srden"]:
        b[nm] = ctx.asb("ml_" + nm, [P, 1], F32)
    b["kb"] = ctx.asb("ml_kb", [P, 4096], BF16)
    b["qb"] = ctx.asb("ml_qb", [P, 1024], BF16)
    b["v_sb"] = ctx.asb("ml_v", [P, 32 * P], BF16)
    b["ab"] = ctx.asb("ml_ab", [P, 4096], BF16)
    b["abT"] = ctx.asb("ml_abT", [P, 32 * P], BF16)
    b["sc_sb"] = ctx.asb("ml_sc", [P, 4096], F32)
    b["rot"] = ctx.nc.alloc_sbuf_tensor_at("ml_rot", [64, 4096], BF16,
                                           offset=ctx.nc.lookup_mloc(b["kvaT_sb"]).addr + 256)
    b["rt1"] = ctx.nc.alloc_sbuf_tensor_at("ml_rt1", [64, 4096], BF16,
                                           offset=ctx.nc.lookup_mloc(b["kvaT_sb"]).addr + 256 + 8192)
    b["kcos"] = ctx.asb("ml_kcos", [64, 4096], BF16)
    b["ksin"] = ctx.asb("ml_ksin", [64, 4096], BF16)
    b["qcs"] = [[ctx.asb(f"ml_qcos{t}", [64, 1024], BF16), ctx.asb(f"ml_qsin{t}", [64, 1024], BF16)]
                for t in range(2)]
    b["qwb"] = ctx.asb("ml_qwb", [P, 2048], BF16)
    b["kvbb"] = ctx.asb("ml_kvbb", [P, 1024], BF16)
    nc = ctx.nc
    def sb_at(name, shape, dt, donor, off):
        base = nc.lookup_mloc(donor).addr
        return nc.alloc_sbuf_tensor_at(name, shape, dt, offset=base + off)
    b["owb"] = sb_at("ml_owb", [P, 2048], BF16, b["kb"], 0)
    b["aon"] = sb_at("ml_aon", [P, 16 * P], BF16, b["kb"], 4096)
    b["aoT"] = sb_at("ml_aoT", [P, 16 * P], BF16, b["ab"], 0)
    b["ao_t"] = sb_at("ml_aot", [P, P], BF16, b["kvaT_sb"], 0)
    b["xrm"] = [sb_at(f"ml_xrm{t}", [P, 512], F32, b["sc_sb"], t * 2048) for t in range(4)]
    b["xrw"] = [sb_at(f"ml_xrw{t}", [P, 512], F32, b["sc_sb"], 8192 + t * 2048) for t in range(4)]
    b["qw_slot"] = Slot(ctx, "mlqw")
    b["kvb_slot"] = Slot(ctx, "mlkvb")
    b["ow_slot"] = Slot(ctx, "mlow")
    b["ao_slot"] = Slot(ctx, "mlao")
    b["xr_slot"] = Slot(ctx, "mlxr")
    if pool is None:
        b["pkv"] = ctx.ps("ml_pkv", [P, 512], F32)
        b["psc"] = [ctx.ps(f"ml_psc{j}", [P, 512], F32) for j in range(2)]
        b["pav"] = ctx.ps("ml_pav", [P, 512], F32)
        b["po2"] = [ctx.ps(f"ml_po{j}", [P, 512], F32) for j in range(2)]
    else:
        b["pkv"] = pool[0]
        b["psc"] = pool[1:3]
        b["pav"] = pool[3]
        b["po2"] = pool[4:6]
    b["po"] = b["po2"] + [b["pkv"], b["pav"]]
    b["attn_out"] = attn_out_dram
    b["kevac"] = {}
    return b


def build_mla(nc):
    import contextlib
    ctx = Ctx(nc)
    ctx._es = contextlib.ExitStack()
    dr = lambda nm, sh, dt, kind: nc.dram_tensor(nm, sh, dt, kind=kind).ap()
    I, O = "ExternalInput", "ExternalOutput"
    x_full = dr("x_full", [4096, 2048], F32, I)
    x_own = dr("x_own", [2048, 2048], F32, I)
    kvaT = dr("kvaT", [16, P, 512], BF16, I)
    kvbT = dr("kvbT", [32, P, 512], BF16, I)
    qwT = dr("qwT", [16, P, 2048], BF16, I)
    owT = dr("owT", [16, P, 2048], BF16, I)
    kvnw = dr("kvnw", [1, 512], F32, I)
    kcos = dr("kcos", [64, 4096], BF16, I)
    ksin = dr("ksin", [64, 4096], BF16, I)
    qcos = dr("qcos", [64, 2048], BF16, I)
    qsin = dr("qsin", [64, 2048], BF16, I)
    identb = dr("identb", [P, P], BF16, I)
    onesf = dr("onesf", [1, P], F32, I)
    xo = dr("xo", [2048, 2048], F32, O)
    attn_out = nc.dram_tensor("attn_sc", [16, 2048, P], BF16).ap()

    lnb = alloc_ln_bufs(ctx, nx=1)
    lnb["ident_bf"] = ctx.sb("identb_sb", [P, P], BF16)
    lnb["psbf"] = ctx.ps("psbf", [P, 2048], BF16)
    b = alloc_mla_bufs(ctx, attn_out)
    def _sb_at(name, shape, dt, donor, off):
        return nc.alloc_sbuf_tensor_at(name, shape, dt, offset=nc.lookup_mloc(donor).addr + off)
    ones_sb = _sb_at("ml_ones", [1, P], F32, b["aoT"], 0)
    kvnw_sb = _sb_at("ml_kvnw", [1, 512], F32, b["aoT"], 512)

    idl = Slot(ctx, "idl")
    idl.load(ctx.Y, [(lnb["ident_bf"][:], identb), (ones_sb[:], onesf),
                     (kvnw_sb[:], kvnw), (b["kcos"][:], kcos), (b["ksin"][:], ksin),
                     (b["qcs"][0][0][:], qcos[:, 0:1024]), (b["qcs"][0][1][:], qsin[:, 0:1024]),
                     (b["qcs"][1][0][:], qcos[:, 1024:2048]), (b["qcs"][1][1][:], qsin[:, 1024:2048]),
                     (b["kvaT_sb"][:], kvaT.rearrange("k p f -> p k f"))])
    idl.wait(ctx.T)
    mm = ctx.sem("mm"); ve = ctx.sem("ve")
    nT = inc(ctx.T.matmul(b["pkv"][:], ones_sb[0:1, :], kvnw_sb[:], start=True, stop=True), mm)
    ctx.V.wait_ge(mm.h, nT)
    nV = inc(ctx.V.tensor_copy(b["wkv_bc"][:], b["pkv"][:]), ve)
    ctx.V.wait_ge(ve.h, nV)

    emit_mla_body(ctx, XV(ap=x_full), XV(ap=x_own), XV(ap=xo), kvaT, kvbT, qwT, owT, lnb, b)
    s = ctx.sem("mlast")
    ctx.G.wait_ge(s.h, s.n)
    finalize(ctx)
    ctx._es.close()
    return nc


def build_fused(nc):
    import contextlib, os
    STOP = int(os.environ.get("FUSED_STOP_AFTER", "6"))
    ctx = Ctx(nc)
    ctx._es = contextlib.ExitStack()
    dr = lambda nm, sh, dt, kind, **kw: nc.dram_tensor(nm, sh, dt, kind=kind, **kw).ap()
    I, O = "ExternalInput", "ExternalOutput"
    x_in = dr("x_in", [2048, 2048], F32, I)
    slot = dr("slot", [1, 1], I32, I)
    kconst = dr("kconst", [1, 16], I32, I)
    gidx = dr("gidx", [5504, 2048], I32, I)
    uidx = dr("uidx", [5504, 2048], I32, I)
    didx = dr("didx", [2048, 5504], I32, I)
    lut = dr("lut", [16], F32, I)
    wk = dr("wk", [2048, 2048], F32, I); wv = dr("wv", [2048, 2048], F32, I)
    wr = dr("wr", [2048, 2048], F32, I); wo = dr("wo", [2048, 2048], F32, I)
    qw = dr("qw", [2048, 2048], F32, I); kva = dr("kva", [576, 2048], F32, I)
    kvb = dr("kvb", [4096, 512], F32, I); ow = dr("ow", [2048, 2048], F32, I)
    mg = dr("mg", [8, 2048, 1408], F32, I); mu = dr("mu", [8, 2048, 1408], F32, I)
    md = dr("md", [8, 1408, 2048], F32, I)
    swg = dr("swg", [2048, 2816], F32, I); swu = dr("swu", [2048, 2816], F32, I)
    swd = dr("swd", [2816, 2048], F32, I)
    gw = dr("gw", [2048, 8], F32, I)
    identb = dr("identb", [P, P], BF16, I)
    identf = dr("identf", [P, P], F32, I)
    onesf = dr("onesf", [1, P], F32, I)
    sel8 = dr("sel8", [8, 8 * P], F32, I)
    td = dr("td", [P, 16], F32, I); tf = dr("tf", [P, 16], F32, I)
    mkc = dr("mkc", [P, 16], F32, I); mvc = dr("mvc", [P, 16], F32, I)
    mrc = dr("mrc", [P, 16], F32, I)
    warmb = dr("warmb", [P, 1], F32, I)
    kvnw = dr("kvnw", [1, 512], F32, I)
    kcos = dr("kcos", [64, 4096], BF16, I); ksin = dr("ksin", [64, 4096], BF16, I)
    qcos = dr("qcos", [64, 2048], BF16, I); qsin = dr("qsin", [64, 2048], BF16, I)
    xo_ext = dr("xo", [2048, 2048], F32, O)
    wgt = dr("wgt", [43, P, 2048], BF16, "Internal")
    wut = dr("wut", [43, P, 2048], BF16, "Internal")
    wdt = dr("wdt", [43, P, 2048], BF16, "Internal")
    wkT = dr("wkT", [16, P, 2048], BF16, "Internal")
    wvT = dr("wvT", [16, P, 2048], BF16, "Internal")
    wrT = dr("wrT", [16, P, 2048], BF16, "Internal")
    woT = dr("woT", [16, P, 2048], BF16, "Internal")
    qwT = dr("qwT", [16, P, 2048], BF16, "Internal")
    kvaT = dr("kvaT", [16, P, 512], BF16, "Internal")
    kvbT = dr("kvbT", [32, P, 512], BF16, "Internal")
    owT = dr("owT", [16, P, 2048], BF16, "Internal")
    mgb = dr("mgb", [8, 2048, 1408], BF16, "Internal")
    mub = dr("mub", [8, 2048, 1408], BF16, "Internal")
    mdb = dr("mdb", [8, 1408, 2048], BF16, "Internal")
    swgb = dr("swgb", [2048, 2816], BF16, "Internal")
    swub = dr("swub", [2048, 2816], BF16, "Internal")
    swdb = dr("swdb", [2816, 2048], BF16, "Internal")
    attn_out = dr("attn_sc", [16, 2048, P], BF16, "Internal")
    xs0 = dr("xs0", [4608, 2048], F32, "Internal", addr_space="Shared")
    xs1 = dr("xs1", [4608, 2048], F32, "Internal", addr_space="Shared")
    flags = dr("flags", [2, 16], I32, "Internal", addr_space="Shared")

    T, V, S, G, Y = ctx.T, ctx.V, ctx.S, ctx.G, ctx.Y

    lnb = alloc_ln_bufs(ctx, nx=1)
    lnb["ident_bf"] = ctx.sb("identb_sb", [P, P], BF16)
    lnb["psbf"] = ctx.ps("psbf", [P, 2048], BF16)
    pool = [ctx.ps(f"pf{j}", [P, 512], F32) for j in range(6)]
    ident_f = ctx.sb("identf_sb", [P, P], F32)
    ones_sb = ctx.sb("ones_sb", [1, P], F32)
    sel8_sb = ctx.sb("sel8_sb", [8, 8 * P], F32)
    kconst_sb = ctx.sb("kconst_sb", [1, 16], I32)
    pollbuf = ctx.sb("pollbuf", [1, 2], I32)
    gw_f = ctx.sb("gw_f", [P, 16 * 8], F32)
    gwb = ctx.sb("gwb", [P, 16 * 8], BF16)
    gwlo = ctx.sb("gwlo", [P, 16 * 8], BF16)
    lut_sb = ctx.sb("lut_sb", [P, 16], F32)
    emit_slot_regs(ctx, slot)
    avail = nc.sbuf_top - nc.sbuf_base - 64
    avail = (avail // 32) * 32
    arena = Arena(ctx, avail)
    ctx.arena = arena
    ve = ctx.sem("ve"); se = ctx.sem("se"); mm = ctx.sem("mm")

    idl = Slot(ctx, "idl")
    idl.load(Y, [(lnb["ident_bf"][:], identb), (ident_f[:], identf), (ones_sb[:], onesf),
                 (sel8_sb[:], sel8), (kconst_sb[:], kconst),
                 (gw_f[:], gw.rearrange("(k p) e -> p k e", p=P))])
    idl.wait(V)
    nV = inc(V.tensor_copy(gwb[:], gw_f[:]), ve)
    V.wait_ge(ve.h, nV)
    nV = inc(V.tensor_tensor(gwlo[:], gw_f[:], gwb[:], ALU.subtract), ve)
    V.wait_ge(ve.h, nV)
    idl.wait(T)
    idl.wait(S)

    zinit = ctx.sem("zinit")
    nV = inc(V.memset(lnb["sq"][:], 0.0), ve)
    G.wait_ge(ve.h, nV)
    for i in range(4):
        inc(G.dma_start(out=xs0[i * P:(i + 1) * P, :], in_=lnb["sq"][:]), zinit, 16)
    G.wait_ge(zinit.h, zinit.n)

    sb_ = alloc_stage_bufs(ctx)
    sb_["ident_bf"] = lnb["ident_bf"]
    sb_["psbf"] = lnb["psbf"]
    nV = emit_lut_bcast(ctx, lut, lut_sb, ones_sb, pool[0])
    V.wait_ge(ctx.sem("ve").h, nV)
    emit_stage_ghost_lhsT(ctx, gidx, wgt, lut_sb, sb_, range(43))
    emit_stage_ghost_lhsT(ctx, uidx, wut, lut_sb, sb_, range(43))
    emit_stage_ghost_rhs(ctx, didx, wdt, lut_sb, sb_, range(16))
    emit_stage_transpose(ctx, wk, 2048, 2048, wkT, "blk", sb_)
    emit_stage_transpose(ctx, wv, 2048, 2048, wvT, "blk", sb_)
    emit_stage_transpose(ctx, wr, 2048, 2048, wrT, "blk", sb_)
    emit_stage_transpose(ctx, wo, 2048, 2048, woT, "rowmajor", sb_)
    emit_stage_transpose(ctx, qw, 2048, 2048, qwT, "blk", sb_)
    emit_stage_transpose(ctx, kva, 512, 2048, kvaT, "rowmajor", sb_, src_row0=64)
    emit_stage_transpose(ctx, kvb, 4096, 512, kvbT, "blk", sb_)
    emit_stage_transpose(ctx, ow, 2048, 2048, owT, "rowmajor", sb_)
    for e in range(8):
        emit_stage_cast(ctx, mg[e], 2048, 1408, mgb[e], sb_)
        emit_stage_cast(ctx, mu[e], 2048, 1408, mub[e], sb_)
        emit_stage_cast(ctx, md[e], 1408, 2048, mdb[e], sb_)
    emit_stage_cast(ctx, swg, 2048, 2816, swgb, sb_)
    emit_stage_cast(ctx, swu, 2048, 2816, swub, sb_)
    emit_stage_cast(ctx, swd, 2816, 2048, swdb, sb_)
    for s in sb_["stg"] + sb_["stgc"]:
        G.wait_ge(s.h, s.n)
    emit_join(ctx)
    arena.reset()

    ffn_b = alloc_ffn0_bufs(ctx, pool=pool)
    emit_ffn0_body(ctx, XV(ap=x_in), XV(shared_ap=xs0), wgt, wut, wdt, 2048, 512, 0.5,
                   lnb, b=ffn_b)
    for pn in ("stx0", "stx1"):
        s = ctx.sem(pn)
        G.wait_ge(s.h, s.n)
    emit_flag_write(ctx, flags, kconst_sb, 1)
    emit_join(ctx)
    arena.reset()

    def dump_and_stop(src_xv):
        dsem = ctx.sem("dumpst")
        dsl = Slot(ctx, "dumpld")
        for tt in range(16):
            dsl.load(Y, [(lnb["xa"][0][:], src_xv.sl(tt * P, (tt + 1) * P))])
            dsl.wait(V)
            if dsem.n > 0:
                V.wait_ge(dsem.h, dsem.n)
            nV = inc(V.tensor_copy(lnb["sq"][:], lnb["xa"][0][:]), ctx.sem("ve"))
            dsl.set_free(ctx.sem("ve"), nV)
            G.wait_ge(ctx.sem("ve").h, nV)
            ins = G.dma_start(out=xo_ext[tt * P:(tt + 1) * P, :], in_=lnb["sq"][:])
            inc(ins, dsem, 16)
            G.wait_ge(dsem.h, dsem.n)
        finalize(ctx)
        ctx._es.close()
        return ctx

    if STOP == 1:
        return dump_and_stop(XV(shared_ap=xs0))

    rw_b = alloc_rwkv_bufs(ctx, pool=pool)
    v1 = emit_flag_poll(ctx, flags, 1, pollbuf, tag="a")
    Y.wait_ge(ctx.sems["barok"].h, v1)
    emit_rwkv_consts(ctx, rw_b, td, tf, mkc, mvc, mrc, warmb)

    def rflag_cb():
        for j in range(len(lnb["xa_slot"])):
            sl = lnb["xa_slot"][j]
            G.wait_ge(sl.ld.h, sl.ld.n)
        emit_flag_write(ctx, flags, kconst_sb, 2)
    emit_rwkv_body(ctx, XV(shared_ap=xs0, ext=True), XV(shared_ap=xs1), wkT, wvT, wrT, woT,
                   lnb, rw_b, after_chunk0_ln=rflag_cb)
    s = ctx.sem("rwst0")
    G.wait_ge(s.h, s.n)
    emit_join(ctx)
    arena.reset()

    if STOP == 2:
        return dump_and_stop(XV(shared_ap=xs1))

    v2 = emit_flag_poll(ctx, flags, 2, pollbuf, tag="b")
    G.wait_ge(ctx.sems["barok"].h, v2)
    ffn_b2 = alloc_ffn0_bufs(ctx, pool=pool)
    emit_ffn0_body(ctx, XV(shared_ap=xs1), XV(shared_ap=xs0), wgt, wut, wdt, 2048, 512, 0.5,
                   lnb, b=ffn_b2)
    for pn in ("stx0", "stx1"):
        s = ctx.sem(pn)
        G.wait_ge(s.h, s.n)
    emit_join(ctx)
    arena.reset()

    if STOP == 3:
        return dump_and_stop(XV(shared_ap=xs0))

    moe_b = alloc_moe_bufs(ctx, pool=pool)
    moe_b["ident_f"] = ident_f
    moe_b["ones_f"] = ones_sb
    moe_b["sel8"] = sel8_sb
    emit_moe_body(ctx, XV(shared_ap=xs0), XV(shared_ap=xs1), mgb, mub, mdb, swgb, swub, swdb,
                  (gwb, gwlo), lnb, moe_b)
    s = ctx.sem("moest0")
    G.wait_ge(s.h, s.n)
    emit_flag_write(ctx, flags, kconst_sb, 4)
    emit_join(ctx)
    arena.reset()

    if STOP == 4:
        return dump_and_stop(XV(shared_ap=xs1))

    v4 = emit_flag_poll(ctx, flags, 4, pollbuf, tag="c")
    Y.wait_ge(ctx.sems["barok"].h, v4)
    mla_b = alloc_mla_bufs(ctx, attn_out, pool=pool)
    kvnw_sb = nc.alloc_sbuf_tensor_at("ml_kvnw", [1, 512], F32,
                                      offset=nc.lookup_mloc(mla_b["aoT"]).addr)
    mla_sl = Slot(ctx, "mlconst")
    mla_sl.load(Y, [(kvnw_sb[:], kvnw), (mla_b["kcos"][:], kcos), (mla_b["ksin"][:], ksin),
                    (mla_b["qcs"][0][0][:], qcos[:, 0:1024]), (mla_b["qcs"][0][1][:], qsin[:, 0:1024]),
                    (mla_b["qcs"][1][0][:], qcos[:, 1024:2048]), (mla_b["qcs"][1][1][:], qsin[:, 1024:2048]),
                    (mla_b["kvaT_sb"][:], kvaT.rearrange("k p f -> p k f"))])
    mla_sl.wait(T)
    nT = inc(T.matmul(mla_b["pkv"][:], ones_sb[0:1, :], kvnw_sb[:], start=True, stop=True), mm)
    V.wait_ge(mm.h, nT)
    nV = inc(V.tensor_copy(mla_b["wkv_bc"][:], mla_b["pkv"][:]), ve)
    V.wait_ge(ve.h, nV)
    emit_mla_body(ctx, XV(ap=xs1, row0=512), XV(shared_ap=xs1), XV(shared_ap=xs0),
                  kvaT, kvbT, qwT, owT, lnb, mla_b)
    s = ctx.sem("mlast")
    G.wait_ge(s.h, s.n)
    emit_join(ctx)
    arena.reset()

    if STOP == 5:
        return dump_and_stop(XV(shared_ap=xs0))

    moe_b2 = alloc_moe_bufs(ctx, pool=pool)
    moe_b2["ident_f"] = ident_f
    moe_b2["ones_f"] = ones_sb
    moe_b2["sel8"] = sel8_sb
    emit_moe_body(ctx, XV(shared_ap=xs0), XV(ap=xo_ext), mgb, mub, mdb, swgb, swub, swdb,
                  (gwb, gwlo), lnb, moe_b2)
    s = ctx.sem("moest0")
    G.wait_ge(s.h, s.n)
    finalize(ctx)
    ctx._es.close()
    return ctx



import math as _math

def _yarn_tables():
    RD = 64
    MAX_POS, ORIG_POS, BASE = 131072, 8192, 500000.0
    scaling = MAX_POS / ORIG_POS
    pos_freqs = BASE ** (np.arange(0, RD, 2, dtype=np.float64) / RD)
    extrap = 1.0 / pos_freqs
    interp = 1.0 / (scaling * pos_freqs)
    low = max(_math.floor(RD * _math.log(ORIG_POS / (32.0 * 2 * _math.pi)) / (2 * _math.log(BASE))), 0)
    high = min(_math.ceil(RD * _math.log(ORIG_POS / (1.0 * 2 * _math.pi)) / (2 * _math.log(BASE))), RD - 1)
    i = np.arange(RD // 2, dtype=np.float64)
    smooth = np.clip((i - low) / max(high - low, 1), 0.0, 1.0)
    inv_freq = (1.0 - smooth) * interp + smooth * extrap
    pos = np.arange(4096, dtype=np.float64)
    freqs = pos[:, None] * inv_freq[None, :]
    emb = np.concatenate([freqs, freqs], axis=-1)
    mscale = 0.1 * _math.log(scaling) + 1.0
    return np.cos(emb) * mscale, np.sin(emb) * mscale


def kernel(**inputs):
    from concourse.bass_utils import run_bass_kernel_spmd
    bfc = lambda a: np.ascontiguousarray(a).astype(ml_dtypes.bfloat16)
    f32 = lambda a: np.ascontiguousarray(np.asarray(a), dtype=np.float32)
    identb = np.eye(P, dtype=ml_dtypes.bfloat16)
    identf = np.eye(P, dtype=np.float32)
    onesf = np.ones((1, P), np.float32)
    sel8 = np.concatenate([(np.arange(8)[:, None] == e).astype(np.float32) * np.ones((8, P), np.float32)
                           for e in range(8)], axis=1)
    pt = lambda a: np.ascontiguousarray(np.asarray(a, dtype=np.float32).reshape(16, P).T)
    cos_t, sin_t = _yarn_tables()

    x = f32(inputs["L0_attn_x"]).reshape(4, 4096, 2048)
    nc = bass.Bass("TRN2", target_bir_lowering=False, debug=False)
    build_fused(nc)
    common = {
        "kconst": flag_const_array(),
        "gidx": np.ascontiguousarray(inputs["L0_ffn_gate_idx"], dtype=np.int32),
        "uidx": np.ascontiguousarray(inputs["L0_ffn_up_idx"], dtype=np.int32),
        "didx": np.ascontiguousarray(inputs["L0_ffn_down_idx"], dtype=np.int32),
        "lut": f32(inputs["L0_ffn_lut"]),
        "wk": f32(inputs["L0_attn_Wk"]), "wv": f32(inputs["L0_attn_Wv"]),
        "wr": f32(inputs["L0_attn_Wr"]), "wo": f32(inputs["L0_attn_Wo"]),
        "qw": f32(inputs["L1_attn_q_w"]), "kva": f32(inputs["L1_attn_kv_a_w"]),
        "kvb": f32(inputs["L1_attn_kv_b_w"]), "ow": f32(inputs["L1_attn_o_w"]),
        "mg": f32(inputs["L1_ffn_w_gate"]), "mu": f32(inputs["L1_ffn_w_up"]),
        "md": f32(inputs["L1_ffn_w_down"]),
        "swg": f32(inputs["L1_ffn_sw_gate"]), "swu": f32(inputs["L1_ffn_sw_up"]),
        "swd": f32(inputs["L1_ffn_sw_down"]), "gw": f32(inputs["L1_ffn_gate_w"]),
        "identb": identb, "identf": identf, "onesf": onesf, "sel8": sel8,
        "td": pt(inputs["L0_attn_time_decay"]), "tf": pt(inputs["L0_attn_time_first"]),
        "mkc": pt(np.asarray(inputs["L0_attn_time_mix_k"]).reshape(2048)),
        "mvc": pt(np.asarray(inputs["L0_attn_time_mix_v"]).reshape(2048)),
        "mrc": pt(np.asarray(inputs["L0_attn_time_mix_r"]).reshape(2048)),
        "kvnw": f32(inputs["L1_attn_kv_norm_w"]).reshape(1, 512),
        "kcos": bfc(cos_t.T), "ksin": bfc(sin_t.T),
    }
    ims = []
    for c in range(8):
        bidx, h = c // 2, c % 2
        im = dict(common)
        im["x_in"] = np.ascontiguousarray(x[bidx, h * 2048:(h + 1) * 2048])
        im["slot"] = np.array([[h]], np.int32)
        im["warmb"] = (np.full((P, 1), -1e30, np.float32) if h == 0
                       else np.zeros((P, 1), np.float32))
        im["qcos"] = bfc(cos_t[h * 2048:(h + 1) * 2048].T)
        im["qsin"] = bfc(sin_t[h * 2048:(h + 1) * 2048].T)
        ims.append(im)
    res = run_bass_kernel_spmd(nc, ims, core_ids=list(range(8)))
    out = np.empty((4, 4096, 2048), np.float32)
    for c in range(8):
        out[c // 2, (c % 2) * 2048:(c % 2 + 1) * 2048] = \
            np.asarray(res.results[c]["xo"]).reshape(2048, 2048)
    return out


# version 03 (final)
# speedup vs baseline: 1.0116x; 1.0116x over previous
import math
import numpy as np
import ml_dtypes
import concourse.bass as bass
import concourse.mybir as mybir

F32 = mybir.dt.float32
BF16 = mybir.dt.bfloat16
I32 = mybir.dt.int32
AF = mybir.ActivationFunctionType
ALU = mybir.AluOpType
AX = mybir.AxisListType
P = 128


class _Inst:
    __slots__ = ("rec",)
    def __init__(self, rec):
        self.rec = rec
    def then_inc(self, sem, v=1):
        self.rec["incs"].append((sem, v))
        return self

class EngRec:
    def __init__(self, name):
        self.name = name
        self.ops = []
    def op(self, method, *a, **kw):
        rec = {"m": method, "a": a, "kw": kw, "incs": []}
        self.ops.append(rec)
        return _Inst(rec)
    def raw(self, fn):
        self.ops.append({"m": None, "fn": fn})
    def __getattr__(self, name):
        if name.startswith("_"):
            raise AttributeError(name)
        def call(*a, **kw):
            return self.op(name, *a, **kw)
        return call
    def replay(self, eng):
        for rec in self.ops:
            if rec["m"] is None:
                rec["fn"](eng)
                continue
            ins = getattr(eng, rec["m"])(*rec["a"], **rec["kw"])
            for sem, v in rec["incs"]:
                ins = ins.then_inc(sem, v)

class Sem:
    def __init__(self, handle, name=""):
        self.h = handle
        self.n = 0
        self.name = name
    def inc(self, k=1):
        self.n += k
        return self.n

def inc(inst, sem, k=1):
    inst.then_inc(sem.h, k)
    sem.inc(k)
    return sem.n

class Ctx:
    def __init__(self, nc):
        self.nc = nc
        self.T = EngRec("tensor")
        self.V = EngRec("vector")
        self.S = EngRec("scalar")
        self.G = EngRec("gpsimd")
        self.Y = EngRec("sync")
        self.sems = {}
        self._es = None
    def sem(self, name):
        if name not in self.sems:
            self.sems[name] = Sem(self._es.enter_context(self.nc.semaphore(name)), name)
        return self.sems[name]
    def sb(self, name, shape, dt):
        return self._es.enter_context(self.nc.sbuf_tensor(name, shape, dt))
    def asb(self, name, shape, dt):
        ar = getattr(self, "arena", None)
        if ar is None:
            return self.sb(name, shape, dt)
        return ar.alloc(name, shape, dt)
    def ps(self, name, shape, dt):
        return self._es.enter_context(self.nc.psum_tensor(name, shape, dt))

def finalize(ctx):
    nc = ctx.nc
    with nc.Block() as block:
        @block.sync
        def _(eng):
            ctx.Y.replay(eng)
        @block.tensor
        def _(eng):
            ctx.T.replay(eng)
        @block.vector
        def _(eng):
            ctx.V.replay(eng)
        @block.scalar
        def _(eng):
            ctx.S.replay(eng)
        @block.gpsimd
        def _(eng):
            ctx.G.replay(eng)


def host_blk_lhsT(wT):
    Din, Dout = wT.shape
    n_kt, n_m = Din // P, Dout // P
    w4 = wT.reshape(n_kt, P, n_m, P)
    return np.ascontiguousarray(w4.transpose(2, 1, 0, 3).reshape(n_m, P, n_kt * P))

def ceil_div(a, b):
    return (a + b - 1) // b


def _branched_dma(E, sem_h, out=None, in_=None, pair=None, store=False):
    a0, a1 = pair
    def fn(eng):
        with eng.If_eq(eng._slot_reg, 0):
            if store:
                eng.dma_start(out=a0, in_=in_).then_inc(sem_h, 16)
            else:
                eng.dma_start(out=out, in_=a0).then_inc(sem_h, 16)
        with eng.Else():
            if store:
                eng.dma_start(out=a1, in_=in_).then_inc(sem_h, 16)
            else:
                eng.dma_start(out=out, in_=a1).then_inc(sem_h, 16)
    E.raw(fn)


class XV:
    def __init__(self, ap=None, shared_ap=None, pad=512, ext=False, row0=0):
        self.ap = ap
        self.sh = shared_ap
        self.pad = pad
        self.ext = ext
        self.row0 = row0
    def sl(self, r0, r1, c0=0, c1=2048):
        if self.ap is not None:
            return self.ap[self.row0 + r0:self.row0 + r1, c0:c1]
        if self.ext:
            return (self.sh[r0:r1, c0:c1], self.sh[2048 + r0:2048 + r1, c0:c1])
        return (self.sh[self.pad + r0:self.pad + r1, c0:c1],
                self.sh[self.pad + 2048 + r0:self.pad + 2048 + r1, c0:c1])


class Slot:
    def __init__(self, ctx, name):
        self.ld = ctx.sem(name + "_ld")
        self.free_sem = None
        self.free_val = 0
        self.nload = 0
    def load(self, E, pairs):
        if self.free_sem is not None:
            E.wait_ge(self.free_sem.h, self.free_val)
        self.nload += 1
        for out, in_ in pairs:
            if isinstance(in_, tuple):
                _branched_dma(E, self.ld.h, out=out, pair=in_)
                self.ld.inc(16)
            else:
                inc(E.dma_start(out=out, in_=in_), self.ld, 16)
    def wait(self, E):
        E.wait_ge(self.ld.h, self.ld.n)
    def set_free(self, sem, val):
        self.free_sem, self.free_val = sem, val


def store_x(E, sem, out_sel, in_sb):
    if isinstance(out_sel, tuple):
        _branched_dma(E, sem.h, in_=in_sb, pair=out_sel, store=True)
        sem.inc(16)
    else:
        inc(E.dma_start(out=out_sel, in_=in_sb), sem, 16)


def emit_join(ctx, tag=""):
    engs = [ctx.T, ctx.V, ctx.S, ctx.G, ctx.Y]
    names = ["jT", "jV", "jS", "jG", "jY"]
    vals = {}
    for E, nm in zip(engs, names):
        s = ctx.sem(nm)
        vals[nm] = inc(E.nop(), s)
    for E in engs:
        for nm in names:
            E.wait_ge(ctx.sems[nm].h, vals[nm])


def emit_slot_regs(ctx, slot_in_ap):
    sl_sb = ctx.sb("slot_sb", [1, 1], I32)
    s = ctx.sem("slotld")
    inc(ctx.Y.dma_start(out=sl_sb[:], in_=slot_in_ap), s, 16)
    for E in (ctx.Y, ctx.G):
        E.wait_ge(s.h, s.n)
        def fn(eng):
            eng._slot_reg = eng.alloc_register()
            eng.reg_load(eng._slot_reg, sl_sb[0:1, 0:1])
        E.raw(fn)


def flag_magics(k):
    m = (k * 2654435761) % (2 ** 31 - 7) + 3
    c = (m ^ 0x5A5A5A5A) & 0x7FFFFFFF
    return m, c


def flag_const_array():
    a = np.zeros((1, 16), np.int32)
    for k in range(1, 9):
        m, c = flag_magics(k)
        a[0, k - 1] = m
        a[0, 8 + k - 1] = c
    return a


def emit_flag_write(ctx, flags_ap, kconst_sb, k):
    G = ctx.G
    s = ctx.sem("barw")
    _branched_dma(G, s.h, in_=kconst_sb[0:1, k - 1:k],
                  pair=(flags_ap[0:1, k - 1:k], flags_ap[1:2, k - 1:k]), store=True)
    s.inc(16)
    _branched_dma(G, s.h, in_=kconst_sb[0:1, 7 + k:8 + k],
                  pair=(flags_ap[0:1, 7 + k:8 + k], flags_ap[1:2, 7 + k:8 + k]), store=True)
    s.inc(16)
    G.wait_ge(s.h, s.n)


def emit_flag_poll(ctx, flags_ap, k, pollbuf, tag=""):
    G = ctx.G
    bar_ok = ctx.sem("barok")
    pl = ctx.sem("barpl" + tag)
    m, c = flag_magics(k)

    def fn(eng):
        rf1 = eng.alloc_register()
        rf2 = eng.alloc_register()
        rcnt = eng.alloc_register()
        rnd = eng.alloc_register()
        eng.reg_mov(rcnt, 0)
        eng.reg_mov(rnd, 1)
        with eng.While(rnd):
            eng.reg_add(rcnt, rcnt, 32)
            with eng.If_eq(eng._slot_reg, 0):
                eng.dma_start(out=pollbuf[0:1, 0:1], in_=flags_ap[1:2, k - 1:k]).then_inc(pl.h, 16)
                eng.dma_start(out=pollbuf[0:1, 1:2], in_=flags_ap[1:2, 7 + k:8 + k]).then_inc(pl.h, 16)
            with eng.Else():
                eng.dma_start(out=pollbuf[0:1, 0:1], in_=flags_ap[0:1, k - 1:k]).then_inc(pl.h, 16)
                eng.dma_start(out=pollbuf[0:1, 1:2], in_=flags_ap[0:1, 7 + k:8 + k]).then_inc(pl.h, 16)
            eng.wait_ge(pl.h, rcnt)
            eng.reg_load(rf1, pollbuf[0:1, 0:1])
            eng.reg_load(rf2, pollbuf[0:1, 1:2])
            with eng.If_eq(rf1, m):
                with eng.If_eq(rf2, c):
                    eng.reg_mov(rnd, 0)
    G.raw(fn)
    return inc(G.nop(), bar_ok)


DT_SIZE_MAP = {F32: 4, BF16: 2, I32: 4}


class Arena:
    def __init__(self, ctx, nbytes):
        self.ctx = ctx
        self.t = ctx.sb("arena", [P, nbytes], mybir.dt.uint8)
        self.base = ctx.nc.lookup_mloc(self.t).addr
        self.nbytes = nbytes
        self.off = 0
        self.peak = 0
    def reset(self):
        self.off = 0
    def alloc(self, name, shape, dt):
        self.off = (self.off + 31) // 32 * 32
        h = self.ctx.nc.alloc_sbuf_tensor_at(name, shape, dt, offset=self.base + self.off)
        self.off += shape[1] * DT_SIZE_MAP[dt]
        self.peak = max(self.peak, self.off)
        assert self.off <= self.nbytes, "arena overflow at " + name
        return h



def emit_ln_transpose(ctx, bufs, xv, n_tiles, dst_sb, dst_tc, dst_col0=0,
                      eps=1e-5, row_base=0):
    T, V, S, Y = ctx.T, ctx.V, ctx.S, ctx.Y
    ve = ctx.sem("ve"); se = ctx.sem("se"); mm = ctx.sem("mm"); frp = ctx.sem("frp")
    D = 2048
    for i in range(n_tiles):
        par = i % len(bufs["xa"])
        xa = bufs["xa"][par]; xn = bufs["xn"][par]
        sl = bufs["xa_slot"][par]
        r0 = row_base + i * P
        sl.load(Y, [(xa[:], xv.sl(r0, r0 + P))])
        sl.wait(V)
        nV = inc(V.reduce_sum(bufs["s1"][:], xa[:], axis=AX.X), ve)
        sl.wait(S)
        nS = inc(S.activation(bufs["sq"][:], xa[:], AF.Square, accum_out=bufs["ss"][:]), se)
        V.wait_ge(se.h, nS)
        V.wait_ge(ve.h, nV)
        nV = inc(V.tensor_scalar(bufs["va"][:], bufs["ss"][:], 1.0 / D, None, ALU.mult), ve)
        nV = inc(V.tensor_scalar(bufs["vb"][:], bufs["s1"][:], 1.0 / D, None, ALU.mult), ve)
        V.wait_ge(ve.h, nV)
        nV = inc(V.tensor_tensor(bufs["vc"][:], bufs["vb"][:], bufs["vb"][:], ALU.mult), ve)
        V.wait_ge(ve.h, nV)
        nV = inc(V.tensor_tensor(bufs["vc"][:], bufs["va"][:], bufs["vc"][:], ALU.subtract), ve)
        V.wait_ge(ve.h, nV)
        nV = inc(V.tensor_scalar(bufs["vc"][:], bufs["vc"][:], eps, None, ALU.add), ve)
        S.wait_ge(ve.h, nV)
        nS = inc(S.activation(bufs["lnv"][:], bufs["vc"][:], AF.Ln), se)
        S.wait_ge(se.h, nS)
        nS = inc(S.activation(bufs["rstd"][:], bufs["lnv"][:], AF.Exp, scale=-0.5), se)
        V.wait_ge(se.h, nS)
        nV = inc(V.tensor_tensor(bufs["nmr"][:], bufs["vb"][:], bufs["rstd"][:], ALU.mult), ve)
        V.wait_ge(ve.h, nV)
        nV = inc(V.tensor_scalar(bufs["nmr"][:], bufs["nmr"][:], -1.0, None, ALU.mult), ve)
        S.wait_ge(ve.h, nV)
        nS = inc(S.activation(xn[:], xa[:], AF.Identity, bias=bufs["nmr"][:], scale=bufs["rstd"][:]), se)
        sl.set_free(se, nS)
        T.wait_ge(se.h, nS)
        if bufs["psbf_busy"]:
            T.wait_ge(frp.h, bufs["psbf_busy"])
        for kt in range(16):
            ins = T.transpose(bufs["psbf"][:, kt * P:(kt + 1) * P],
                              xn[:, kt * P:(kt + 1) * P], bufs["ident_bf"][:])
        nT = inc(ins, mm)
        V.wait_ge(mm.h, nT)
        for kt in range(16):
            dsl = dst_sb[:, kt * dst_tc + dst_col0 + i * P: kt * dst_tc + dst_col0 + (i + 1) * P]
            ins = V.tensor_copy(dsl, bufs["psbf"][:, kt * P:(kt + 1) * P])
        inc(ins, frp)
        bufs["psbf_busy"] = frp.n
    return ctx.sem("frp").n


def alloc_ln_bufs(ctx, tag="", nx=2):
    b = {}
    b["xa"] = [ctx.sb(f"ln_xa{tag}{j}", [P, 2048], F32) for j in range(nx)]
    b["xn"] = [ctx.sb(f"ln_xn{tag}{j}", [P, 2048], BF16) for j in range(nx)]
    b["xa_slot"] = [Slot(ctx, f"lnxa{tag}{j}") for j in range(nx)]
    b["sq"] = ctx.sb(f"ln_sq{tag}", [P, 2048], F32)
    for nm in ["s1", "ss", "va", "vb", "vc", "rstd", "lnv", "nmr"]:
        b[nm] = ctx.sb(f"ln_{nm}{tag}", [P, 1], F32)
    b["psbf_busy"] = 0
    return b


def emit_ffn0_body(ctx, xv_in, xv_out, wgt, wut, wdt, n_rows, Tc, residual_scale, lnb, b=None):
    T, V, S, Y, G = ctx.T, ctx.V, ctx.S, ctx.Y, ctx.G
    mm = ctx.sem("mm"); ve = ctx.sem("ve"); se = ctx.sem("se")
    evf = ctx.sem("evf"); frp = ctx.sem("frp")
    n_chunks = n_rows // Tc
    nt = Tc // P
    NOT = 43
    assert Tc == 512
    if b is None:
        b = ctx.bufs
    xnT, h_sb = b["xnT"], b["h_sb"]

    for c in range(n_chunks):
        ln_done = emit_ln_transpose(ctx, lnb, xv_in, nt, xnT, Tc, row_base=c * Tc)
        evf_base = evf.n
        for m in range(NOT):
            par = m % 2
            wsl = b["wgu_slot"][par]
            wsl.load(Y, [(b["wgbuf"][par][:], wgt[m, :, :]),
                         (b["wubuf"][par][:], wut[m, :, :])])
            wsl.wait(T)
            if m == 0:
                T.wait_ge(frp.h, ln_done)
            if m >= 2:
                T.wait_ge(evf.h, evf_base + (m - 2) + 1)
            pg = b["pgu"][2 * par]; pu = b["pgu"][2 * par + 1]
            for kt in range(16):
                T.matmul(pg[:], b["wgbuf"][par][:, kt * P:(kt + 1) * P],
                         xnT[:, kt * Tc:(kt + 1) * Tc], start=(kt == 0), stop=(kt == 15))
            for kt in range(16):
                ins = T.matmul(pu[:], b["wubuf"][par][:, kt * P:(kt + 1) * P],
                               xnT[:, kt * Tc:(kt + 1) * Tc], start=(kt == 0), stop=(kt == 15))
            nT = inc(ins, mm)
            wsl.set_free(mm, nT)
            S.wait_ge(mm.h, nT)
            nS = inc(S.activation(b["sgbuf"][par][:], pg[:], AF.Silu), se)
            V.wait_ge(se.h, nS)
            ins = V.tensor_tensor(h_sb[:, m * Tc:(m + 1) * Tc], b["sgbuf"][par][:], pu[:], ALU.mult)
            inc(ins, evf)
        h_done = evf.n
        dnev_vals = []
        for dc in range(8):
            par = dc % 2
            wsl = b["wd_slot"][par]
            wsl.load(Y, [(b["wdbuf"][par][:],
                          wdt[:, :, dc * 256:(dc + 1) * 256].rearrange("o p f -> p o f"))])
            wsl.wait(T)
            if dc == 0:
                T.wait_ge(evf.h, h_done)
            if dc >= 1:
                T.wait_ge(ve.h, dnev_vals[dc - 1])
            for ot in range(NOT):
                for tt in range(nt):
                    psl = b["pdn"][tt // 2][:, (tt % 2) * 256:(tt % 2) * 256 + 256]
                    ins = T.matmul(psl, h_sb[:, ot * Tc + tt * P: ot * Tc + (tt + 1) * P],
                                   b["wdbuf"][par][:, ot * 256:(ot + 1) * 256],
                                   start=(ot == 0 and tt % 2 == 0),
                                   stop=(ot == NOT - 1 and tt % 2 == 1))
            nT = inc(ins, mm)
            wsl.set_free(mm, nT)
            xsl = b["xrm_slot"][par]
            xsl.load(Y, [(b["xrm"][par][tt][:],
                          xv_in.sl(c * Tc + tt * P, c * Tc + (tt + 1) * P, dc * 256, (dc + 1) * 256))
                         for tt in range(nt)])
            V.wait_ge(mm.h, nT)
            xsl.wait(V)
            stx_p = ctx.sem("stx" + str(par))
            if stx_p.n > 0:
                V.wait_ge(stx_p.h, stx_p.n)
            for tt in range(nt):
                psl = b["pdn"][tt // 2][:, (tt % 2) * 256:(tt % 2) * 256 + 256]
                ins = V.scalar_tensor_tensor(b["xrw"][par][tt][:], psl, residual_scale,
                                             b["xrm"][par][tt][:], ALU.mult, ALU.add)
            nV = inc(ins, ve)
            xsl.set_free(ve, nV)
            dnev_vals.append(nV)
            G.wait_ge(ve.h, nV)
            for tt in range(nt):
                store_x(G, stx_p, xv_out.sl(c * Tc + tt * P, c * Tc + (tt + 1) * P,
                                            dc * 256, (dc + 1) * 256), b["xrw"][par][tt][:])


def alloc_ffn0_bufs(ctx, Tc=512, pool=None):
    NOT = 43
    b = {}
    b["xnT"] = ctx.asb("xnT", [P, 16 * Tc], BF16)
    b["h_sb"] = ctx.asb("h_sb", [P, NOT * Tc], BF16)
    b["wgbuf"] = [ctx.asb(f"wgbuf{j}", [P, 2048], BF16) for j in range(2)]
    b["wubuf"] = [ctx.asb(f"wubuf{j}", [P, 2048], BF16) for j in range(2)]
    b["wdbuf"] = [ctx.asb(f"wdbuf{j}", [P, NOT * 256], BF16) for j in range(2)]
    b["sgbuf"] = [ctx.asb(f"sgbuf{j}", [P, Tc], F32) for j in range(2)]
    b["xrm"] = [[ctx.asb(f"xrm{j}_{tt}", [P, 256], F32) for tt in range(4)] for j in range(2)]
    b["xrw"] = [[ctx.asb(f"xrw{j}_{tt}", [P, 256], F32) for tt in range(4)] for j in range(2)]
    b["wgu_slot"] = [Slot(ctx, f"wgu{j}") for j in range(2)]
    b["wd_slot"] = [Slot(ctx, f"wd{j}") for j in range(2)]
    b["xrm_slot"] = [Slot(ctx, f"xrm{j}") for j in range(2)]
    b["xrw_slot"] = [Slot(ctx, f"xrw{j}") for j in range(2)]
    if pool is None:
        b["pgu"] = [ctx.ps(f"pgu{j}", [P, 512], F32) for j in range(4)]
        b["pdn"] = [ctx.ps(f"pdn{j}", [P, 512], F32) for j in range(2)]
    else:
        b["pgu"] = pool[0:4]
        b["pdn"] = pool[4:6]
    return b


def build_ffn0(nc, n_rows=2048, Tc=512):
    import contextlib
    ctx = Ctx(nc)
    ctx._es = contextlib.ExitStack()
    x = nc.dram_tensor("x", [n_rows, 2048], F32, kind="ExternalInput").ap()
    wgt = nc.dram_tensor("wgt", [43, P, 2048], BF16, kind="ExternalInput").ap()
    wut = nc.dram_tensor("wut", [43, P, 2048], BF16, kind="ExternalInput").ap()
    wdt = nc.dram_tensor("wdt", [43, P, 2048], BF16, kind="ExternalInput").ap()
    identb = nc.dram_tensor("identb", [P, P], BF16, kind="ExternalInput").ap()
    xo = nc.dram_tensor("xo", [n_rows, 2048], F32, kind="ExternalOutput").ap()

    lnb = alloc_ln_bufs(ctx)
    lnb["ident_bf"] = ctx.sb("identb_sb", [P, P], BF16)
    lnb["psbf"] = ctx.ps("psbf", [P, 2048], BF16)
    ctx.bufs = alloc_ffn0_bufs(ctx, Tc)

    idl = Slot(ctx, "idl")
    idl.load(ctx.Y, [(lnb["ident_bf"][:], identb)])
    idl.wait(ctx.T)

    emit_ffn0_body(ctx, XV(ap=x), XV(ap=xo), wgt, wut, wdt, n_rows, Tc, 0.5, lnb)

    for pn in ("stx0", "stx1"):
        s = ctx.sem(pn)
        ctx.G.wait_ge(s.h, s.n)
    finalize(ctx)
    ctx._es.close()
    return nc


def emit_lut_bcast(ctx, lut_in_ap, lut_sb, ones_sb, pbank):
    T, V, Y = ctx.T, ctx.V, ctx.Y
    mm = ctx.sem("mm"); ve = ctx.sem("ve")
    lrow = ctx.asb("lut_row", [1, 16], F32)
    sl = Slot(ctx, "lutld")
    sl.load(Y, [(lrow[:], lut_in_ap.rearrange("(a b) -> a b", a=1))])
    sl.wait(T)
    nT = inc(T.matmul(pbank[:, 0:16], ones_sb[0:1, :], lrow[:], start=True, stop=True), mm)
    V.wait_ge(mm.h, nT)
    nV = inc(V.tensor_copy(lut_sb[:], pbank[:, 0:16]), ve)
    return nV

def emit_gather_tile(ctx, idxb, acc, tmp, lut_sb, width):
    V = ctx.V
    ve = ctx.sem("ve")
    n = inc(V.tensor_scalar(acc[:, 0:width], idxb[:, 0:width], 0.0, lut_sb[:, 0:1],
                            ALU.is_equal, ALU.mult), ve)
    for j in range(1, 16):
        n = inc(V.tensor_scalar(tmp[:, 0:width], idxb[:, 0:width], float(j), lut_sb[:, j:j + 1],
                                ALU.is_equal, ALU.mult), ve)
        V.wait_ge(ve.h, n)
        n = inc(V.tensor_tensor(acc[:, 0:width], acc[:, 0:width], tmp[:, 0:width], ALU.add), ve)
        V.wait_ge(ve.h, n)

def emit_stage_ghost_lhsT(ctx, idx_dram, out_blk_dram, lut_sb, b, ot_range, ot_range2=None):
    T, V, Y, G = ctx.T, ctx.V, ctx.Y, ctx.G
    ve = ctx.sem("ve"); mm = ctx.sem("mm"); frp = ctx.sem("frp")
    for i, ot in enumerate(ot_range):
        ot2 = ot if ot_range2 is None else ot_range2[i]
        par = i % 2
        stg = b["stg"][par]
        isl = b["idx_slot"][par]
        src_in = (idx_dram[ot * P:(ot + 1) * P, :] if ot_range2 is None else
                  (idx_dram[ot * P:(ot + 1) * P, :], idx_dram[ot2 * P:(ot2 + 1) * P, :]))
        isl.load(Y, [(b["idx_sb"][par][:, 0:2048], src_in)])
        isl.wait(V)
        nV = inc(V.tensor_copy(b["idxb"][:, 0:2048], b["idx_sb"][par][:, 0:2048]), ve)
        isl.set_free(ve, nV)
        V.wait_ge(ve.h, nV)
        emit_gather_tile(ctx, b["idxb"], b["acc"], b["tmp"], lut_sb, 2048)
        nV = inc(V.engine_nop(), ve)
        T.wait_ge(ve.h, nV)
        if b["psbf_busy"]:
            T.wait_ge(frp.h, b["psbf_busy"])
        for kt in range(16):
            ins = T.transpose(b["psbf"][:, kt * P:(kt + 1) * P],
                              b["acc"][:, kt * P:(kt + 1) * P], b["ident_bf"][:])
        nT = inc(ins, mm)
        V.wait_ge(mm.h, nT)
        wsl = b["wt_slot"][par]
        if wsl.free_sem is not None:
            V.wait_ge(wsl.free_sem.h, wsl.free_val)
        for kt in range(16):
            ins = V.tensor_copy(b["wt_sb"][par][:, kt * P:(kt + 1) * P],
                                b["psbf"][:, kt * P:(kt + 1) * P])
        nV = inc(ins, frp)
        b["psbf_busy"] = frp.n
        G.wait_ge(frp.h, nV)
        if ot_range2 is None:
            inc(G.dma_start(out=out_blk_dram[ot, :, :], in_=b["wt_sb"][par][:]), stg, 16)
        else:
            store_x(G, stg, (out_blk_dram[ot, :, :], out_blk_dram[ot2, :, :]), b["wt_sb"][par][:])
        n_st = stg.n
        wsl.free_sem, wsl.free_val = stg, n_st

def emit_stage_ghost_rhs(ctx, idx_dram, out_dram, lut_sb, b, dt_range, dt_range2=None):
    T, V, Y, G = ctx.T, ctx.V, ctx.Y, ctx.G
    ve = ctx.sem("ve"); mm = ctx.sem("mm"); frp = ctx.sem("frp")
    W = 5504
    for i, dt in enumerate(dt_range):
        dt2 = dt if dt_range2 is None else dt_range2[i]
        par = i % 2
        stg = b["stg"][par]
        isl = b["idx_slot"][par]
        src_in = (idx_dram[dt * P:(dt + 1) * P, :] if dt_range2 is None else
                  (idx_dram[dt * P:(dt + 1) * P, :], idx_dram[dt2 * P:(dt2 + 1) * P, :]))
        isl.load(Y, [(b["idx_sb"][par][:, 0:W], src_in)])
        isl.wait(V)
        nV = inc(V.tensor_copy(b["idxb"][:, 0:W], b["idx_sb"][par][:, 0:W]), ve)
        isl.set_free(ve, nV)
        V.wait_ge(ve.h, nV)
        emit_gather_tile(ctx, b["idxb"], b["acc"], b["tmp"], lut_sb, W)
        nV = inc(V.engine_nop(), ve)
        T.wait_ge(ve.h, nV)
        done = 0
        for batch0 in range(0, 43, 16):
            nb = min(16, 43 - batch0)
            if b["psbf_busy"]:
                T.wait_ge(frp.h, b["psbf_busy"])
            for j in range(nb):
                ins = T.transpose(b["psbf"][:, j * P:(j + 1) * P],
                                  b["acc"][:, (batch0 + j) * P:(batch0 + j + 1) * P],
                                  b["ident_bf"][:])
            nT = inc(ins, mm)
            V.wait_ge(mm.h, nT)
            wsl = b["wt_slot"][par]
            if wsl.free_sem is not None:
                V.wait_ge(wsl.free_sem.h, wsl.free_val)
            for j in range(nb):
                ins = V.tensor_copy(b["wt_sb"][par][:, j * P:(j + 1) * P],
                                    b["psbf"][:, j * P:(j + 1) * P])
            nV = inc(ins, frp)
            b["psbf_busy"] = frp.n
            G.wait_ge(frp.h, nV)
            for j in range(nb):
                if dt_range2 is None:
                    inc(G.dma_start(out=out_dram[batch0 + j, :, dt * P:(dt + 1) * P],
                                    in_=b["wt_sb"][par][:, j * P:(j + 1) * P]), stg, 16)
                else:
                    store_x(G, stg, (out_dram[batch0 + j, :, dt * P:(dt + 1) * P],
                                     out_dram[batch0 + j, :, dt2 * P:(dt2 + 1) * P]),
                            b["wt_sb"][par][:, j * P:(j + 1) * P])
            n_st = stg.n
            wsl.free_sem, wsl.free_val = stg, n_st

def emit_stage_transpose(ctx, src_dram, O, I, out_dram, layout, b, slab_range=None,
                         src_row0=0, slab_range2=None):
    T, V, Y, G = ctx.T, ctx.V, ctx.Y, ctx.G
    ve = ctx.sem("ve"); mm = ctx.sem("mm"); frp = ctx.sem("frp")
    n_slab = O // P
    n_i = I // P
    rng = slab_range if slab_range is not None else range(n_slab)
    for ii, s in enumerate(rng):
        s2 = s if slab_range2 is None else slab_range2[ii]
        par = ii % 2
        stg = b["stg"][par]
        fsl = b["f32_slot"][par]
        src_in = (src_dram[src_row0 + s * P: src_row0 + (s + 1) * P, :] if slab_range2 is None else
                  (src_dram[src_row0 + s * P: src_row0 + (s + 1) * P, :],
                   src_dram[src_row0 + s2 * P: src_row0 + (s2 + 1) * P, :]))
        fsl.load(Y, [(b["f32_sb"][par][:, 0:I], src_in)])
        fsl.wait(V)
        nV = inc(V.tensor_copy(b["acc"][:, 0:I], b["f32_sb"][par][:, 0:I]), ve)
        fsl.set_free(ve, nV)
        T.wait_ge(ve.h, nV)
        for batch0 in range(0, n_i, 16):
            nb = min(16, n_i - batch0)
            if b["psbf_busy"]:
                T.wait_ge(frp.h, b["psbf_busy"])
            for j in range(nb):
                ins = T.transpose(b["psbf"][:, j * P:(j + 1) * P],
                                  b["acc"][:, (batch0 + j) * P:(batch0 + j + 1) * P],
                                  b["ident_bf"][:])
            nT = inc(ins, mm)
            V.wait_ge(mm.h, nT)
            wsl = b["wt_slot"][par]
            if wsl.free_sem is not None:
                V.wait_ge(wsl.free_sem.h, wsl.free_val)
            for j in range(nb):
                ins = V.tensor_copy(b["wt_sb"][par][:, j * P:(j + 1) * P],
                                    b["psbf"][:, j * P:(j + 1) * P])
            nV = inc(ins, frp)
            b["psbf_busy"] = frp.n
            G.wait_ge(frp.h, nV)
            for j in range(nb):
                it = batch0 + j
                if layout == "blk":
                    dst = out_dram[s, :, it * P:(it + 1) * P]
                    dst2 = out_dram[s2, :, it * P:(it + 1) * P]
                else:
                    dst = out_dram[it, :, s * P:(s + 1) * P]
                    dst2 = out_dram[it, :, s2 * P:(s2 + 1) * P]
                if slab_range2 is None:
                    inc(G.dma_start(out=dst, in_=b["wt_sb"][par][:, j * P:(j + 1) * P]), stg, 16)
                else:
                    store_x(G, stg, (dst, dst2), b["wt_sb"][par][:, j * P:(j + 1) * P])
            n_st = stg.n
            wsl.free_sem, wsl.free_val = stg, n_st

def emit_stage_cast(ctx, src_dram, R, C, out_dram, b, slab_range=None, slab_range2=None):
    V, Y, G = ctx.V, ctx.Y, ctx.G
    ve = ctx.sem("ve")
    rng = slab_range if slab_range is not None else range(R // P)
    for ii, s in enumerate(rng):
        s2 = s if slab_range2 is None else slab_range2[ii]
        par = ii % 2
        stg = b["stgc"][par]
        fsl = b["f32_slot"][par]
        src_in = (src_dram[s * P:(s + 1) * P, :] if slab_range2 is None else
                  (src_dram[s * P:(s + 1) * P, :], src_dram[s2 * P:(s2 + 1) * P, :]))
        fsl.load(Y, [(b["f32_sb"][par][:, 0:C], src_in)])
        fsl.wait(V)
        csl = b["cast_slot"][par]
        if csl.free_sem is not None:
            V.wait_ge(csl.free_sem.h, csl.free_val)
        nV = inc(V.tensor_copy(b["cast_sb"][par][:, 0:C], b["f32_sb"][par][:, 0:C]), ve)
        fsl.set_free(ve, nV)
        G.wait_ge(ve.h, nV)
        if slab_range2 is None:
            inc(G.dma_start(out=out_dram[s * P:(s + 1) * P, :],
                            in_=b["cast_sb"][par][:, 0:C]), stg, 16)
        else:
            store_x(G, stg, (out_dram[s * P:(s + 1) * P, :], out_dram[s2 * P:(s2 + 1) * P, :]),
                    b["cast_sb"][par][:, 0:C])
        n_st = stg.n
        csl.free_sem, csl.free_val = stg, n_st

def alloc_stage_bufs(ctx, pool=None):
    b = {}
    b["idx_sb"] = [ctx.asb(f"st_idx{j}", [P, 5504], I32) for j in range(2)]
    b["idx_slot"] = [Slot(ctx, f"stidx{j}") for j in range(2)]
    b["idxb"] = ctx.asb("st_idxb", [P, 5504], BF16)
    b["acc"] = ctx.asb("st_acc", [P, 5504], BF16)
    b["tmp"] = ctx.asb("st_tmp", [P, 5504], BF16)
    b["wt_sb"] = [ctx.asb(f"st_wt{j}", [P, 2048], BF16) for j in range(2)]
    b["wt_slot"] = [Slot(ctx, f"stwt{j}") for j in range(2)]
    b["f32_sb"] = [ctx.asb(f"st_f32{j}", [P, 2816], F32) for j in range(2)]
    b["f32_slot"] = [Slot(ctx, f"stf32{j}") for j in range(2)]
    b["cast_sb"] = [ctx.asb(f"st_cast{j}", [P, 2816], BF16) for j in range(2)]
    b["cast_slot"] = [Slot(ctx, f"stcast{j}") for j in range(2)]
    b["stg"] = [ctx.sem("stgA"), ctx.sem("stgB")]
    b["stgc"] = [ctx.sem("stgcA"), ctx.sem("stgcB")]
    b["psbf_busy"] = 0
    return b

def build_stage(nc):
    import contextlib
    ctx = Ctx(nc)
    ctx._es = contextlib.ExitStack()
    dr = lambda nm, sh, dt, kind: nc.dram_tensor(nm, sh, dt, kind=kind).ap()
    I, O = "ExternalInput", "ExternalOutput"
    gidx = dr("gidx", [5504, 2048], I32, I)
    uidx = dr("uidx", [5504, 2048], I32, I)
    didx = dr("didx", [2048, 5504], I32, I)
    lut = dr("lut", [16], F32, I)
    wk = dr("wk", [2048, 2048], F32, I)
    wv = dr("wv", [2048, 2048], F32, I)
    wr = dr("wr", [2048, 2048], F32, I)
    wo = dr("wo", [2048, 2048], F32, I)
    qw = dr("qw", [2048, 2048], F32, I)
    kva = dr("kva", [576, 2048], F32, I)
    kvb = dr("kvb", [4096, 512], F32, I)
    ow = dr("ow", [2048, 2048], F32, I)
    mg = dr("mg", [8, 2048, 1408], F32, I)
    mu = dr("mu", [8, 2048, 1408], F32, I)
    md = dr("md", [8, 1408, 2048], F32, I)
    swg = dr("swg", [2048, 2816], F32, I)
    swu = dr("swu", [2048, 2816], F32, I)
    swd = dr("swd", [2816, 2048], F32, I)
    identb = dr("identb", [P, P], BF16, I)
    onesf = dr("onesf", [1, P], F32, I)

    wgt = dr("wgt", [43, P, 2048], BF16, O)
    wut = dr("wut", [43, P, 2048], BF16, O)
    wdt = dr("wdt", [43, P, 2048], BF16, O)
    wkT = dr("wkT", [16, P, 2048], BF16, O)
    wvT = dr("wvT", [16, P, 2048], BF16, O)
    wrT = dr("wrT", [16, P, 2048], BF16, O)
    woT = dr("woT", [16, P, 2048], BF16, O)
    qwT = dr("qwT", [16, P, 2048], BF16, O)
    kvaT = dr("kvaT", [16, P, 512], BF16, O)
    kvbT = dr("kvbT", [32, P, 512], BF16, O)
    owT = dr("owT", [16, P, 2048], BF16, O)
    mgb = dr("mgb", [8, 2048, 1408], BF16, O)
    mub = dr("mub", [8, 2048, 1408], BF16, O)
    mdb = dr("mdb", [8, 1408, 2048], BF16, O)
    swgb = dr("swgb", [2048, 2816], BF16, O)
    swub = dr("swub", [2048, 2816], BF16, O)
    swdb = dr("swdb", [2816, 2048], BF16, O)

    b = alloc_stage_bufs(ctx)
    b["ident_bf"] = ctx.sb("identb_sb", [P, P], BF16)
    b["psbf"] = ctx.ps("psbf", [P, 2048], BF16)
    pbank = ctx.ps("pb0", [P, 512], F32)
    ones_sb = ctx.sb("ones_sb", [1, P], F32)
    lut_sb = ctx.sb("lut_sb", [P, 16], F32)

    idl = Slot(ctx, "idl")
    idl.load(ctx.Y, [(b["ident_bf"][:], identb), (ones_sb[:], onesf)])
    idl.wait(ctx.T)
    nV = emit_lut_bcast(ctx, lut, lut_sb, ones_sb, pbank)
    ctx.V.wait_ge(ctx.sem("ve").h, nV)

    emit_stage_ghost_lhsT(ctx, gidx, wgt, lut_sb, b, range(43))
    emit_stage_ghost_lhsT(ctx, uidx, wut, lut_sb, b, range(43))
    emit_stage_ghost_rhs(ctx, didx, wdt, lut_sb, b, range(16))
    emit_stage_transpose(ctx, wk, 2048, 2048, wkT, "blk", b)
    emit_stage_transpose(ctx, wv, 2048, 2048, wvT, "blk", b)
    emit_stage_transpose(ctx, wr, 2048, 2048, wrT, "blk", b)
    emit_stage_transpose(ctx, wo, 2048, 2048, woT, "rowmajor", b)
    emit_stage_transpose(ctx, qw, 2048, 2048, qwT, "blk", b)
    emit_stage_transpose(ctx, kva, 512, 2048, kvaT, "rowmajor", b, src_row0=64)
    emit_stage_transpose(ctx, kvb, 4096, 512, kvbT, "blk", b)
    emit_stage_transpose(ctx, ow, 2048, 2048, owT, "rowmajor", b)
    for e in range(8):
        emit_stage_cast(ctx, mg[e], 2048, 1408, mgb[e], b)
        emit_stage_cast(ctx, mu[e], 2048, 1408, mub[e], b)
        emit_stage_cast(ctx, md[e], 1408, 2048, mdb[e], b)
    emit_stage_cast(ctx, swg, 2048, 2816, swgb, b)
    emit_stage_cast(ctx, swu, 2048, 2816, swub, b)
    emit_stage_cast(ctx, swd, 2816, 2048, swdb, b)

    for s in b["stg"] + b["stgc"]:
        ctx.G.wait_ge(s.h, s.n)
    finalize(ctx)
    ctx._es.close()
    return nc


def emit_rwkv_body(ctx, xv_ext, xv_out, wkT, wvT, wrT, woT, lnb, b, after_chunk0_ln=None):
    T, V, S, Y, G = ctx.T, ctx.V, ctx.S, ctx.Y, ctx.G
    ve = ctx.sem("ve"); se = ctx.sem("se"); mm = ctx.sem("mm"); frp = ctx.sem("frp")
    Tc = 512
    nt = 4

    def vch(op):
        n = inc(op, ve)
        V.wait_ge(ve.h, n)
        return n

    for c in range(5):
        warm = (c == 0)
        ln_done = emit_ln_transpose(ctx, lnb, xv_ext, nt, b["xnT"], Tc, row_base=c * Tc)
        if c == 0 and after_chunk0_ln is not None:
            after_chunk0_ln()
        V.wait_ge(frp.h, ln_done)
        for kt in range(16):
            xns = b["xnT"][:, kt * Tc:(kt + 1) * Tc]
            xsh = b["xsh"]
            vch(V.tensor_copy(xsh[:, 1:Tc], xns[:, 0:Tc - 1]))
            vch(V.tensor_copy(xsh[:, 0:1], b["pcol"][:, kt:kt + 1]))
            vch(V.tensor_tensor(b["tdiff"][:], xns, xsh[:], ALU.subtract))
            vch(V.scalar_tensor_tensor(b["xkT"][:, kt * Tc:(kt + 1) * Tc], b["tdiff"][:],
                                       b["mk_sb"][:, kt:kt + 1], xsh[:], ALU.mult, ALU.add))
            vch(V.scalar_tensor_tensor(b["xvT"][:, kt * Tc:(kt + 1) * Tc], b["tdiff"][:],
                                       b["mv_sb"][:, kt:kt + 1], xsh[:], ALU.mult, ALU.add))
            if not warm:
                vch(V.scalar_tensor_tensor(b["xrT"][:, kt * Tc:(kt + 1) * Tc], b["tdiff"][:],
                                           b["mr_sb"][:, kt:kt + 1], xsh[:], ALU.mult, ALU.add))
            vch(V.tensor_copy(b["pcol"][:, kt:kt + 1], xns[:, Tc - 1:Tc]))
        mix_done = ve.n
        free_at = b.setdefault("pkv_free", {})
        for m in range(16):
            par = m % 2
            wsl = b["wkvr_slot"][par]
            pairs = [(b["wkb"][par][:], wkT[m, :, :]), (b["wvb"][par][:], wvT[m, :, :])]
            if not warm:
                pairs.append((b["wrb"][par][:], wrT[m, :, :]))
            wsl.load(Y, pairs)
            wsl.wait(T)
            if m == 0:
                T.wait_ge(ve.h, mix_done)
            if par in free_at:
                for s_, v_ in free_at[par]:
                    T.wait_ge(s_.h, v_)
            pk = b["pk"][par]; pv = b["pv"][par]; pr = b["pr"][par]
            for kt in range(16):
                T.matmul(pk[:], b["wkb"][par][:, kt * P:(kt + 1) * P],
                         b["xkT"][:, kt * Tc:(kt + 1) * Tc], start=(kt == 0), stop=(kt == 15))
            for kt in range(16):
                ins = T.matmul(pv[:], b["wvb"][par][:, kt * P:(kt + 1) * P],
                               b["xvT"][:, kt * Tc:(kt + 1) * Tc], start=(kt == 0), stop=(kt == 15))
            if not warm:
                for kt in range(16):
                    ins = T.matmul(pr[:], b["wrb"][par][:, kt * P:(kt + 1) * P],
                                   b["xrT"][:, kt * Tc:(kt + 1) * Tc], start=(kt == 0), stop=(kt == 15))
            nT = inc(ins, mm)
            wsl.set_free(mm, nT)
            S.wait_ge(mm.h, nT)
            if b.get("scan_ve_prev"):
                S.wait_ge(ve.h, b["scan_ve_prev"])
            nS = inc(S.activation(b["e_sb"][:], pk[:], AF.Exp,
                                  bias=(b["warmb"][:] if warm else 0.0)), se)
            if not warm:
                nS2 = inc(S.activation(b["sr_sb"][:], pr[:], AF.Sigmoid), se)
            else:
                nS2 = nS
            V.wait_ge(se.h, nS)
            vch(V.tensor_tensor(b["p_sb"][:], b["e_sb"][:], pv[:], ALU.mult))
            lam_bc = b["lam_sb"][:, m:m + 1].broadcast_to((P, Tc))
            vch(V.tensor_tensor_scan(b["s_sb"][:], lam_bc, b["p_sb"][:],
                                     b["stS"][:, m:m + 1], ALU.mult, ALU.add))
            vch(V.tensor_tensor_scan(b["z_sb"][:], lam_bc, b["e_sb"][:],
                                     b["stZ"][:, m:m + 1], ALU.mult, ALU.add))
            nfree_v = ve.n
            if not warm:
                vch(V.tensor_copy(b["sh_sb"][:, 1:Tc], b["s_sb"][:, 0:Tc - 1]))
                vch(V.tensor_copy(b["sh_sb"][:, 0:1], b["stS"][:, m:m + 1]))
                vch(V.tensor_copy(b["zh_sb"][:, 1:Tc], b["z_sb"][:, 0:Tc - 1]))
                vch(V.tensor_copy(b["zh_sb"][:, 0:1], b["stZ"][:, m:m + 1]))
            vch(V.tensor_copy(b["stS"][:, m:m + 1], b["s_sb"][:, Tc - 1:Tc]))
            vch(V.tensor_copy(b["stZ"][:, m:m + 1], b["z_sb"][:, Tc - 1:Tc]))
            if not warm:
                vch(V.scalar_tensor_tensor(b["num_sb"][:], b["p_sb"][:], b["eu_sb"][:, m:m + 1],
                                           b["sh_sb"][:], ALU.mult, ALU.add))
                vch(V.scalar_tensor_tensor(b["den_sb"][:], b["e_sb"][:], b["eu_sb"][:, m:m + 1],
                                           b["zh_sb"][:], ALU.mult, ALU.add))
                vch(V.reciprocal(b["den_sb"][:], b["den_sb"][:]))
                V.wait_ge(se.h, nS2)
                vch(V.tensor_tensor(b["num_sb"][:], b["num_sb"][:], b["den_sb"][:], ALU.mult))
                vch(V.tensor_tensor(b["wkvsr"][:, m * Tc:(m + 1) * Tc], b["num_sb"][:],
                                    b["sr_sb"][:], ALU.mult))
            free_at[par] = [(ctx.sem("ve"), ve.n), (ctx.sem("se"), nS2)]
            b["scan_ve_prev"] = ve.n
        scan_done = ve.n
        if warm:
            continue
        for dq in range(4):
            par = 0
            wsl = b["wo_slot"][par]
            wsl.load(Y, [(b["wob"][par][:],
                          woT[:, :, dq * 512:(dq + 1) * 512].rearrange("k p f -> p k f"))])
            wsl.wait(T)
            if dq == 0:
                T.wait_ge(ve.h, scan_done)
            if dq >= 1:
                T.wait_ge(ve.h, b["wo_ev"][dq - 1])
            for kt in range(16):
                for tt in range(nt):
                    ins = T.matmul(b["po"][tt][:], b["wkvsr"][:, kt * Tc + tt * P: kt * Tc + (tt + 1) * P],
                                   b["wob"][par][:, kt * 512:(kt + 1) * 512],
                                   start=(kt == 0), stop=(kt == 15))
            nT = inc(ins, mm)
            wsl.set_free(mm, nT)
            xsl = b["xr_slot"][par]
            xsl.load(Y, [(b["xrm"][par][tt][:],
                          xv_ext.sl(c * Tc + tt * P, c * Tc + (tt + 1) * P, dq * 512, (dq + 1) * 512))
                         for tt in range(nt)])
            V.wait_ge(mm.h, nT)
            xsl.wait(V)
            stx_p = ctx.sem("rwst0")
            if stx_p.n > 0:
                V.wait_ge(stx_p.h, stx_p.n)
            for tt in range(nt):
                ins = V.tensor_tensor(b["xrw"][par][tt][:], b["po"][tt][:], b["xrm"][par][tt][:],
                                      ALU.add)
            nV = inc(ins, ve)
            xsl.set_free(ve, nV)
            b["wo_ev"][dq] = nV
            G.wait_ge(ve.h, nV)
            for tt in range(nt):
                store_x(G, stx_p, xv_out.sl((c - 1) * Tc + tt * P, (c - 1) * Tc + (tt + 1) * P,
                                            dq * 512, (dq + 1) * 512), b["xrw"][par][tt][:])
        b["pkv_free"] = {0: [(ve, b["wo_ev"][3])], 1: [(ve, b["wo_ev"][3])]}


def alloc_rwkv_bufs(ctx, pool=None):
    Tc = 512
    b = {}
    b["xnT"] = ctx.asb("rw_xnT", [P, 16 * Tc], BF16)
    b["xkT"] = ctx.asb("rw_xkT", [P, 16 * Tc], BF16)
    b["xvT"] = ctx.asb("rw_xvT", [P, 16 * Tc], BF16)
    b["xrT"] = ctx.asb("rw_xrT", [P, 16 * Tc], BF16)
    b["wkvsr"] = ctx.asb("rw_wkvsr", [P, 16 * Tc], BF16)
    b["xsh"] = ctx.asb("rw_xsh", [P, Tc], BF16)
    b["tdiff"] = ctx.asb("rw_tdiff", [P, Tc], BF16)
    b["pcol"] = ctx.asb("rw_pcol", [P, 16], BF16)
    for nm in ["e_sb", "p_sb", "s_sb", "z_sb", "sh_sb", "zh_sb", "num_sb", "den_sb", "sr_sb"]:
        b[nm] = ctx.asb("rw_" + nm, [P, Tc], F32)
    b["stS"] = ctx.asb("rw_stS", [P, 16], F32)
    b["stZ"] = ctx.asb("rw_stZ", [P, 16], F32)
    for nm in ["lam_sb", "eu_sb", "mk_sb", "mv_sb", "mr_sb"]:
        b[nm] = ctx.asb("rw_" + nm, [P, 16], F32)
    b["warmb"] = ctx.asb("rw_warmb", [P, 1], F32)
    b["td_sb"] = ctx.asb("rw_td", [P, 16], F32)
    b["tf_sb"] = ctx.asb("rw_tf", [P, 16], F32)
    b["tmp16"] = ctx.asb("rw_tmp16", [P, 16], F32)
    b["wkb"] = [ctx.asb(f"rw_wkb{j}", [P, 2048], BF16) for j in range(2)]
    b["wvb"] = [ctx.asb(f"rw_wvb{j}", [P, 2048], BF16) for j in range(2)]
    b["wrb"] = [ctx.asb(f"rw_wrb{j}", [P, 2048], BF16) for j in range(2)]
    b["wob"] = [ctx.asb("rw_wob0", [P, 16 * 512], BF16)]
    b["wkvr_slot"] = [Slot(ctx, f"rwkvr{j}") for j in range(2)]
    b["wo_slot"] = [Slot(ctx, "rwwo0")]
    b["xr_slot"] = [Slot(ctx, "rwxr0")]
    b["xrm"] = [[ctx.asb(f"rw_xrm0_{t}", [P, 512], F32) for t in range(4)]]
    b["xrw"] = [[ctx.asb(f"rw_xrw0_{t}", [P, 512], F32) for t in range(4)]]
    if pool is None:
        b["pk"] = [ctx.ps(f"rw_pk{j}", [P, 512], F32) for j in range(2)]
        b["pv"] = [ctx.ps(f"rw_pv{j}", [P, 512], F32) for j in range(2)]
        b["pr"] = [ctx.ps(f"rw_pr{j}", [P, 512], F32) for j in range(2)]
    else:
        b["pk"] = pool[0:2]
        b["pv"] = pool[2:4]
        b["pr"] = pool[4:6]
    b["po"] = b["pk"] + b["pv"]
    b["wo_ev"] = {}
    return b


def emit_rwkv_consts(ctx, b, td_in, tf_in, mk_in, mv_in, mr_in, warmb_in):
    V, S, Y = ctx.V, ctx.S, ctx.Y
    ve = ctx.sem("ve"); se = ctx.sem("se")
    sl = Slot(ctx, "rwconst")
    sl.load(Y, [(b["td_sb"][:], td_in), (b["tf_sb"][:], tf_in), (b["mk_sb"][:], mk_in),
                (b["mv_sb"][:], mv_in), (b["mr_sb"][:], mr_in), (b["warmb"][:], warmb_in)])
    sl.wait(S)
    nS = inc(S.activation(b["tmp16"][:], b["td_sb"][:], AF.Exp), se)
    S.wait_ge(se.h, nS)
    nS = inc(S.activation(b["lam_sb"][:], b["tmp16"][:], AF.Exp, scale=-1.0), se)
    S.wait_ge(se.h, nS)
    nS = inc(S.activation(b["eu_sb"][:], b["tf_sb"][:], AF.Exp), se)
    V.wait_ge(se.h, nS)
    nV = inc(V.memset(b["pcol"][:], 0.0), ve)
    nV = inc(V.memset(b["stS"][:], 0.0), ve)
    nV = inc(V.memset(b["stZ"][:], 0.0), ve)
    V.wait_ge(ve.h, nV)
    return nS, nV


def build_rwkv(nc):
    import contextlib
    ctx = Ctx(nc)
    ctx._es = contextlib.ExitStack()
    dr = lambda nm, sh, dt, kind: nc.dram_tensor(nm, sh, dt, kind=kind).ap()
    I, O = "ExternalInput", "ExternalOutput"
    x_ext = dr("x_ext", [2560, 2048], F32, I)
    wkT = dr("wkT", [16, P, 2048], BF16, I)
    wvT = dr("wvT", [16, P, 2048], BF16, I)
    wrT = dr("wrT", [16, P, 2048], BF16, I)
    woT = dr("woT", [16, P, 2048], BF16, I)
    td = dr("td", [P, 16], F32, I)
    tf = dr("tf", [P, 16], F32, I)
    mkc = dr("mkc", [P, 16], F32, I)
    mvc = dr("mvc", [P, 16], F32, I)
    mrc = dr("mrc", [P, 16], F32, I)
    warmb = dr("warmb", [P, 1], F32, I)
    identb = dr("identb", [P, P], BF16, I)
    xo = dr("xo", [2048, 2048], F32, O)

    lnb = alloc_ln_bufs(ctx)
    lnb["ident_bf"] = ctx.sb("identb_sb", [P, P], BF16)
    lnb["psbf"] = ctx.ps("psbf", [P, 2048], BF16)
    b = alloc_rwkv_bufs(ctx)

    idl = Slot(ctx, "idl")
    idl.load(ctx.Y, [(lnb["ident_bf"][:], identb)])
    idl.wait(ctx.T)
    emit_rwkv_consts(ctx, b, td, tf, mkc, mvc, mrc, warmb)
    emit_rwkv_body(ctx, XV(ap=x_ext), XV(ap=xo), wkT, wvT, wrT, woT, lnb, b)
    s = ctx.sem("rwst0")
    ctx.G.wait_ge(s.h, s.n)
    finalize(ctx)
    ctx._es.close()
    return nc



def emit_moe_ln_gating(ctx, lnb, b, xv, row_base, gwb, gwlo, n_tiles, Tc):
    T, V, S, Y = ctx.T, ctx.V, ctx.S, ctx.Y
    ve = ctx.sem("ve"); se = ctx.sem("se"); mm = ctx.sem("mm"); frp = ctx.sem("frp")
    D = 2048

    def vch(op):
        n = inc(op, ve)
        V.wait_ge(ve.h, n)
        return n

    for i in range(n_tiles):
        par = i % len(lnb["xa"])
        xa = lnb["xa"][par]; xn = lnb["xn"][par]
        sl = lnb["xa_slot"][par]
        sl.load(Y, [(xa[:], xv.sl(row_base + i * P, row_base + (i + 1) * P))])
        sl.wait(V)
        nV = inc(V.reduce_sum(lnb["s1"][:], xa[:], axis=AX.X), ve)
        sl.wait(S)
        nS = inc(S.activation(lnb["sq"][:], xa[:], AF.Square, accum_out=lnb["ss"][:]), se)
        V.wait_ge(se.h, nS)
        V.wait_ge(ve.h, nV)
        vch(V.tensor_scalar(lnb["va"][:], lnb["ss"][:], 1.0 / D, None, ALU.mult))
        vch(V.tensor_scalar(lnb["vb"][:], lnb["s1"][:], 1.0 / D, None, ALU.mult))
        vch(V.tensor_tensor(lnb["vc"][:], lnb["vb"][:], lnb["vb"][:], ALU.mult))
        vch(V.tensor_tensor(lnb["vc"][:], lnb["va"][:], lnb["vc"][:], ALU.subtract))
        nV = vch(V.tensor_scalar(lnb["vc"][:], lnb["vc"][:], 1e-5, None, ALU.add))
        S.wait_ge(ve.h, nV)
        nS = inc(S.activation(lnb["lnv"][:], lnb["vc"][:], AF.Ln), se)
        S.wait_ge(se.h, nS)
        nS = inc(S.activation(lnb["rstd"][:], lnb["lnv"][:], AF.Exp, scale=-0.5), se)
        V.wait_ge(se.h, nS)
        vch(V.tensor_tensor(lnb["nmr"][:], lnb["vb"][:], lnb["rstd"][:], ALU.mult))
        nV = vch(V.tensor_scalar(lnb["nmr"][:], lnb["nmr"][:], -1.0, None, ALU.mult))
        S.wait_ge(ve.h, nV)
        if b.get("sq_free"):
            S.wait_ge(ve.h, b["sq_free"])
        nS = inc(S.activation(lnb["sq"][:], xa[:], AF.Identity, bias=lnb["nmr"][:],
                              scale=lnb["rstd"][:]), se)
        sl.set_free(se, nS)
        V.wait_ge(se.h, nS)
        nV = vch(V.tensor_copy(xn[:], lnb["sq"][:]))
        nV = vch(V.tensor_tensor(b["lo_t"][:], lnb["sq"][:], xn[:], ALU.subtract))
        b["sq_free"] = nV
        T.wait_ge(ve.h, nV)
        if lnb["psbf_busy"]:
            T.wait_ge(frp.h, lnb["psbf_busy"])
        for kt in range(16):
            ins = T.transpose(lnb["psbf"][:, kt * P:(kt + 1) * P], xn[:, kt * P:(kt + 1) * P],
                              lnb["ident_bf"][:])
        nT = inc(ins, mm)
        V.wait_ge(mm.h, nT)
        for kt in range(16):
            ins = V.tensor_copy(b["xnT"][:, kt * Tc + i * P: kt * Tc + (i + 1) * P],
                                lnb["psbf"][:, kt * P:(kt + 1) * P])
        nfrp = inc(ins, frp)
        lnb["psbf_busy"] = nfrp
        T.wait_ge(frp.h, nfrp)
        for kt in range(16):
            ins = T.transpose(lnb["psbf"][:, kt * P:(kt + 1) * P], b["lo_t"][:, kt * P:(kt + 1) * P],
                              lnb["ident_bf"][:])
        nT = inc(ins, mm)
        V.wait_ge(mm.h, nT)
        for kt in range(16):
            ins = V.tensor_copy(b["loT_t"][:, kt * P:(kt + 1) * P], lnb["psbf"][:, kt * P:(kt + 1) * P])
        nfrp = inc(ins, frp)
        lnb["psbf_busy"] = nfrp
        nloT = inc(V.engine_nop(), ve)
        T.wait_ge(ve.h, nloT)
        if b.get("gt_ev"):
            T.wait_ge(ve.h, b["gt_ev"])
        first = True
        for kt in range(16):
            T.matmul(b["pgt"][:, 0:8], b["xnT"][:, kt * Tc + i * P: kt * Tc + (i + 1) * P],
                     gwb[:, kt * 8:(kt + 1) * 8], start=first, stop=False)
            first = False
            T.matmul(b["pgt"][:, 0:8], b["xnT"][:, kt * Tc + i * P: kt * Tc + (i + 1) * P],
                     gwlo[:, kt * 8:(kt + 1) * 8], start=False, stop=False)
            ins = T.matmul(b["pgt"][:, 0:8], b["loT_t"][:, kt * P:(kt + 1) * P],
                           gwb[:, kt * 8:(kt + 1) * 8], start=False, stop=(kt == 15))
        nT = inc(ins, mm)
        V.wait_ge(mm.h, nT)
        tt = i
        lg = b["lg"]; mx = b["mx"]; es_ = b["es"]
        vch(V.tensor_copy(lg[:], b["pgt"][:, 0:8]))
        b["gt_ev"] = ve.n
        vch(V.reduce_max(mx[:], lg[:], axis=AX.X))
        nV = vch(V.tensor_scalar(mx[:], mx[:], -1.0, None, ALU.mult))
        S.wait_ge(ve.h, nV)
        nS = inc(S.activation(es_[:], lg[:], AF.Exp, bias=mx[:], accum_out=b["den8"][:]), se)
        V.wait_ge(se.h, nS)
        vch(V.reciprocal(b["rden8"][:], b["den8"][:]))
        sc = b["sc_sb"][:, tt * 8:(tt + 1) * 8]
        vch(V.tensor_scalar(sc, es_[:], b["rden8"][:], None, ALU.mult))
        vch(V.reduce_max(b["mx1"][:], sc, axis=AX.X))
        vch(V.tensor_scalar(b["is1"][:], sc, b["mx1"][:], None, ALU.is_equal))
        vch(V.scalar_tensor_tensor(b["tmsk"][:], b["is1"][:], -1e30, sc, ALU.mult, ALU.add))
        vch(V.reduce_max(b["mx2"][:], b["tmsk"][:], axis=AX.X))
        vch(V.tensor_scalar(b["is2"][:], b["tmsk"][:], b["mx2"][:], None, ALU.is_equal))
        vch(V.tensor_tensor(b["is1"][:], b["is1"][:], b["is2"][:], ALU.add))
        vch(V.tensor_tensor(b["comb"][:, tt * 8:(tt + 1) * 8], sc, b["is1"][:], ALU.mult))
    return ve.n

def emit_moe_body(ctx, xv_in, xv_out, mgb, mub, mdb, swgb, swub, swdb, gwb_pair, lnb, b):
    import os
    """x, xo [2048, 2048] f32. mgb/mub [8, 2048, 1408] bf16, mdb [8, 1408, 2048] bf16,
    swgb/swub [2048, 2816] bf16, swdb [2816, 2048] bf16. gwb_sb [128, 16*8] bf16 resident."""
    T, V, S, Y, G = ctx.T, ctx.V, ctx.S, ctx.Y, ctx.G
    ve = ctx.sem("ve"); se = ctx.sem("se"); mm = ctx.sem("mm"); frp = ctx.sem("frp")
    Tc = 1024
    nt = 8

    def vch(op):
        n = inc(op, ve)
        V.wait_ge(ve.h, n)
        return n

    gwb_sb, gwlo_sb = gwb_pair
    for c in range(int(os.environ.get('MOE_NCHUNK', 2))):
        comb_done = emit_moe_ln_gating(ctx, lnb, b, xv_in, c * Tc, gwb_sb, gwlo_sb, nt, Tc)
        if "dbg_comb" in b:
            G.wait_ge(ve.h, comb_done)
            dbs = ctx.sem("dbs")
            inc(G.dma_start(out=b["dbg_comb"][c], in_=b["comb"][:]), dbs, 16)
            inc(G.dma_start(out=b["dbg_sc"][c], in_=b["sc_sb"][:]), dbs, 16)
            if c == 1:
                inc(G.dma_start(out=b["dbg_xnT"], in_=b["xnT"][:]), dbs, 16)
            G.wait_ge(dbs.h, dbs.n)
        T.wait_ge(ve.h, comb_done)
        for tt in range(nt):
            ins = T.transpose(b["pct"][tt // 4][0:8, (tt % 4) * P:(tt % 4 + 1) * P],
                              b["comb"][:, tt * 8:(tt + 1) * 8], b["ident_f"][:])
        nT = inc(ins, mm)
        V.wait_ge(mm.h, nT)
        vch(V.tensor_copy(b["combT"][0:8, 0:512], b["pct"][0][0:8, :]))
        vch(V.tensor_copy(b["combT"][0:8, 512:1024], b["pct"][1][0:8, :]))
        ct_done = ve.n

        glist = [4] if os.environ.get('MOE_ONLYSHARED') else list(range(int(os.environ.get('MOE_MAXG', 5))))
        for g in glist:
            shared = (g == 4)
            n_m_half = [(0, 6), (6, 5)] if not shared else [(0, 6), (6, 5), (11, 6), (17, 5)]
            NKT_D = 22
            nexp = 1 if shared else 2
            src_xv = xv_in if (g == 0 or (os.environ.get('MOE_ONLYSHARED') and g == 4)) else xv_out
            h_cols = 0
            if shared and 7 in b["dn_ev"]:
                b["gu_ev0"] = max(b.get("gu_ev0", 0), b["dn_ev"][7])
                b["gu_ev1"] = max(b.get("gu_ev1", 0), b["dn_ev"][7])
            for ei in range(nexp):
                e = 2 * g + ei if not shared else 0
                if not shared:
                    T.wait_ge(ve.h, ct_done)
                    if b.get("cbc_ev"):
                        T.wait_ge(ve.h, b["cbc_ev"])
                    if b.get("gu_ev"):
                        T.wait_ge(ve.h, b["gu_ev"])
                    for nb in range(2):
                        if nb == 1:
                            T.wait_ge(ve.h, ve.n)
                        ins = T.matmul(b["pcb"][:], b["sel8"][:, e * P:(e + 1) * P],
                                       b["combT"][0:8, nb * 512:(nb + 1) * 512],
                                       start=True, stop=True)
                        nT = inc(ins, mm)
                        V.wait_ge(mm.h, nT)
                        vch(V.tensor_copy(b["cbc"][:, nb * 512:(nb + 1) * 512], b["pcb"][:]))
                    b["cbc_ev"] = ve.n
                    if 7 in b["dn_ev"]:
                        b["gu_ev0"] = max(b.get("gu_ev0", 0), b["dn_ev"][7])
                        b["gu_ev1"] = max(b.get("gu_ev1", 0), b["dn_ev"][7])
                for (o0, no) in n_m_half:
                    wsl = b["wgu_slot"]
                    if shared:
                        gsrc, usrc = swgb, swub
                        pairs = []
                        for kt in range(16):
                            pairs.append((b["wg_sb"][:, kt * no * P:(kt + 1) * no * P],
                                          gsrc[kt * P:(kt + 1) * P, o0 * P:(o0 + no) * P]))
                            pairs.append((b["wu_sb"][:, kt * no * P:(kt + 1) * no * P],
                                          usrc[kt * P:(kt + 1) * P, o0 * P:(o0 + no) * P]))
                    else:
                        pairs = []
                        for kt in range(16):
                            pairs.append((b["wg_sb"][:, kt * no * P:(kt + 1) * no * P],
                                          mgb[e, kt * P:(kt + 1) * P, o0 * P:(o0 + no) * P]))
                            pairs.append((b["wu_sb"][:, kt * no * P:(kt + 1) * no * P],
                                          mub[e, kt * P:(kt + 1) * P, o0 * P:(o0 + no) * P]))
                    wsl.load(Y, pairs)
                    wsl.wait(T)
                    for m in range(no):
                        hc = (ei * 11 + o0 + m) * Tc
                        for th in range(2):
                            gpar = b.get("gu_par", 0)
                            b["gu_par"] = 1 - gpar
                            if b.get("gu_ev%d" % gpar):
                                T.wait_ge(ve.h, b["gu_ev%d" % gpar])
                            pg = b["pgu4"][2 * gpar]; pu = b["pgu4"][2 * gpar + 1]
                            for kt in range(16):
                                T.matmul(pg[:], b["wg_sb"][:, (kt * no + m) * P:(kt * no + m + 1) * P],
                                         b["xnT"][:, kt * Tc + th * 512: kt * Tc + th * 512 + 512],
                                         start=(kt == 0), stop=(kt == 15))
                            for kt in range(16):
                                ins = T.matmul(pu[:], b["wu_sb"][:, (kt * no + m) * P:(kt * no + m + 1) * P],
                                               b["xnT"][:, kt * Tc + th * 512: kt * Tc + th * 512 + 512],
                                               start=(kt == 0), stop=(kt == 15))
                            nT = inc(ins, mm)
                            if m == no - 1 and th == 1:
                                wsl.set_free(mm, nT)
                            S.wait_ge(mm.h, nT)
                            if b.get("sg_ev%d" % gpar):
                                S.wait_ge(ve.h, b["sg_ev%d" % gpar])
                            nS = inc(S.activation(b["sg%d" % gpar][:], pg[:], AF.Silu), se)
                            V.wait_ge(se.h, nS)
                            vch(V.tensor_tensor(b["hf"][:, 0:512], b["sg%d" % gpar][:], pu[:], ALU.mult))
                            b["sg_ev%d" % gpar] = ve.n
                            if shared:
                                ins = V.tensor_copy(b["h_sb"][:, hc + th * 512: hc + (th + 1) * 512],
                                                    b["hf"][:, 0:512])
                            else:
                                ins = V.tensor_tensor(b["h_sb"][:, hc + th * 512: hc + (th + 1) * 512],
                                                      b["hf"][:, 0:512],
                                                      b["cbc"][:, th * 512:(th + 1) * 512], ALU.mult)
                            nV = inc(ins, ve)
                            V.wait_ge(ve.h, nV)
                            b["gu_ev%d" % gpar] = nV
            h_done = ve.n
            if "dbg_h" in b and c == 1 and shared:
                dbs = ctx.sem("dbs")
                G.wait_ge(ve.h, h_done)
                inc(G.dma_start(out=b["dbg_h"], in_=b["h_sb"][:]), dbs, 16)
                G.wait_ge(dbs.h, dbs.n)
            for dc in range(8):
                par = dc % 2
                wsl = b["wd_slot"][0]
                if shared:
                    wd_pairs = [(b["wd_sb"][0][:, 0:22 * 256],
                                 swdb[:, dc * 256:(dc + 1) * 256].rearrange("(k p) f -> p k f", p=P))]
                else:
                    wd_pairs = [(b["wd_sb"][0][:, ei * 11 * 256:(ei + 1) * 11 * 256],
                                 mdb[2 * g + ei][:, dc * 256:(dc + 1) * 256].rearrange("(k p) f -> p k f", p=P))
                                for ei in range(nexp)]
                wsl.load(Y, wd_pairs)
                xsl = b["xrm_slot"][par]
                for pp in (0, 1):
                    stq = ctx.sem("moestp" + str(pp))
                    if g > 0 and stq.n > 0:
                        Y.wait_ge(stq.h, stq.n)
                xsl.load(Y, [(b["xrm"][par][tt][:],
                              src_xv.sl(c * Tc + tt * P, c * Tc + (tt + 1) * P, dc * 256, (dc + 1) * 256))
                             for tt in range(nt)])
                wsl.wait(T)
                if dc == 0:
                    T.wait_ge(ve.h, h_done)
                    for gp in (0, 1):
                        if b.get("gu_ev%d" % gp):
                            T.wait_ge(ve.h, b["gu_ev%d" % gp])
                    if 7 in b["dn_ev"]:
                        T.wait_ge(ve.h, b["dn_ev"][7])
                if dc >= 1:
                    T.wait_ge(ve.h, b["dn_ev"][dc - 1])
                for ks in range(NKT_D):
                    for tt in range(nt):
                        psl = b["pdn"][tt // 2][:, (tt % 2) * 256:(tt % 2) * 256 + 256]
                        ins = T.matmul(psl, b["h_sb"][:, ks * Tc + tt * P: ks * Tc + (tt + 1) * P],
                                       b["wd_sb"][0][:, ks * 256:(ks + 1) * 256],
                                       start=(ks == 0 and tt % 2 == 0),
                                       stop=(ks == NKT_D - 1 and tt % 2 == 1))
                nT = inc(ins, mm)
                wsl.set_free(mm, nT)
                V.wait_ge(mm.h, nT)
                xsl.wait(V)
                stx_p = ctx.sem("moestp0")
                if stx_p.n > 0:
                    V.wait_ge(stx_p.h, stx_p.n)
                for tt in range(nt):
                    psl = b["pdn"][tt // 2][:, (tt % 2) * 256:(tt % 2) * 256 + 256]
                    ins = V.scalar_tensor_tensor(b["xrw"][par][tt][:], psl, 0.5,
                                                 b["xrm"][par][tt][:], ALU.mult, ALU.add)
                nV = inc(ins, ve)
                xsl.set_free(ve, nV)
                b["dn_ev"][dc] = nV
                G.wait_ge(ve.h, nV)
                for tt in range(nt):
                    store_x(G, stx_p, xv_out.sl(c * Tc + tt * P, c * Tc + (tt + 1) * P,
                                                dc * 256, (dc + 1) * 256), b["xrw"][par][tt][:])


def alloc_moe_bufs(ctx, pool=None):
    Tc = 1024
    b = {}
    b["xnT"] = ctx.asb("mo_xnT", [P, 16 * Tc], BF16)
    b["h_sb"] = ctx.asb("mo_h", [P, 22 * Tc], BF16)
    b["wg_sb"] = ctx.asb("mo_wg", [P, 16 * 768], BF16)
    b["wu_sb"] = ctx.asb("mo_wu", [P, 16 * 768], BF16)
    b["wd_sb"] = [ctx.asb("mo_wd0", [P, 22 * 256], BF16)]
    b["sg0"] = ctx.asb("mo_sg0", [P, 512], F32)
    b["lo_t"] = ctx.asb("mo_lo_t", [P, 2048], BF16)
    b["sg1"] = ctx.nc.alloc_sbuf_tensor_at("mo_sg1", [P, 512], F32,
                                           offset=ctx.nc.lookup_mloc(b["lo_t"]).addr)
    b["loT_t"] = ctx.asb("mo_loT_t", [P, 2048], BF16)
    b["hf"] = ctx.asb("mo_hf", [P, 512], F32)
    b["cbc"] = ctx.asb("mo_cbc", [P, Tc], F32)
    b["combT"] = ctx.asb("mo_combT", [8, Tc], F32)
    b["sc_sb"] = ctx.asb("mo_sc", [P, 64], F32)
    b["comb"] = ctx.asb("mo_comb", [P, 64], F32)
    for nm in ["lg", "es", "is1", "is2", "tmsk"]:
        b[nm] = ctx.asb("mo_" + nm, [P, 8], F32)
    for nm in ["mx", "mx1", "mx2", "den8", "rden8"]:
        b[nm] = ctx.asb("mo_" + nm, [P, 1], F32)
    b["ones_f"] = ctx.asb("mo_ones", [1, P], F32)
    b["sel8"] = ctx.asb("mo_sel8", [8, 8 * P], F32)
    b["ident_f"] = ctx.asb("mo_identf", [P, P], F32)
    b["xrm"] = [[ctx.asb(f"mo_xrm0_{t}", [P, 256], F32) for t in range(8)]]
    _x1 = [ctx.asb(f"mo_xrm1_{t}", [P, 256], F32) for t in range(4)]
    _x1 += [ctx.nc.alloc_sbuf_tensor_at(f"mo_xrm1_{t}", [P, 256], F32,
                                        offset=ctx.nc.lookup_mloc(b["loT_t"]).addr + (t - 4) * 1024)
            for t in range(4, 8)]
    b["xrm"].append(_x1)
    _xrw = [ctx.asb(f"mo_xrw0_{t}", [P, 256], F32) for t in range(8)]
    b["xrw"] = [_xrw, _xrw]
    b["wgu_slot"] = Slot(ctx, "mowgu")
    b["wd_slot"] = [Slot(ctx, "mowd0")]
    b["xrm_slot"] = [Slot(ctx, "moxrm0"), Slot(ctx, "moxrm1")]
    if pool is None:
        pool = [ctx.ps(f"mo_pl{j}", [P, 512], F32) for j in range(6)]
    b["pgu4"] = pool[0:4]
    b["pdn"] = pool[0:4]
    b["pgt"] = pool[4]
    b["pcb"] = pool[5]
    b["pct"] = [pool[4], pool[5]]
    b["st_mark"] = {}
    b["dn_ev"] = {}
    return b


def build_moe(nc):
    import contextlib
    ctx = Ctx(nc)
    ctx._es = contextlib.ExitStack()
    dr = lambda nm, sh, dt, kind: nc.dram_tensor(nm, sh, dt, kind=kind).ap()
    I, O = "ExternalInput", "ExternalOutput"
    x = dr("x", [2048, 2048], F32, I)
    mgb = dr("mgb", [8, 2048, 1408], BF16, I)
    mub = dr("mub", [8, 2048, 1408], BF16, I)
    mdb = dr("mdb", [8, 1408, 2048], BF16, I)
    swgb = dr("swgb", [2048, 2816], BF16, I)
    swub = dr("swub", [2048, 2816], BF16, I)
    swdb = dr("swdb", [2816, 2048], BF16, I)
    gw = dr("gw", [2048, 8], F32, I)
    identb = dr("identb", [P, P], BF16, I)
    identf = dr("identf", [P, P], F32, I)
    onesf = dr("onesf", [1, P], F32, I)
    sel8 = dr("sel8", [8, 8 * P], F32, I)
    xo = dr("xo", [2048, 2048], F32, O)

    lnb = alloc_ln_bufs(ctx, nx=1)
    lnb["ident_bf"] = ctx.sb("identb_sb", [P, P], BF16)
    lnb["psbf"] = ctx.ps("psbf", [P, 2048], BF16)
    b = alloc_moe_bufs(ctx)

    gw_f = ctx.sb("mo_gwf", [P, 16 * 8], F32)
    gwb = ctx.sb("mo_gwb", [P, 16 * 8], BF16)
    gwlo = ctx.sb("mo_gwlo", [P, 16 * 8], BF16)
    idl = Slot(ctx, "idl")
    idl.load(ctx.Y, [(lnb["ident_bf"][:], identb), (b["ident_f"][:], identf),
                     (b["ones_f"][:], onesf), (b["sel8"][:], sel8),
                     (gw_f[:], gw.rearrange("(k p) e -> p k e", p=P))])
    idl.wait(ctx.V)
    ve = ctx.sem("ve")
    nV = inc(ctx.V.tensor_copy(gwb[:], gw_f[:]), ve)
    ctx.V.wait_ge(ve.h, nV)
    nV = inc(ctx.V.tensor_tensor(gwlo[:], gw_f[:], gwb[:], ALU.subtract), ve)
    ctx.V.wait_ge(ve.h, nV)
    idl.wait(ctx.T)

    import os
    if os.environ.get("MOE_DEBUG"):
        b["dbg_comb"] = dr("dbg_comb", [2, P, 64], F32, O)
        b["dbg_sc"] = dr("dbg_sc", [2, P, 64], F32, O)
        b["dbg_xnT"] = dr("dbg_xnT", [P, 16 * 1024], BF16, O)
        b["dbg_h"] = dr("dbg_h", [P, 22 * 1024], BF16, O)
    emit_moe_body(ctx, XV(ap=x), XV(ap=xo), mgb, mub, mdb, swgb, swub, swdb, (gwb, gwlo), lnb, b)
    for pp in (0, 1):
        s = ctx.sem("moestp" + str(pp))
        ctx.G.wait_ge(s.h, s.n)
    finalize(ctx)
    ctx._es.close()
    return nc


def emit_mla_body(ctx, xv_full, xv_own, xv_out, kvaT, kvbT, qwT, owT, lnb, b):
    T, V, S, Y, G = ctx.T, ctx.V, ctx.S, ctx.Y, ctx.G
    ve = ctx.sem("ve"); se = ctx.sem("se"); mm = ctx.sem("mm"); frp = ctx.sem("frp")
    EPS = 1e-6
    SCALE = 1.0 / math.sqrt(128.0)

    def vch(op):
        n = inc(op, ve)
        V.wait_ge(ve.h, n)
        return n

    for i in range(32):
        par = i % len(lnb["xa"])
        xa = lnb["xa"][par]; xn = lnb["xn"][par]
        sl = lnb["xa_slot"][par]
        sl.load(Y, [(xa[:], xv_full.sl(i * P, (i + 1) * P))])
        sl.wait(V)
        nV = inc(V.reduce_sum(lnb["s1"][:], xa[:], axis=AX.X), ve)
        sl.wait(S)
        nS = inc(S.activation(lnb["sq"][:], xa[:], AF.Square, accum_out=lnb["ss"][:]), se)
        V.wait_ge(se.h, nS)
        V.wait_ge(ve.h, nV)
        vch(V.tensor_scalar(lnb["va"][:], lnb["ss"][:], 1.0 / 2048, None, ALU.mult))
        vch(V.tensor_scalar(lnb["vb"][:], lnb["s1"][:], 1.0 / 2048, None, ALU.mult))
        vch(V.tensor_tensor(lnb["vc"][:], lnb["vb"][:], lnb["vb"][:], ALU.mult))
        vch(V.tensor_tensor(lnb["vc"][:], lnb["va"][:], lnb["vc"][:], ALU.subtract))
        nV = vch(V.tensor_scalar(lnb["vc"][:], lnb["vc"][:], 1e-5, None, ALU.add))
        S.wait_ge(ve.h, nV)
        nS = inc(S.activation(lnb["lnv"][:], lnb["vc"][:], AF.Ln), se)
        S.wait_ge(se.h, nS)
        nS = inc(S.activation(lnb["rstd"][:], lnb["lnv"][:], AF.Exp, scale=-0.5), se)
        V.wait_ge(se.h, nS)
        vch(V.tensor_tensor(lnb["nmr"][:], lnb["vb"][:], lnb["rstd"][:], ALU.mult))
        nV = vch(V.tensor_scalar(lnb["nmr"][:], lnb["nmr"][:], -1.0, None, ALU.mult))
        S.wait_ge(ve.h, nV)
        nS = inc(S.activation(xn[:], xa[:], AF.Identity, bias=lnb["nmr"][:], scale=lnb["rstd"][:]), se)
        sl.set_free(se, nS)
        T.wait_ge(se.h, nS)
        if lnb["psbf_busy"]:
            T.wait_ge(frp.h, lnb["psbf_busy"])
        for kt in range(16):
            ins = T.transpose(lnb["psbf"][:, kt * P:(kt + 1) * P], xn[:, kt * P:(kt + 1) * P],
                              lnb["ident_bf"][:])
        nT = inc(ins, mm)
        V.wait_ge(mm.h, nT)
        if b.get("xnt_ev"):
            V.wait_ge(mm.h, b["xnt_ev"])
        for kt in range(16):
            ins = V.tensor_copy(b["xnT_t"][:, kt * P:(kt + 1) * P], lnb["psbf"][:, kt * P:(kt + 1) * P])
        nfrp = inc(ins, frp)
        lnb["psbf_busy"] = nfrp
        T.wait_ge(frp.h, nfrp)
        if b.get("kvc_ev"):
            T.wait_ge(ve.h, b["kvc_ev"])
        for kt in range(16):
            ins = T.matmul(b["pkv"][:], b["xnT_t"][:, kt * P:(kt + 1) * P],
                           b["kvaT_sb"][:, kt * 512:(kt + 1) * 512], start=(kt == 0), stop=(kt == 15))
        nT = inc(ins, mm)
        b["xnt_ev"] = nT
        S.wait_ge(mm.h, nT)
        if b.get("lat_sq_ev"):
            S.wait_ge(ve.h, b["lat_sq_ev"])
        nS = inc(S.activation(b["sq512"][:], b["pkv"][:], AF.Square, accum_out=b["ss2"][:]), se)
        V.wait_ge(se.h, nS)
        vch(V.tensor_scalar(b["va2"][:], b["ss2"][:], 1.0 / 512, None, ALU.mult))
        nV = vch(V.tensor_scalar(b["va2"][:], b["va2"][:], EPS, None, ALU.add))
        S.wait_ge(ve.h, nV)
        nS = inc(S.activation(b["lnv2"][:], b["va2"][:], AF.Ln), se)
        S.wait_ge(se.h, nS)
        nS = inc(S.activation(b["rstd2"][:], b["lnv2"][:], AF.Exp, scale=-0.5), se)
        V.wait_ge(se.h, nS)
        vch(V.tensor_scalar(b["lat32"][:], b["pkv"][:], b["rstd2"][:], None, ALU.mult))
        b["kvc_ev"] = ve.n
        nV = vch(V.tensor_tensor(b["lat_t"][:], b["lat32"][:], b["wkv_bc"][:], ALU.mult))
        b["lat_sq_ev"] = nV
        T.wait_ge(ve.h, nV)
        T.wait_ge(frp.h, lnb["psbf_busy"])
        for kt in range(4):
            ins = T.transpose(lnb["psbf"][:, kt * P:(kt + 1) * P], b["lat_t"][:, kt * P:(kt + 1) * P],
                              lnb["ident_bf"][:])
        nT = inc(ins, mm)
        V.wait_ge(mm.h, nT)
        for kt in range(4):
            ins = V.tensor_copy(b["latT"][:, kt * 4096 + i * P: kt * 4096 + (i + 1) * P],
                                lnb["psbf"][:, kt * P:(kt + 1) * P])
        nfrp = inc(ins, frp)
        lnb["psbf_busy"] = nfrp
    latT_done = ctx.sem("frp").n

    def rope_inplace(buf, cosb, sinb, F):
        rsem = ctx.sem("rope")
        G.wait_ge(ve.h, ve.n)
        if b.get("rope_free"):
            G.wait_ge(ve.h, b["rope_free"])
        inc(G.dma_start(out=b["rot"][0:32, 0:F], in_=buf[32:64, 0:F]), rsem, 16)
        inc(G.dma_start(out=b["rot"][32:64, 0:F], in_=buf[0:32, 0:F]), rsem, 16)
        V.wait_ge(rsem.h, rsem.n)
        vch(V.tensor_scalar(b["rot"][0:32, 0:F], b["rot"][0:32, 0:F], -1.0, None, ALU.mult))
        vch(V.tensor_tensor(b["rt1"][0:64, 0:F], buf[0:64, 0:F], cosb[0:64, 0:F], ALU.mult))
        vch(V.tensor_tensor(b["rot"][0:64, 0:F], b["rot"][0:64, 0:F], sinb[0:64, 0:F], ALU.mult))
        nV = vch(V.tensor_tensor(buf[0:64, 0:F], b["rt1"][0:64, 0:F], b["rot"][0:64, 0:F], ALU.add))
        b["rope_free"] = nV
        return nV

    for toh in range(2):
        own_done = emit_ln_transpose(ctx, lnb, xv_own, 8, b["xnT_own"], 1024,
                                      row_base=toh * 1024)
        for h in range(16):
            qsl = b["qw_slot"]
            qsl.load(Y, [(b["qwb"][:], qwT[h, :, :])])
            ksl = b["kvb_slot"]
            ksl.load(Y, [(b["kvbb"][:, 0:512], kvbT[2 * h, :, :]),
                         (b["kvbb"][:, 512:1024], kvbT[2 * h + 1, :, :])])
            ksl.wait(T)
            if toh == 0 and h == 0:
                T.wait_ge(frp.h, latT_done)
            T.wait_ge(frp.h, own_done)
            for nb in range(8):
                pk = b["psc"][nb % 2]
                if b.get("sc_ev") and (nb % 2) in b["sc_ev"]:
                    T.wait_ge(se.h, b["sc_ev"][nb % 2])
                if nb >= 2:
                    T.wait_ge(ve.h, b["kevac"][nb - 2])
                for kt in range(4):
                    ins = T.matmul(pk[:], b["kvbb"][:, kt * P:(kt + 1) * P],
                                   b["latT"][:, kt * 4096 + nb * 512: kt * 4096 + (nb + 1) * 512],
                                   start=(kt == 0), stop=(kt == 3))
                nT = inc(ins, mm)
                V.wait_ge(mm.h, nT)
                if nb == 0 and b.get("kb_war"):
                    V.wait_ge(mm.h, b["kb_war"])
                ins = V.tensor_copy(b["kb"][:, nb * 512:(nb + 1) * 512], pk[:])
                b["kevac"][nb] = inc(ins, ve)
            rope_inplace(b["kb"], b["kcos"], b["ksin"], 4096)
            kb_done = ctx.sem("ve").n
            for tk in range(32):
                pv = b["pav"]
                if b.get("v_ev"):
                    T.wait_ge(ve.h, b["v_ev"])
                if tk == 0 and b.get("av_ev"):
                    T.wait_ge(ve.h, b["av_ev"])
                for kt in range(4):
                    ins = T.matmul(pv[:, 0:P], b["latT"][:, kt * 4096 + tk * P: kt * 4096 + (tk + 1) * P],
                                   b["kvbb"][:, 512 + kt * P: 512 + (kt + 1) * P],
                                   start=(kt == 0), stop=(kt == 3))
                nT = inc(ins, mm)
                ksl.set_free(mm, nT)
                V.wait_ge(mm.h, nT)
                ins = V.tensor_copy(b["v_sb"][:, tk * P:(tk + 1) * P], pv[:, 0:P])
                b["v_ev"] = inc(ins, ve)
            v_done = ctx.sem("ve").n
            qsl.wait(T)
            for nb in range(2):
                pq = b["psc"][nb % 2]
                if b.get("sc_ev") and (nb % 2) in b["sc_ev"]:
                    T.wait_ge(se.h, b["sc_ev"][nb % 2])
                if b.get("q_ev"):
                    T.wait_ge(ve.h, b["q_ev"])
                for kt in range(16):
                    ins = T.matmul(pq[:], b["qwb"][:, kt * P:(kt + 1) * P],
                                   b["xnT_own"][:, kt * 1024 + nb * 512: kt * 1024 + (nb + 1) * 512],
                                   start=(kt == 0), stop=(kt == 15))
                nT = inc(ins, mm)
                qsl.set_free(mm, nT)
                V.wait_ge(mm.h, nT)
                ins = V.tensor_copy(b["qb"][:, nb * 512:(nb + 1) * 512], pq[:])
                inc(ins, ve)
            q_done = rope_inplace(b["qb"], b["qcs"][toh][0], b["qcs"][toh][1], 1024)
            b["q_ev"] = q_done
            for tq in range(8):
                for nb in range(8):
                    ps = b["psc"][nb % 2]
                    if b.get("sc_ev") and (nb % 2) in b["sc_ev"]:
                        T.wait_ge(se.h, b["sc_ev"][nb % 2])
                    if nb == 0 and tq == 0:
                        T.wait_ge(ve.h, max(kb_done, q_done, v_done))
                    ins = T.matmul(ps[:], b["qb"][:, tq * P:(tq + 1) * P],
                                   b["kb"][:, nb * 512:(nb + 1) * 512], start=True, stop=True)
                    nT = inc(ins, mm)
                    S.wait_ge(mm.h, nT)
                    if nb == 0 and b.get("sm_done"):
                        S.wait_ge(ve.h, b["sm_done"])
                    ins = S.activation(b["sc_sb"][:, nb * 512:(nb + 1) * 512], ps[:], AF.Identity,
                                       scale=SCALE)
                    nS = inc(ins, se)
                    b.setdefault("sc_ev", {})[nb % 2] = nS
                V.wait_ge(se.h, nS)
                nV = inc(V.reduce_max(b["smx"][:], b["sc_sb"][:], axis=AX.X), ve)
                V.wait_ge(ve.h, nV)
                nV = vch(V.tensor_scalar(b["smx"][:], b["smx"][:], -1.0, None, ALU.mult))
                S.wait_ge(ve.h, nV)
                if b.get("ab_free"):
                    S.wait_ge(mm.h, b["ab_free"])
                nS = inc(S.activation(b["ab"][:], b["sc_sb"][:], AF.Exp, bias=b["smx"][:],
                                      accum_out=b["sden"][:]), se)
                V.wait_ge(se.h, nS)
                nV = vch(V.reciprocal(b["srden"][:], b["sden"][:]))
                b["sm_done"] = nV
                T.wait_ge(se.h, nS)
                for half in range(2):
                    T.wait_ge(frp.h, lnb["psbf_busy"])
                    for j in range(16):
                        ins = T.transpose(lnb["psbf"][:, j * P:(j + 1) * P],
                                          b["ab"][:, (half * 16 + j) * P:(half * 16 + j + 1) * P],
                                          lnb["ident_bf"][:])
                    nT = inc(ins, mm)
                    V.wait_ge(mm.h, nT)
                    if b.get("abT_war"):
                        V.wait_ge(mm.h, b["abT_war"])
                    for j in range(16):
                        ins = V.tensor_copy(b["abT"][:, (half * 16 + j) * P:(half * 16 + j + 1) * P],
                                            lnb["psbf"][:, j * P:(j + 1) * P])
                    nfrp = inc(ins, frp)
                    lnb["psbf_busy"] = nfrp
                abT_done = inc(V.engine_nop(), ve)
                b["ab_free"] = 0
                T.wait_ge(ve.h, abT_done)
                if b.get("av_ev"):
                    T.wait_ge(ve.h, b["av_ev"])
                for kt in range(32):
                    ins = T.matmul(b["pav"][:, 0:P], b["abT"][:, kt * P:(kt + 1) * P],
                                   b["v_sb"][:, kt * P:(kt + 1) * P], start=(kt == 0), stop=(kt == 31))
                nT = inc(ins, mm)
                b["ab_free"] = nT
                b["abT_war"] = nT
                V.wait_ge(mm.h, nT)
                if b.get("ao_st"):
                    V.wait_ge(ctx.sem("aost").h, b["ao_st"])
                ins = V.tensor_scalar(b["ao_t"][:], b["pav"][:, 0:P], b["srden"][:], None, ALU.mult)
                nV = inc(ins, ve)
                b["av_ev"] = nV
                G.wait_ge(ve.h, nV)
                aost = ctx.sem("aost")
                b["ao_st"] = inc(G.dma_start(out=b["attn_out"][h, toh * 1024 + tq * P: toh * 1024 + (tq + 1) * P, :],
                                             in_=b["ao_t"][:]), aost, 16)
            b["kb_war"] = ctx.sem("mm").n
    head_done_st = ctx.sem("aost").n

    for tt in range(16):
        asl = b["ao_slot"]
        if tt == 0:
            Y.wait_ge(ctx.sem("aost").h, head_done_st)
        asl.load(Y, [(b["aon"][:, hh * P:(hh + 1) * P], b["attn_out"][hh, tt * P:(tt + 1) * P, :])
                     for hh in range(16)])
        asl.wait(T)
        if lnb["psbf_busy"]:
            T.wait_ge(frp.h, lnb["psbf_busy"])
        for hh in range(16):
            ins = T.transpose(lnb["psbf"][:, hh * P:(hh + 1) * P], b["aon"][:, hh * P:(hh + 1) * P],
                              lnb["ident_bf"][:])
        nT = inc(ins, mm)
        asl.set_free(mm, nT)
        V.wait_ge(mm.h, nT)
        if b.get("aoT_ev"):
            V.wait_ge(mm.h, b["aoT_ev"])
        for hh in range(16):
            ins = V.tensor_copy(b["aoT"][:, hh * P:(hh + 1) * P], lnb["psbf"][:, hh * P:(hh + 1) * P])
        nfrp = inc(ins, frp)
        lnb["psbf_busy"] = nfrp
        aoT_done = inc(V.engine_nop(), ve)
        T.wait_ge(ve.h, aoT_done)
        if b.get("o_ev"):
            T.wait_ge(ve.h, b["o_ev"])
        for kt in range(16):
            osl = b["ow_slot"]
            osl.load(Y, [(b["owb"][:], owT[kt, :, :])])
            osl.wait(T)
            for dc in range(4):
                ins = T.matmul(b["po"][dc][:], b["aoT"][:, kt * P:(kt + 1) * P],
                               b["owb"][:, dc * 512:(dc + 1) * 512],
                               start=(kt == 0), stop=(kt == 15))
            nT = inc(ins, mm)
            osl.set_free(mm, nT)
        b["aoT_ev"] = nT
        xsl = b["xr_slot"]
        xsl.load(Y, [(b["xrm"][dc][:], xv_own.sl(tt * P, (tt + 1) * P, dc * 512, (dc + 1) * 512))
                     for dc in range(4)])
        V.wait_ge(mm.h, nT)
        xsl.wait(V)
        stq = ctx.sem("mlast")
        if stq.n > 0:
            V.wait_ge(stq.h, stq.n)
        for dc in range(4):
            ins = V.tensor_tensor(b["xrw"][dc][:], b["po"][dc][:], b["xrm"][dc][:], ALU.add)
        nV = inc(ins, ve)
        xsl.set_free(ve, nV)
        b["o_ev"] = nV
        G.wait_ge(ve.h, nV)
        for dc in range(4):
            store_x(G, stq, xv_out.sl(tt * P, (tt + 1) * P, dc * 512, (dc + 1) * 512),
                    b["xrw"][dc][:])


def alloc_mla_bufs(ctx, attn_out_dram, pool=None):
    b = {}
    b["kvaT_sb"] = ctx.asb("ml_kvaT", [P, 16 * 512], BF16)
    b["latT"] = ctx.asb("ml_latT", [P, 4 * 4096], BF16)
    b["xnT_own"] = ctx.asb("ml_xnTo", [P, 16 * 1024], BF16)
    b["xnT_t"] = ctx.nc.alloc_sbuf_tensor_at("ml_xnTt", [P, 16 * P], BF16,
                                             offset=ctx.nc.lookup_mloc(b["xnT_own"]).addr)
    b["sq512"] = ctx.asb("ml_sq512", [P, 512], F32)
    b["lat32"] = ctx.asb("ml_lat32", [P, 512], F32)
    b["lat_t"] = ctx.asb("ml_latt", [P, 512], BF16)
    b["wkv_bc"] = ctx.asb("ml_wkvbc", [P, 512], F32)
    for nm in ["ss2", "va2", "lnv2", "rstd2", "smx", "sden", "srden"]:
        b[nm] = ctx.asb("ml_" + nm, [P, 1], F32)
    b["kb"] = ctx.asb("ml_kb", [P, 4096], BF16)
    b["qb"] = ctx.asb("ml_qb", [P, 1024], BF16)
    b["v_sb"] = ctx.asb("ml_v", [P, 32 * P], BF16)
    b["ab"] = ctx.asb("ml_ab", [P, 4096], BF16)
    b["abT"] = ctx.asb("ml_abT", [P, 32 * P], BF16)
    b["sc_sb"] = ctx.asb("ml_sc", [P, 4096], F32)
    b["rot"] = ctx.nc.alloc_sbuf_tensor_at("ml_rot", [64, 4096], BF16,
                                           offset=ctx.nc.lookup_mloc(b["kvaT_sb"]).addr + 256)
    b["rt1"] = ctx.nc.alloc_sbuf_tensor_at("ml_rt1", [64, 4096], BF16,
                                           offset=ctx.nc.lookup_mloc(b["kvaT_sb"]).addr + 256 + 8192)
    b["kcos"] = ctx.asb("ml_kcos", [64, 4096], BF16)
    b["ksin"] = ctx.asb("ml_ksin", [64, 4096], BF16)
    b["qcs"] = [[ctx.asb(f"ml_qcos{t}", [64, 1024], BF16), ctx.asb(f"ml_qsin{t}", [64, 1024], BF16)]
                for t in range(2)]
    b["qwb"] = ctx.asb("ml_qwb", [P, 2048], BF16)
    b["kvbb"] = ctx.asb("ml_kvbb", [P, 1024], BF16)
    nc = ctx.nc
    def sb_at(name, shape, dt, donor, off):
        base = nc.lookup_mloc(donor).addr
        return nc.alloc_sbuf_tensor_at(name, shape, dt, offset=base + off)
    b["owb"] = sb_at("ml_owb", [P, 2048], BF16, b["kb"], 0)
    b["aon"] = sb_at("ml_aon", [P, 16 * P], BF16, b["kb"], 4096)
    b["aoT"] = sb_at("ml_aoT", [P, 16 * P], BF16, b["ab"], 0)
    b["ao_t"] = sb_at("ml_aot", [P, P], BF16, b["kvaT_sb"], 0)
    b["xrm"] = [sb_at(f"ml_xrm{t}", [P, 512], F32, b["sc_sb"], t * 2048) for t in range(4)]
    b["xrw"] = [sb_at(f"ml_xrw{t}", [P, 512], F32, b["sc_sb"], 8192 + t * 2048) for t in range(4)]
    b["qw_slot"] = Slot(ctx, "mlqw")
    b["kvb_slot"] = Slot(ctx, "mlkvb")
    b["ow_slot"] = Slot(ctx, "mlow")
    b["ao_slot"] = Slot(ctx, "mlao")
    b["xr_slot"] = Slot(ctx, "mlxr")
    if pool is None:
        b["pkv"] = ctx.ps("ml_pkv", [P, 512], F32)
        b["psc"] = [ctx.ps(f"ml_psc{j}", [P, 512], F32) for j in range(2)]
        b["pav"] = ctx.ps("ml_pav", [P, 512], F32)
        b["po2"] = [ctx.ps(f"ml_po{j}", [P, 512], F32) for j in range(2)]
    else:
        b["pkv"] = pool[0]
        b["psc"] = pool[1:3]
        b["pav"] = pool[3]
        b["po2"] = pool[4:6]
    b["po"] = b["po2"] + [b["pkv"], b["pav"]]
    b["attn_out"] = attn_out_dram
    b["kevac"] = {}
    return b


def build_mla(nc):
    import contextlib
    ctx = Ctx(nc)
    ctx._es = contextlib.ExitStack()
    dr = lambda nm, sh, dt, kind: nc.dram_tensor(nm, sh, dt, kind=kind).ap()
    I, O = "ExternalInput", "ExternalOutput"
    x_full = dr("x_full", [4096, 2048], F32, I)
    x_own = dr("x_own", [2048, 2048], F32, I)
    kvaT = dr("kvaT", [16, P, 512], BF16, I)
    kvbT = dr("kvbT", [32, P, 512], BF16, I)
    qwT = dr("qwT", [16, P, 2048], BF16, I)
    owT = dr("owT", [16, P, 2048], BF16, I)
    kvnw = dr("kvnw", [1, 512], F32, I)
    kcos = dr("kcos", [64, 4096], BF16, I)
    ksin = dr("ksin", [64, 4096], BF16, I)
    qcos = dr("qcos", [64, 2048], BF16, I)
    qsin = dr("qsin", [64, 2048], BF16, I)
    identb = dr("identb", [P, P], BF16, I)
    onesf = dr("onesf", [1, P], F32, I)
    xo = dr("xo", [2048, 2048], F32, O)
    attn_out = nc.dram_tensor("attn_sc", [16, 2048, P], BF16).ap()

    lnb = alloc_ln_bufs(ctx, nx=1)
    lnb["ident_bf"] = ctx.sb("identb_sb", [P, P], BF16)
    lnb["psbf"] = ctx.ps("psbf", [P, 2048], BF16)
    b = alloc_mla_bufs(ctx, attn_out)
    def _sb_at(name, shape, dt, donor, off):
        return nc.alloc_sbuf_tensor_at(name, shape, dt, offset=nc.lookup_mloc(donor).addr + off)
    ones_sb = _sb_at("ml_ones", [1, P], F32, b["aoT"], 0)
    kvnw_sb = _sb_at("ml_kvnw", [1, 512], F32, b["aoT"], 512)

    idl = Slot(ctx, "idl")
    idl.load(ctx.Y, [(lnb["ident_bf"][:], identb), (ones_sb[:], onesf),
                     (kvnw_sb[:], kvnw), (b["kcos"][:], kcos), (b["ksin"][:], ksin),
                     (b["qcs"][0][0][:], qcos[:, 0:1024]), (b["qcs"][0][1][:], qsin[:, 0:1024]),
                     (b["qcs"][1][0][:], qcos[:, 1024:2048]), (b["qcs"][1][1][:], qsin[:, 1024:2048]),
                     (b["kvaT_sb"][:], kvaT.rearrange("k p f -> p k f"))])
    idl.wait(ctx.T)
    mm = ctx.sem("mm"); ve = ctx.sem("ve")
    nT = inc(ctx.T.matmul(b["pkv"][:], ones_sb[0:1, :], kvnw_sb[:], start=True, stop=True), mm)
    ctx.V.wait_ge(mm.h, nT)
    nV = inc(ctx.V.tensor_copy(b["wkv_bc"][:], b["pkv"][:]), ve)
    ctx.V.wait_ge(ve.h, nV)

    emit_mla_body(ctx, XV(ap=x_full), XV(ap=x_own), XV(ap=xo), kvaT, kvbT, qwT, owT, lnb, b)
    s = ctx.sem("mlast")
    ctx.G.wait_ge(s.h, s.n)
    finalize(ctx)
    ctx._es.close()
    return nc


def build_fused(nc):
    import contextlib, os
    STOP = int(os.environ.get("FUSED_STOP_AFTER", "6"))
    ctx = Ctx(nc)
    ctx._es = contextlib.ExitStack()
    dr = lambda nm, sh, dt, kind, **kw: nc.dram_tensor(nm, sh, dt, kind=kind, **kw).ap()
    I, O = "ExternalInput", "ExternalOutput"
    x_in = dr("x_in", [2048, 2048], F32, I)
    slot = dr("slot", [1, 1], I32, I)
    kconst = dr("kconst", [1, 16], I32, I)
    gidx = dr("gidx", [5504, 2048], I32, I)
    uidx = dr("uidx", [5504, 2048], I32, I)
    didx = dr("didx", [2048, 5504], I32, I)
    lut = dr("lut", [16], F32, I)
    wk = dr("wk", [2048, 2048], F32, I); wv = dr("wv", [2048, 2048], F32, I)
    wr = dr("wr", [2048, 2048], F32, I); wo = dr("wo", [2048, 2048], F32, I)
    qw = dr("qw", [2048, 2048], F32, I); kva = dr("kva", [576, 2048], F32, I)
    kvb = dr("kvb", [4096, 512], F32, I); ow = dr("ow", [2048, 2048], F32, I)
    mg = dr("mg", [8, 2048, 1408], F32, I); mu = dr("mu", [8, 2048, 1408], F32, I)
    md = dr("md", [8, 1408, 2048], F32, I)
    swg = dr("swg", [2048, 2816], F32, I); swu = dr("swu", [2048, 2816], F32, I)
    swd = dr("swd", [2816, 2048], F32, I)
    gw = dr("gw", [2048, 8], F32, I)
    identb = dr("identb", [P, P], BF16, I)
    identf = dr("identf", [P, P], F32, I)
    onesf = dr("onesf", [1, P], F32, I)
    sel8 = dr("sel8", [8, 8 * P], F32, I)
    td = dr("td", [P, 16], F32, I); tf = dr("tf", [P, 16], F32, I)
    mkc = dr("mkc", [P, 16], F32, I); mvc = dr("mvc", [P, 16], F32, I)
    mrc = dr("mrc", [P, 16], F32, I)
    warmb = dr("warmb", [P, 1], F32, I)
    kvnw = dr("kvnw", [1, 512], F32, I)
    kcos = dr("kcos", [64, 4096], BF16, I); ksin = dr("ksin", [64, 4096], BF16, I)
    qcos = dr("qcos", [64, 2048], BF16, I); qsin = dr("qsin", [64, 2048], BF16, I)
    xo_ext = dr("xo", [2048, 2048], F32, O)
    SH = {"addr_space": "Shared"}
    wgt = dr("wgt", [43, P, 2048], BF16, "Internal", **SH)
    wut = dr("wut", [43, P, 2048], BF16, "Internal", **SH)
    wdt = dr("wdt", [43, P, 2048], BF16, "Internal", **SH)
    wkT = dr("wkT", [16, P, 2048], BF16, "Internal", **SH)
    wvT = dr("wvT", [16, P, 2048], BF16, "Internal", **SH)
    wrT = dr("wrT", [16, P, 2048], BF16, "Internal", **SH)
    woT = dr("woT", [16, P, 2048], BF16, "Internal", **SH)
    qwT = dr("qwT", [16, P, 2048], BF16, "Internal", **SH)
    kvaT = dr("kvaT", [16, P, 512], BF16, "Internal", **SH)
    kvbT = dr("kvbT", [32, P, 512], BF16, "Internal", **SH)
    owT = dr("owT", [16, P, 2048], BF16, "Internal", **SH)
    mgb = dr("mgb", [8, 2048, 1408], BF16, "Internal", **SH)
    mub = dr("mub", [8, 2048, 1408], BF16, "Internal", **SH)
    mdb = dr("mdb", [8, 1408, 2048], BF16, "Internal", **SH)
    swgb = dr("swgb", [2048, 2816], BF16, "Internal", **SH)
    swub = dr("swub", [2048, 2816], BF16, "Internal", **SH)
    swdb = dr("swdb", [2816, 2048], BF16, "Internal", **SH)
    attn_out = dr("attn_sc", [16, 2048, P], BF16, "Internal")
    xs0 = dr("xs0", [4608, 2048], F32, "Internal", addr_space="Shared")
    xs1 = dr("xs1", [4608, 2048], F32, "Internal", addr_space="Shared")
    flags = dr("flags", [2, 16], I32, "Internal", addr_space="Shared")

    T, V, S, G, Y = ctx.T, ctx.V, ctx.S, ctx.G, ctx.Y

    lnb = alloc_ln_bufs(ctx, nx=1)
    lnb["ident_bf"] = ctx.sb("identb_sb", [P, P], BF16)
    lnb["psbf"] = ctx.ps("psbf", [P, 2048], BF16)
    pool = [ctx.ps(f"pf{j}", [P, 512], F32) for j in range(6)]
    ident_f = ctx.sb("identf_sb", [P, P], F32)
    ones_sb = ctx.sb("ones_sb", [1, P], F32)
    sel8_sb = ctx.sb("sel8_sb", [8, 8 * P], F32)
    kconst_sb = ctx.sb("kconst_sb", [1, 16], I32)
    pollbuf = ctx.sb("pollbuf", [1, 2], I32)
    gw_f = ctx.sb("gw_f", [P, 16 * 8], F32)
    gwb = ctx.sb("gwb", [P, 16 * 8], BF16)
    gwlo = ctx.sb("gwlo", [P, 16 * 8], BF16)
    lut_sb = ctx.sb("lut_sb", [P, 16], F32)
    emit_slot_regs(ctx, slot)
    avail = nc.sbuf_top - nc.sbuf_base - 64
    avail = (avail // 32) * 32
    arena = Arena(ctx, avail)
    ctx.arena = arena
    ve = ctx.sem("ve"); se = ctx.sem("se"); mm = ctx.sem("mm")

    idl = Slot(ctx, "idl")
    idl.load(Y, [(lnb["ident_bf"][:], identb), (ident_f[:], identf), (ones_sb[:], onesf),
                 (sel8_sb[:], sel8), (kconst_sb[:], kconst),
                 (gw_f[:], gw.rearrange("(k p) e -> p k e", p=P))])
    idl.wait(V)
    nV = inc(V.tensor_copy(gwb[:], gw_f[:]), ve)
    V.wait_ge(ve.h, nV)
    nV = inc(V.tensor_tensor(gwlo[:], gw_f[:], gwb[:], ALU.subtract), ve)
    V.wait_ge(ve.h, nV)
    idl.wait(T)
    idl.wait(S)

    zinit = ctx.sem("zinit")
    nV = inc(V.memset(lnb["sq"][:], 0.0), ve)
    G.wait_ge(ve.h, nV)
    for i in range(4):
        inc(G.dma_start(out=xs0[i * P:(i + 1) * P, :], in_=lnb["sq"][:]), zinit, 16)
    G.wait_ge(zinit.h, zinit.n)

    sb_ = alloc_stage_bufs(ctx)
    sb_["ident_bf"] = lnb["ident_bf"]
    sb_["psbf"] = lnb["psbf"]
    nV = emit_lut_bcast(ctx, lut, lut_sb, ones_sb, pool[0])
    V.wait_ge(ctx.sem("ve").h, nV)
    def _spl(n):
        a = (n + 1) // 2
        return list(range(a)), [min(a + i, n - 1) for i in range(a)]
    gA, gB = _spl(43)
    emit_stage_ghost_lhsT(ctx, gidx, wgt, lut_sb, sb_, gA, gB)
    emit_stage_ghost_lhsT(ctx, uidx, wut, lut_sb, sb_, gA, gB)
    dA, dB = _spl(16)
    emit_stage_ghost_rhs(ctx, didx, wdt, lut_sb, sb_, dA, dB)
    tA, tB = _spl(16)
    emit_stage_transpose(ctx, wk, 2048, 2048, wkT, "blk", sb_, tA, slab_range2=tB)
    emit_stage_transpose(ctx, wv, 2048, 2048, wvT, "blk", sb_, tA, slab_range2=tB)
    emit_stage_transpose(ctx, wr, 2048, 2048, wrT, "blk", sb_, tA, slab_range2=tB)
    emit_stage_transpose(ctx, wo, 2048, 2048, woT, "rowmajor", sb_, tA, slab_range2=tB)
    emit_stage_transpose(ctx, qw, 2048, 2048, qwT, "blk", sb_, tA, slab_range2=tB)
    kA, kB = _spl(4)
    emit_stage_transpose(ctx, kva, 512, 2048, kvaT, "rowmajor", sb_, kA, src_row0=64, slab_range2=kB)
    vA, vB = _spl(32)
    emit_stage_transpose(ctx, kvb, 4096, 512, kvbT, "blk", sb_, vA, slab_range2=vB)
    emit_stage_transpose(ctx, ow, 2048, 2048, owT, "rowmajor", sb_, tA, slab_range2=tB)
    cA16, cB16 = _spl(16)
    cA11, cB11 = _spl(11)
    cA22, cB22 = _spl(22)
    for e in range(8):
        emit_stage_cast(ctx, mg[e], 2048, 1408, mgb[e], sb_, cA16, cB16)
        emit_stage_cast(ctx, mu[e], 2048, 1408, mub[e], sb_, cA16, cB16)
        emit_stage_cast(ctx, md[e], 1408, 2048, mdb[e], sb_, cA11, cB11)
    emit_stage_cast(ctx, swg, 2048, 2816, swgb, sb_, cA16, cB16)
    emit_stage_cast(ctx, swu, 2048, 2816, swub, sb_, cA16, cB16)
    emit_stage_cast(ctx, swd, 2816, 2048, swdb, sb_, cA22, cB22)
    for s in sb_["stg"] + sb_["stgc"]:
        G.wait_ge(s.h, s.n)
    emit_flag_write(ctx, flags, kconst_sb, 3)
    v3 = emit_flag_poll(ctx, flags, 3, pollbuf, tag="st")
    Y.wait_ge(ctx.sems["barok"].h, v3)
    emit_join(ctx)
    arena.reset()

    ffn_b = alloc_ffn0_bufs(ctx, pool=pool)
    emit_ffn0_body(ctx, XV(ap=x_in), XV(shared_ap=xs0), wgt, wut, wdt, 2048, 512, 0.5,
                   lnb, b=ffn_b)
    for pn in ("stx0", "stx1"):
        s = ctx.sem(pn)
        G.wait_ge(s.h, s.n)
    emit_flag_write(ctx, flags, kconst_sb, 1)
    emit_join(ctx)
    arena.reset()

    def dump_and_stop(src_xv):
        dsem = ctx.sem("dumpst")
        dsl = Slot(ctx, "dumpld")
        for tt in range(16):
            dsl.load(Y, [(lnb["xa"][0][:], src_xv.sl(tt * P, (tt + 1) * P))])
            dsl.wait(V)
            if dsem.n > 0:
                V.wait_ge(dsem.h, dsem.n)
            nV = inc(V.tensor_copy(lnb["sq"][:], lnb["xa"][0][:]), ctx.sem("ve"))
            dsl.set_free(ctx.sem("ve"), nV)
            G.wait_ge(ctx.sem("ve").h, nV)
            ins = G.dma_start(out=xo_ext[tt * P:(tt + 1) * P, :], in_=lnb["sq"][:])
            inc(ins, dsem, 16)
            G.wait_ge(dsem.h, dsem.n)
        finalize(ctx)
        ctx._es.close()
        return ctx

    if STOP == 1:
        return dump_and_stop(XV(shared_ap=xs0))

    rw_b = alloc_rwkv_bufs(ctx, pool=pool)
    v1 = emit_flag_poll(ctx, flags, 1, pollbuf, tag="a")
    Y.wait_ge(ctx.sems["barok"].h, v1)
    emit_rwkv_consts(ctx, rw_b, td, tf, mkc, mvc, mrc, warmb)

    def rflag_cb():
        for j in range(len(lnb["xa_slot"])):
            sl = lnb["xa_slot"][j]
            G.wait_ge(sl.ld.h, sl.ld.n)
        emit_flag_write(ctx, flags, kconst_sb, 2)
    emit_rwkv_body(ctx, XV(shared_ap=xs0, ext=True), XV(shared_ap=xs1), wkT, wvT, wrT, woT,
                   lnb, rw_b, after_chunk0_ln=rflag_cb)
    s = ctx.sem("rwst0")
    G.wait_ge(s.h, s.n)
    emit_join(ctx)
    arena.reset()

    if STOP == 2:
        return dump_and_stop(XV(shared_ap=xs1))

    v2 = emit_flag_poll(ctx, flags, 2, pollbuf, tag="b")
    G.wait_ge(ctx.sems["barok"].h, v2)
    ffn_b2 = alloc_ffn0_bufs(ctx, pool=pool)
    emit_ffn0_body(ctx, XV(shared_ap=xs1), XV(shared_ap=xs0), wgt, wut, wdt, 2048, 512, 0.5,
                   lnb, b=ffn_b2)
    for pn in ("stx0", "stx1"):
        s = ctx.sem(pn)
        G.wait_ge(s.h, s.n)
    emit_join(ctx)
    arena.reset()

    if STOP == 3:
        return dump_and_stop(XV(shared_ap=xs0))

    moe_b = alloc_moe_bufs(ctx, pool=pool)
    moe_b["ident_f"] = ident_f
    moe_b["ones_f"] = ones_sb
    moe_b["sel8"] = sel8_sb
    emit_moe_body(ctx, XV(shared_ap=xs0), XV(shared_ap=xs1), mgb, mub, mdb, swgb, swub, swdb,
                  (gwb, gwlo), lnb, moe_b)
    for pp in (0, 1):
        s = ctx.sem("moestp" + str(pp))
        G.wait_ge(s.h, s.n)
    emit_flag_write(ctx, flags, kconst_sb, 4)
    emit_join(ctx)
    arena.reset()

    if STOP == 4:
        return dump_and_stop(XV(shared_ap=xs1))

    v4 = emit_flag_poll(ctx, flags, 4, pollbuf, tag="c")
    Y.wait_ge(ctx.sems["barok"].h, v4)
    mla_b = alloc_mla_bufs(ctx, attn_out, pool=pool)
    kvnw_sb = nc.alloc_sbuf_tensor_at("ml_kvnw", [1, 512], F32,
                                      offset=nc.lookup_mloc(mla_b["aoT"]).addr)
    mla_sl = Slot(ctx, "mlconst")
    mla_sl.load(Y, [(kvnw_sb[:], kvnw), (mla_b["kcos"][:], kcos), (mla_b["ksin"][:], ksin),
                    (mla_b["qcs"][0][0][:], qcos[:, 0:1024]), (mla_b["qcs"][0][1][:], qsin[:, 0:1024]),
                    (mla_b["qcs"][1][0][:], qcos[:, 1024:2048]), (mla_b["qcs"][1][1][:], qsin[:, 1024:2048]),
                    (mla_b["kvaT_sb"][:], kvaT.rearrange("k p f -> p k f"))])
    mla_sl.wait(T)
    nT = inc(T.matmul(mla_b["pkv"][:], ones_sb[0:1, :], kvnw_sb[:], start=True, stop=True), mm)
    V.wait_ge(mm.h, nT)
    nV = inc(V.tensor_copy(mla_b["wkv_bc"][:], mla_b["pkv"][:]), ve)
    V.wait_ge(ve.h, nV)
    emit_mla_body(ctx, XV(ap=xs1, row0=512), XV(shared_ap=xs1), XV(shared_ap=xs0),
                  kvaT, kvbT, qwT, owT, lnb, mla_b)
    s = ctx.sem("mlast")
    G.wait_ge(s.h, s.n)
    emit_join(ctx)
    arena.reset()

    if STOP == 5:
        return dump_and_stop(XV(shared_ap=xs0))

    moe_b2 = alloc_moe_bufs(ctx, pool=pool)
    moe_b2["ident_f"] = ident_f
    moe_b2["ones_f"] = ones_sb
    moe_b2["sel8"] = sel8_sb
    emit_moe_body(ctx, XV(shared_ap=xs0), XV(ap=xo_ext), mgb, mub, mdb, swgb, swub, swdb,
                  (gwb, gwlo), lnb, moe_b2)
    for pp in (0, 1):
        s = ctx.sem("moestp" + str(pp))
        G.wait_ge(s.h, s.n)
    finalize(ctx)
    ctx._es.close()
    return ctx



import math as _math

def _yarn_tables():
    RD = 64
    MAX_POS, ORIG_POS, BASE = 131072, 8192, 500000.0
    scaling = MAX_POS / ORIG_POS
    pos_freqs = BASE ** (np.arange(0, RD, 2, dtype=np.float64) / RD)
    extrap = 1.0 / pos_freqs
    interp = 1.0 / (scaling * pos_freqs)
    low = max(_math.floor(RD * _math.log(ORIG_POS / (32.0 * 2 * _math.pi)) / (2 * _math.log(BASE))), 0)
    high = min(_math.ceil(RD * _math.log(ORIG_POS / (1.0 * 2 * _math.pi)) / (2 * _math.log(BASE))), RD - 1)
    i = np.arange(RD // 2, dtype=np.float64)
    smooth = np.clip((i - low) / max(high - low, 1), 0.0, 1.0)
    inv_freq = (1.0 - smooth) * interp + smooth * extrap
    pos = np.arange(4096, dtype=np.float64)
    freqs = pos[:, None] * inv_freq[None, :]
    emb = np.concatenate([freqs, freqs], axis=-1)
    mscale = 0.1 * _math.log(scaling) + 1.0
    return np.cos(emb) * mscale, np.sin(emb) * mscale


def kernel(**inputs):
    from concourse.bass_utils import run_bass_kernel_spmd
    bfc = lambda a: np.ascontiguousarray(a).astype(ml_dtypes.bfloat16)
    f32 = lambda a: np.ascontiguousarray(np.asarray(a), dtype=np.float32)
    identb = np.eye(P, dtype=ml_dtypes.bfloat16)
    identf = np.eye(P, dtype=np.float32)
    onesf = np.ones((1, P), np.float32)
    sel8 = np.concatenate([(np.arange(8)[:, None] == e).astype(np.float32) * np.ones((8, P), np.float32)
                           for e in range(8)], axis=1)
    pt = lambda a: np.ascontiguousarray(np.asarray(a, dtype=np.float32).reshape(16, P).T)
    cos_t, sin_t = _yarn_tables()

    x = f32(inputs["L0_attn_x"]).reshape(4, 4096, 2048)
    nc = bass.Bass("TRN2", target_bir_lowering=False, debug=False)
    build_fused(nc)
    common = {
        "kconst": flag_const_array(),
        "gidx": np.ascontiguousarray(inputs["L0_ffn_gate_idx"], dtype=np.int32),
        "uidx": np.ascontiguousarray(inputs["L0_ffn_up_idx"], dtype=np.int32),
        "didx": np.ascontiguousarray(inputs["L0_ffn_down_idx"], dtype=np.int32),
        "lut": f32(inputs["L0_ffn_lut"]),
        "wk": f32(inputs["L0_attn_Wk"]), "wv": f32(inputs["L0_attn_Wv"]),
        "wr": f32(inputs["L0_attn_Wr"]), "wo": f32(inputs["L0_attn_Wo"]),
        "qw": f32(inputs["L1_attn_q_w"]), "kva": f32(inputs["L1_attn_kv_a_w"]),
        "kvb": f32(inputs["L1_attn_kv_b_w"]), "ow": f32(inputs["L1_attn_o_w"]),
        "mg": f32(inputs["L1_ffn_w_gate"]), "mu": f32(inputs["L1_ffn_w_up"]),
        "md": f32(inputs["L1_ffn_w_down"]),
        "swg": f32(inputs["L1_ffn_sw_gate"]), "swu": f32(inputs["L1_ffn_sw_up"]),
        "swd": f32(inputs["L1_ffn_sw_down"]), "gw": f32(inputs["L1_ffn_gate_w"]),
        "identb": identb, "identf": identf, "onesf": onesf, "sel8": sel8,
        "td": pt(inputs["L0_attn_time_decay"]), "tf": pt(inputs["L0_attn_time_first"]),
        "mkc": pt(np.asarray(inputs["L0_attn_time_mix_k"]).reshape(2048)),
        "mvc": pt(np.asarray(inputs["L0_attn_time_mix_v"]).reshape(2048)),
        "mrc": pt(np.asarray(inputs["L0_attn_time_mix_r"]).reshape(2048)),
        "kvnw": f32(inputs["L1_attn_kv_norm_w"]).reshape(1, 512),
        "kcos": bfc(cos_t.T), "ksin": bfc(sin_t.T),
    }
    ims = []
    for c in range(8):
        bidx, h = c // 2, c % 2
        im = dict(common)
        im["x_in"] = np.ascontiguousarray(x[bidx, h * 2048:(h + 1) * 2048])
        im["slot"] = np.array([[h]], np.int32)
        im["warmb"] = (np.full((P, 1), -1e30, np.float32) if h == 0
                       else np.zeros((P, 1), np.float32))
        im["qcos"] = bfc(cos_t[h * 2048:(h + 1) * 2048].T)
        im["qsin"] = bfc(sin_t[h * 2048:(h + 1) * 2048].T)
        ims.append(im)
    res = run_bass_kernel_spmd(nc, ims, core_ids=list(range(8)))
    out = np.empty((4, 4096, 2048), np.float32)
    for c in range(8):
        out[c // 2, (c % 2) * 2048:(c % 2 + 1) * 2048] = \
            np.asarray(res.results[c]["xo"]).reshape(2048, 2048)
    return out


# version 04
# speedup vs baseline: 1.2003x; 1.1865x over previous
import math
import numpy as np
import ml_dtypes
import concourse.bass as bass
import concourse.mybir as mybir

F32 = mybir.dt.float32
BF16 = mybir.dt.bfloat16
I32 = mybir.dt.int32
AF = mybir.ActivationFunctionType
ALU = mybir.AluOpType
AX = mybir.AxisListType
P = 128


class _Inst:
    __slots__ = ("rec",)
    def __init__(self, rec):
        self.rec = rec
    def then_inc(self, sem, v=1):
        self.rec["incs"].append((sem, v))
        return self

class EngRec:
    def __init__(self, name):
        self.name = name
        self.ops = []
    def op(self, method, *a, **kw):
        rec = {"m": method, "a": a, "kw": kw, "incs": []}
        self.ops.append(rec)
        return _Inst(rec)
    def raw(self, fn):
        self.ops.append({"m": None, "fn": fn})
    def __getattr__(self, name):
        if name.startswith("_"):
            raise AttributeError(name)
        def call(*a, **kw):
            return self.op(name, *a, **kw)
        return call
    def replay(self, eng):
        for rec in self.ops:
            if rec["m"] is None:
                rec["fn"](eng)
                continue
            ins = getattr(eng, rec["m"])(*rec["a"], **rec["kw"])
            for sem, v in rec["incs"]:
                ins = ins.then_inc(sem, v)

class Sem:
    def __init__(self, handle, name=""):
        self.h = handle
        self.n = 0
        self.name = name
    def inc(self, k=1):
        self.n += k
        return self.n

def inc(inst, sem, k=1):
    inst.then_inc(sem.h, k)
    sem.inc(k)
    return sem.n

class Ctx:
    def __init__(self, nc):
        self.nc = nc
        self.T = EngRec("tensor")
        self.V = EngRec("vector")
        self.S = EngRec("scalar")
        self.G = EngRec("gpsimd")
        self.Y = EngRec("sync")
        self.sems = {}
        self._es = None
    def sem(self, name):
        if name not in self.sems:
            self.sems[name] = Sem(self._es.enter_context(self.nc.semaphore(name)), name)
        return self.sems[name]
    def sb(self, name, shape, dt):
        return self._es.enter_context(self.nc.sbuf_tensor(name, shape, dt))
    def asb(self, name, shape, dt):
        ar = getattr(self, "arena", None)
        if ar is None:
            return self.sb(name, shape, dt)
        return ar.alloc(name, shape, dt)
    def ps(self, name, shape, dt):
        return self._es.enter_context(self.nc.psum_tensor(name, shape, dt))

def finalize(ctx):
    nc = ctx.nc
    with nc.Block() as block:
        @block.sync
        def _(eng):
            ctx.Y.replay(eng)
        @block.tensor
        def _(eng):
            ctx.T.replay(eng)
        @block.vector
        def _(eng):
            ctx.V.replay(eng)
        @block.scalar
        def _(eng):
            ctx.S.replay(eng)
        @block.gpsimd
        def _(eng):
            ctx.G.replay(eng)


def host_blk_lhsT(wT):
    Din, Dout = wT.shape
    n_kt, n_m = Din // P, Dout // P
    w4 = wT.reshape(n_kt, P, n_m, P)
    return np.ascontiguousarray(w4.transpose(2, 1, 0, 3).reshape(n_m, P, n_kt * P))

def ceil_div(a, b):
    return (a + b - 1) // b


def _branched_dma(E, sem_h, out=None, in_=None, pair=None, store=False):
    a0, a1 = pair
    def fn(eng):
        with eng.If_eq(eng._slot_reg, 0):
            if store:
                eng.dma_start(out=a0, in_=in_).then_inc(sem_h, 16)
            else:
                eng.dma_start(out=out, in_=a0).then_inc(sem_h, 16)
        with eng.Else():
            if store:
                eng.dma_start(out=a1, in_=in_).then_inc(sem_h, 16)
            else:
                eng.dma_start(out=out, in_=a1).then_inc(sem_h, 16)
    E.raw(fn)


class XV:
    def __init__(self, ap=None, shared_ap=None, pad=512, ext=False, row0=0):
        self.ap = ap
        self.sh = shared_ap
        self.pad = pad
        self.ext = ext
        self.row0 = row0
    def sl(self, r0, r1, c0=0, c1=2048):
        if self.ap is not None:
            return self.ap[self.row0 + r0:self.row0 + r1, c0:c1]
        if self.ext:
            return (self.sh[r0:r1, c0:c1], self.sh[2048 + r0:2048 + r1, c0:c1])
        return (self.sh[self.pad + r0:self.pad + r1, c0:c1],
                self.sh[self.pad + 2048 + r0:self.pad + 2048 + r1, c0:c1])


class Slot:
    def __init__(self, ctx, name):
        self.ld = ctx.sem(name + "_ld")
        self.free_sem = None
        self.free_val = 0
        self.nload = 0
    def load(self, E, pairs):
        if self.free_sem is not None:
            E.wait_ge(self.free_sem.h, self.free_val)
        self.nload += 1
        for out, in_ in pairs:
            if isinstance(in_, tuple):
                _branched_dma(E, self.ld.h, out=out, pair=in_)
                self.ld.inc(16)
            else:
                inc(E.dma_start(out=out, in_=in_), self.ld, 16)
    def wait(self, E):
        E.wait_ge(self.ld.h, self.ld.n)
    def set_free(self, sem, val):
        self.free_sem, self.free_val = sem, val


def store_x(E, sem, out_sel, in_sb):
    if isinstance(out_sel, tuple):
        _branched_dma(E, sem.h, in_=in_sb, pair=out_sel, store=True)
        sem.inc(16)
    else:
        inc(E.dma_start(out=out_sel, in_=in_sb), sem, 16)


def emit_join(ctx, tag=""):
    engs = [ctx.T, ctx.V, ctx.S, ctx.G, ctx.Y]
    names = ["jT", "jV", "jS", "jG", "jY"]
    vals = {}
    for E, nm in zip(engs, names):
        s = ctx.sem(nm)
        vals[nm] = inc(E.nop(), s)
    for E in engs:
        for nm in names:
            E.wait_ge(ctx.sems[nm].h, vals[nm])


def emit_slot_regs(ctx, slot_in_ap):
    sl_sb = ctx.sb("slot_sb", [1, 1], I32)
    s = ctx.sem("slotld")
    inc(ctx.Y.dma_start(out=sl_sb[:], in_=slot_in_ap), s, 16)
    for E in (ctx.Y, ctx.G):
        E.wait_ge(s.h, s.n)
        def fn(eng):
            eng._slot_reg = eng.alloc_register()
            eng.reg_load(eng._slot_reg, sl_sb[0:1, 0:1])
        E.raw(fn)


def flag_magics(k):
    m = (k * 2654435761) % (2 ** 31 - 7) + 3
    c = (m ^ 0x5A5A5A5A) & 0x7FFFFFFF
    return m, c


def flag_const_array():
    a = np.zeros((1, 16), np.int32)
    for k in range(1, 9):
        m, c = flag_magics(k)
        a[0, k - 1] = m
        a[0, 8 + k - 1] = c
    return a


def emit_flag_write(ctx, flags_ap, kconst_sb, k):
    G = ctx.G
    s = ctx.sem("barw")
    _branched_dma(G, s.h, in_=kconst_sb[0:1, k - 1:k],
                  pair=(flags_ap[0:1, k - 1:k], flags_ap[1:2, k - 1:k]), store=True)
    s.inc(16)
    _branched_dma(G, s.h, in_=kconst_sb[0:1, 7 + k:8 + k],
                  pair=(flags_ap[0:1, 7 + k:8 + k], flags_ap[1:2, 7 + k:8 + k]), store=True)
    s.inc(16)
    G.wait_ge(s.h, s.n)


def emit_flag_poll(ctx, flags_ap, k, pollbuf, tag=""):
    G = ctx.G
    bar_ok = ctx.sem("barok")
    pl = ctx.sem("barpl" + tag)
    m, c = flag_magics(k)

    def fn(eng):
        rf1 = eng.alloc_register()
        rf2 = eng.alloc_register()
        rcnt = eng.alloc_register()
        rnd = eng.alloc_register()
        eng.reg_mov(rcnt, 0)
        eng.reg_mov(rnd, 1)
        with eng.While(rnd):
            eng.reg_add(rcnt, rcnt, 32)
            with eng.If_eq(eng._slot_reg, 0):
                eng.dma_start(out=pollbuf[0:1, 0:1], in_=flags_ap[1:2, k - 1:k]).then_inc(pl.h, 16)
                eng.dma_start(out=pollbuf[0:1, 1:2], in_=flags_ap[1:2, 7 + k:8 + k]).then_inc(pl.h, 16)
            with eng.Else():
                eng.dma_start(out=pollbuf[0:1, 0:1], in_=flags_ap[0:1, k - 1:k]).then_inc(pl.h, 16)
                eng.dma_start(out=pollbuf[0:1, 1:2], in_=flags_ap[0:1, 7 + k:8 + k]).then_inc(pl.h, 16)
            eng.wait_ge(pl.h, rcnt)
            eng.reg_load(rf1, pollbuf[0:1, 0:1])
            eng.reg_load(rf2, pollbuf[0:1, 1:2])
            with eng.If_eq(rf1, m):
                with eng.If_eq(rf2, c):
                    eng.reg_mov(rnd, 0)
    G.raw(fn)
    return inc(G.nop(), bar_ok)


DT_SIZE_MAP = {F32: 4, BF16: 2, I32: 4}


class Arena:
    def __init__(self, ctx, nbytes):
        self.ctx = ctx
        self.t = ctx.sb("arena", [P, nbytes], mybir.dt.uint8)
        self.base = ctx.nc.lookup_mloc(self.t).addr
        self.nbytes = nbytes
        self.off = 0
        self.peak = 0
    def reset(self):
        self.off = 0
    def alloc(self, name, shape, dt):
        self.off = (self.off + 31) // 32 * 32
        h = self.ctx.nc.alloc_sbuf_tensor_at(name, shape, dt, offset=self.base + self.off)
        self.off += shape[1] * DT_SIZE_MAP[dt]
        self.peak = max(self.peak, self.off)
        assert self.off <= self.nbytes, "arena overflow at " + name
        return h



def emit_ln_transpose(ctx, bufs, xv, n_tiles, dst_sb, dst_tc, dst_col0=0,
                      eps=1e-5, row_base=0):
    T, V, S, Y = ctx.T, ctx.V, ctx.S, ctx.Y
    ve = ctx.sem("ve"); se = ctx.sem("se"); mm = ctx.sem("mm"); frp = ctx.sem("frp")
    D = 2048
    for i in range(n_tiles):
        par = i % len(bufs["xa"])
        xa = bufs["xa"][par]; xn = bufs["xn"][par]
        sl = bufs["xa_slot"][par]
        r0 = row_base + i * P
        sl.load(Y, [(xa[:], xv.sl(r0, r0 + P))])
        sl.wait(V)
        nV = inc(V.reduce_sum(bufs["s1"][:], xa[:], axis=AX.X), ve)
        sl.wait(S)
        nS = inc(S.activation(bufs["sq"][:], xa[:], AF.Square, accum_out=bufs["ss"][:]), se)
        V.wait_ge(se.h, nS)
        V.wait_ge(ve.h, nV)
        nV = inc(V.tensor_scalar(bufs["va"][:], bufs["ss"][:], 1.0 / D, None, ALU.mult), ve)
        nV = inc(V.tensor_scalar(bufs["vb"][:], bufs["s1"][:], 1.0 / D, None, ALU.mult), ve)
        V.wait_ge(ve.h, nV)
        nV = inc(V.tensor_tensor(bufs["vc"][:], bufs["vb"][:], bufs["vb"][:], ALU.mult), ve)
        V.wait_ge(ve.h, nV)
        nV = inc(V.tensor_tensor(bufs["vc"][:], bufs["va"][:], bufs["vc"][:], ALU.subtract), ve)
        V.wait_ge(ve.h, nV)
        nV = inc(V.tensor_scalar(bufs["vc"][:], bufs["vc"][:], eps, None, ALU.add), ve)
        S.wait_ge(ve.h, nV)
        nS = inc(S.activation(bufs["lnv"][:], bufs["vc"][:], AF.Ln), se)
        S.wait_ge(se.h, nS)
        nS = inc(S.activation(bufs["rstd"][:], bufs["lnv"][:], AF.Exp, scale=-0.5), se)
        V.wait_ge(se.h, nS)
        nV = inc(V.tensor_tensor(bufs["nmr"][:], bufs["vb"][:], bufs["rstd"][:], ALU.mult), ve)
        V.wait_ge(ve.h, nV)
        nV = inc(V.tensor_scalar(bufs["nmr"][:], bufs["nmr"][:], -1.0, None, ALU.mult), ve)
        S.wait_ge(ve.h, nV)
        nS = inc(S.activation(xn[:], xa[:], AF.Identity, bias=bufs["nmr"][:], scale=bufs["rstd"][:]), se)
        sl.set_free(se, nS)
        T.wait_ge(se.h, nS)
        if bufs["psbf_busy"]:
            T.wait_ge(frp.h, bufs["psbf_busy"])
        for kt in range(16):
            ins = T.transpose(bufs["psbf"][:, kt * P:(kt + 1) * P],
                              xn[:, kt * P:(kt + 1) * P], bufs["ident_bf"][:])
        nT = inc(ins, mm)
        V.wait_ge(mm.h, nT)
        for kt in range(16):
            dsl = dst_sb[:, kt * dst_tc + dst_col0 + i * P: kt * dst_tc + dst_col0 + (i + 1) * P]
            ins = V.tensor_copy(dsl, bufs["psbf"][:, kt * P:(kt + 1) * P])
        inc(ins, frp)
        bufs["psbf_busy"] = frp.n
    return ctx.sem("frp").n


def alloc_ln_bufs(ctx, tag="", nx=2):
    b = {}
    b["xa"] = [ctx.sb(f"ln_xa{tag}{j}", [P, 2048], F32) for j in range(nx)]
    b["xn"] = [ctx.sb(f"ln_xn{tag}{j}", [P, 2048], BF16) for j in range(nx)]
    b["xa_slot"] = [Slot(ctx, f"lnxa{tag}{j}") for j in range(nx)]
    b["sq"] = ctx.sb(f"ln_sq{tag}", [P, 2048], F32)
    for nm in ["s1", "ss", "va", "vb", "vc", "rstd", "lnv", "nmr"]:
        b[nm] = ctx.sb(f"ln_{nm}{tag}", [P, 1], F32)
    b["psbf_busy"] = 0
    return b


def emit_ffn0_body(ctx, xv_in, xv_out, wgt, wut, wdt, n_rows, Tc, residual_scale, lnb, b=None):
    T, V, S, Y, G = ctx.T, ctx.V, ctx.S, ctx.Y, ctx.G
    mm = ctx.sem("mm"); ve = ctx.sem("ve"); se = ctx.sem("se")
    evf = ctx.sem("evf"); frp = ctx.sem("frp")
    n_chunks = n_rows // Tc
    nt = Tc // P
    NOT = 43
    assert Tc == 512
    if b is None:
        b = ctx.bufs
    xnT, h_sb = b["xnT"], b["h_sb"]

    for c in range(n_chunks):
        ln_done = emit_ln_transpose(ctx, lnb, xv_in, nt, xnT, Tc, row_base=c * Tc)
        evf_base = evf.n
        for m in range(NOT):
            par = m % 2
            wsl = b["wgu_slot"][par]
            wsl.load(Y, [(b["wgbuf"][par][:], wgt[m, :, :]),
                         (b["wubuf"][par][:], wut[m, :, :])])
            wsl.wait(T)
            if m == 0:
                T.wait_ge(frp.h, ln_done)
            if m >= 2:
                T.wait_ge(evf.h, evf_base + (m - 2) + 1)
            pg = b["pgu"][2 * par]; pu = b["pgu"][2 * par + 1]
            for kt in range(16):
                T.matmul(pg[:], b["wgbuf"][par][:, kt * P:(kt + 1) * P],
                         xnT[:, kt * Tc:(kt + 1) * Tc], start=(kt == 0), stop=(kt == 15))
            for kt in range(16):
                ins = T.matmul(pu[:], b["wubuf"][par][:, kt * P:(kt + 1) * P],
                               xnT[:, kt * Tc:(kt + 1) * Tc], start=(kt == 0), stop=(kt == 15))
            nT = inc(ins, mm)
            wsl.set_free(mm, nT)
            S.wait_ge(mm.h, nT)
            nS = inc(S.activation(b["sgbuf"][par][:], pg[:], AF.Silu), se)
            V.wait_ge(se.h, nS)
            ins = V.tensor_tensor(h_sb[:, m * Tc:(m + 1) * Tc], b["sgbuf"][par][:], pu[:], ALU.mult)
            inc(ins, evf)
        h_done = evf.n
        dnev_vals = []
        for dc in range(8):
            par = dc % 2
            wsl = b["wd_slot"][par]
            wsl.load(Y, [(b["wdbuf"][par][:],
                          wdt[:, :, dc * 256:(dc + 1) * 256].rearrange("o p f -> p o f"))])
            wsl.wait(T)
            if dc == 0:
                T.wait_ge(evf.h, h_done)
            if dc >= 1:
                T.wait_ge(ve.h, dnev_vals[dc - 1])
            for ot in range(NOT):
                for tt in range(nt):
                    psl = b["pdn"][tt // 2][:, (tt % 2) * 256:(tt % 2) * 256 + 256]
                    ins = T.matmul(psl, h_sb[:, ot * Tc + tt * P: ot * Tc + (tt + 1) * P],
                                   b["wdbuf"][par][:, ot * 256:(ot + 1) * 256],
                                   start=(ot == 0 and tt % 2 == 0),
                                   stop=(ot == NOT - 1 and tt % 2 == 1))
            nT = inc(ins, mm)
            wsl.set_free(mm, nT)
            xsl = b["xrm_slot"][par]
            xsl.load(Y, [(b["xrm"][par][tt][:],
                          xv_in.sl(c * Tc + tt * P, c * Tc + (tt + 1) * P, dc * 256, (dc + 1) * 256))
                         for tt in range(nt)])
            V.wait_ge(mm.h, nT)
            xsl.wait(V)
            stx_p = ctx.sem("stx" + str(par))
            if stx_p.n > 0:
                V.wait_ge(stx_p.h, stx_p.n)
            for tt in range(nt):
                psl = b["pdn"][tt // 2][:, (tt % 2) * 256:(tt % 2) * 256 + 256]
                ins = V.scalar_tensor_tensor(b["xrw"][par][tt][:], psl, residual_scale,
                                             b["xrm"][par][tt][:], ALU.mult, ALU.add)
            nV = inc(ins, ve)
            xsl.set_free(ve, nV)
            dnev_vals.append(nV)
            G.wait_ge(ve.h, nV)
            for tt in range(nt):
                store_x(G, stx_p, xv_out.sl(c * Tc + tt * P, c * Tc + (tt + 1) * P,
                                            dc * 256, (dc + 1) * 256), b["xrw"][par][tt][:])


def alloc_ffn0_bufs(ctx, Tc=512, pool=None):
    NOT = 43
    b = {}
    b["xnT"] = ctx.asb("xnT", [P, 16 * Tc], BF16)
    b["h_sb"] = ctx.asb("h_sb", [P, NOT * Tc], BF16)
    b["wgbuf"] = [ctx.asb(f"wgbuf{j}", [P, 2048], BF16) for j in range(2)]
    b["wubuf"] = [ctx.asb(f"wubuf{j}", [P, 2048], BF16) for j in range(2)]
    b["wdbuf"] = [ctx.asb(f"wdbuf{j}", [P, NOT * 256], BF16) for j in range(2)]
    b["sgbuf"] = [ctx.asb(f"sgbuf{j}", [P, Tc], F32) for j in range(2)]
    b["xrm"] = [[ctx.asb(f"xrm{j}_{tt}", [P, 256], F32) for tt in range(4)] for j in range(2)]
    b["xrw"] = [[ctx.asb(f"xrw{j}_{tt}", [P, 256], F32) for tt in range(4)] for j in range(2)]
    b["wgu_slot"] = [Slot(ctx, f"wgu{j}") for j in range(2)]
    b["wd_slot"] = [Slot(ctx, f"wd{j}") for j in range(2)]
    b["xrm_slot"] = [Slot(ctx, f"xrm{j}") for j in range(2)]
    b["xrw_slot"] = [Slot(ctx, f"xrw{j}") for j in range(2)]
    if pool is None:
        b["pgu"] = [ctx.ps(f"pgu{j}", [P, 512], F32) for j in range(4)]
        b["pdn"] = [ctx.ps(f"pdn{j}", [P, 512], F32) for j in range(2)]
    else:
        b["pgu"] = pool[0:4]
        b["pdn"] = pool[4:6]
    return b


def build_ffn0(nc, n_rows=2048, Tc=512):
    import contextlib
    ctx = Ctx(nc)
    ctx._es = contextlib.ExitStack()
    x = nc.dram_tensor("x", [n_rows, 2048], F32, kind="ExternalInput").ap()
    wgt = nc.dram_tensor("wgt", [43, P, 2048], BF16, kind="ExternalInput").ap()
    wut = nc.dram_tensor("wut", [43, P, 2048], BF16, kind="ExternalInput").ap()
    wdt = nc.dram_tensor("wdt", [43, P, 2048], BF16, kind="ExternalInput").ap()
    identb = nc.dram_tensor("identb", [P, P], BF16, kind="ExternalInput").ap()
    xo = nc.dram_tensor("xo", [n_rows, 2048], F32, kind="ExternalOutput").ap()

    lnb = alloc_ln_bufs(ctx)
    lnb["ident_bf"] = ctx.sb("identb_sb", [P, P], BF16)
    lnb["psbf"] = ctx.ps("psbf", [P, 2048], BF16)
    ctx.bufs = alloc_ffn0_bufs(ctx, Tc)

    idl = Slot(ctx, "idl")
    idl.load(ctx.Y, [(lnb["ident_bf"][:], identb)])
    idl.wait(ctx.T)

    emit_ffn0_body(ctx, XV(ap=x), XV(ap=xo), wgt, wut, wdt, n_rows, Tc, 0.5, lnb)

    for pn in ("stx0", "stx1"):
        s = ctx.sem(pn)
        ctx.G.wait_ge(s.h, s.n)
    finalize(ctx)
    ctx._es.close()
    return nc


def emit_lut_bcast(ctx, lut_in_ap, lut_sb, ones_sb, pbank):
    T, V, Y = ctx.T, ctx.V, ctx.Y
    mm = ctx.sem("mm"); ve = ctx.sem("ve")
    lrow = ctx.asb("lut_row", [1, 16], F32)
    sl = Slot(ctx, "lutld")
    sl.load(Y, [(lrow[:], lut_in_ap.rearrange("(a b) -> a b", a=1))])
    sl.wait(T)
    nT = inc(T.matmul(pbank[:, 0:16], ones_sb[0:1, :], lrow[:], start=True, stop=True), mm)
    V.wait_ge(mm.h, nT)
    nV = inc(V.tensor_copy(lut_sb[:], pbank[:, 0:16]), ve)
    return nV

def emit_gather_tile(ctx, idxb, acc, tmp, lut_sb, width):
    V = ctx.V
    ve = ctx.sem("ve")
    n = inc(V.tensor_scalar(acc[:, 0:width], idxb[:, 0:width], 0.0, lut_sb[:, 0:1],
                            ALU.is_equal, ALU.mult), ve)
    for j in range(1, 16):
        n = inc(V.tensor_scalar(tmp[:, 0:width], idxb[:, 0:width], float(j), lut_sb[:, j:j + 1],
                                ALU.is_equal, ALU.mult), ve)
        V.wait_ge(ve.h, n)
        n = inc(V.tensor_tensor(acc[:, 0:width], acc[:, 0:width], tmp[:, 0:width], ALU.add), ve)
        V.wait_ge(ve.h, n)

def emit_stage_ghost_lhsT(ctx, idx_dram, out_blk_dram, lut_sb, b, ot_range, ot_range2=None):
    T, V, Y, G = ctx.T, ctx.V, ctx.Y, ctx.G
    ve = ctx.sem("ve"); mm = ctx.sem("mm"); frp = ctx.sem("frp")
    for i, ot in enumerate(ot_range):
        ot2 = ot if ot_range2 is None else ot_range2[i]
        par = i % 2
        stg = b["stg"][par]
        isl = b["idx_slot"][par]
        src_in = (idx_dram[ot * P:(ot + 1) * P, :] if ot_range2 is None else
                  (idx_dram[ot * P:(ot + 1) * P, :], idx_dram[ot2 * P:(ot2 + 1) * P, :]))
        isl.load(Y, [(b["idx_sb"][par][:, 0:2048], src_in)])
        isl.wait(V)
        nV = inc(V.tensor_copy(b["idxb"][:, 0:2048], b["idx_sb"][par][:, 0:2048]), ve)
        isl.set_free(ve, nV)
        V.wait_ge(ve.h, nV)
        emit_gather_tile(ctx, b["idxb"], b["acc"], b["tmp"], lut_sb, 2048)
        nV = inc(V.engine_nop(), ve)
        T.wait_ge(ve.h, nV)
        if b["psbf_busy"]:
            T.wait_ge(frp.h, b["psbf_busy"])
        for kt in range(16):
            ins = T.transpose(b["psbf"][:, kt * P:(kt + 1) * P],
                              b["acc"][:, kt * P:(kt + 1) * P], b["ident_bf"][:])
        nT = inc(ins, mm)
        V.wait_ge(mm.h, nT)
        wsl = b["wt_slot"][par]
        if wsl.free_sem is not None:
            V.wait_ge(wsl.free_sem.h, wsl.free_val)
        for kt in range(16):
            ins = V.tensor_copy(b["wt_sb"][par][:, kt * P:(kt + 1) * P],
                                b["psbf"][:, kt * P:(kt + 1) * P])
        nV = inc(ins, frp)
        b["psbf_busy"] = frp.n
        G.wait_ge(frp.h, nV)
        if ot_range2 is None:
            inc(G.dma_start(out=out_blk_dram[ot, :, :], in_=b["wt_sb"][par][:]), stg, 16)
        else:
            store_x(G, stg, (out_blk_dram[ot, :, :], out_blk_dram[ot2, :, :]), b["wt_sb"][par][:])
        n_st = stg.n
        wsl.free_sem, wsl.free_val = stg, n_st

def emit_stage_ghost_rhs(ctx, idx_dram, out_dram, lut_sb, b, dt_range, dt_range2=None):
    T, V, Y, G = ctx.T, ctx.V, ctx.Y, ctx.G
    ve = ctx.sem("ve"); mm = ctx.sem("mm"); frp = ctx.sem("frp")
    W = 5504
    for i, dt in enumerate(dt_range):
        dt2 = dt if dt_range2 is None else dt_range2[i]
        par = i % 2
        stg = b["stg"][par]
        isl = b["idx_slot"][par]
        src_in = (idx_dram[dt * P:(dt + 1) * P, :] if dt_range2 is None else
                  (idx_dram[dt * P:(dt + 1) * P, :], idx_dram[dt2 * P:(dt2 + 1) * P, :]))
        isl.load(Y, [(b["idx_sb"][par][:, 0:W], src_in)])
        isl.wait(V)
        nV = inc(V.tensor_copy(b["idxb"][:, 0:W], b["idx_sb"][par][:, 0:W]), ve)
        isl.set_free(ve, nV)
        V.wait_ge(ve.h, nV)
        emit_gather_tile(ctx, b["idxb"], b["acc"], b["tmp"], lut_sb, W)
        nV = inc(V.engine_nop(), ve)
        T.wait_ge(ve.h, nV)
        done = 0
        for batch0 in range(0, 43, 16):
            nb = min(16, 43 - batch0)
            if b["psbf_busy"]:
                T.wait_ge(frp.h, b["psbf_busy"])
            for j in range(nb):
                ins = T.transpose(b["psbf"][:, j * P:(j + 1) * P],
                                  b["acc"][:, (batch0 + j) * P:(batch0 + j + 1) * P],
                                  b["ident_bf"][:])
            nT = inc(ins, mm)
            V.wait_ge(mm.h, nT)
            wsl = b["wt_slot"][par]
            if wsl.free_sem is not None:
                V.wait_ge(wsl.free_sem.h, wsl.free_val)
            for j in range(nb):
                ins = V.tensor_copy(b["wt_sb"][par][:, j * P:(j + 1) * P],
                                    b["psbf"][:, j * P:(j + 1) * P])
            nV = inc(ins, frp)
            b["psbf_busy"] = frp.n
            G.wait_ge(frp.h, nV)
            for j in range(nb):
                if dt_range2 is None:
                    inc(G.dma_start(out=out_dram[batch0 + j, :, dt * P:(dt + 1) * P],
                                    in_=b["wt_sb"][par][:, j * P:(j + 1) * P]), stg, 16)
                else:
                    store_x(G, stg, (out_dram[batch0 + j, :, dt * P:(dt + 1) * P],
                                     out_dram[batch0 + j, :, dt2 * P:(dt2 + 1) * P]),
                            b["wt_sb"][par][:, j * P:(j + 1) * P])
            n_st = stg.n
            wsl.free_sem, wsl.free_val = stg, n_st

def emit_stage_transpose(ctx, src_dram, O, I, out_dram, layout, b, slab_range=None,
                         src_row0=0, slab_range2=None):
    T, V, Y, G = ctx.T, ctx.V, ctx.Y, ctx.G
    ve = ctx.sem("ve"); mm = ctx.sem("mm"); frp = ctx.sem("frp")
    n_slab = O // P
    n_i = I // P
    rng = slab_range if slab_range is not None else range(n_slab)
    for ii, s in enumerate(rng):
        s2 = s if slab_range2 is None else slab_range2[ii]
        par = ii % 2
        stg = b["stg"][par]
        fsl = b["f32_slot"][par]
        src_in = (src_dram[src_row0 + s * P: src_row0 + (s + 1) * P, :] if slab_range2 is None else
                  (src_dram[src_row0 + s * P: src_row0 + (s + 1) * P, :],
                   src_dram[src_row0 + s2 * P: src_row0 + (s2 + 1) * P, :]))
        fsl.load(Y, [(b["f32_sb"][par][:, 0:I], src_in)])
        fsl.wait(V)
        nV = inc(V.tensor_copy(b["acc"][:, 0:I], b["f32_sb"][par][:, 0:I]), ve)
        fsl.set_free(ve, nV)
        T.wait_ge(ve.h, nV)
        for batch0 in range(0, n_i, 16):
            nb = min(16, n_i - batch0)
            if b["psbf_busy"]:
                T.wait_ge(frp.h, b["psbf_busy"])
            for j in range(nb):
                ins = T.transpose(b["psbf"][:, j * P:(j + 1) * P],
                                  b["acc"][:, (batch0 + j) * P:(batch0 + j + 1) * P],
                                  b["ident_bf"][:])
            nT = inc(ins, mm)
            V.wait_ge(mm.h, nT)
            wsl = b["wt_slot"][par]
            if wsl.free_sem is not None:
                V.wait_ge(wsl.free_sem.h, wsl.free_val)
            for j in range(nb):
                ins = V.tensor_copy(b["wt_sb"][par][:, j * P:(j + 1) * P],
                                    b["psbf"][:, j * P:(j + 1) * P])
            nV = inc(ins, frp)
            b["psbf_busy"] = frp.n
            G.wait_ge(frp.h, nV)
            for j in range(nb):
                it = batch0 + j
                if layout == "blk":
                    dst = out_dram[s, :, it * P:(it + 1) * P]
                    dst2 = out_dram[s2, :, it * P:(it + 1) * P]
                else:
                    dst = out_dram[it, :, s * P:(s + 1) * P]
                    dst2 = out_dram[it, :, s2 * P:(s2 + 1) * P]
                if slab_range2 is None:
                    inc(G.dma_start(out=dst, in_=b["wt_sb"][par][:, j * P:(j + 1) * P]), stg, 16)
                else:
                    store_x(G, stg, (dst, dst2), b["wt_sb"][par][:, j * P:(j + 1) * P])
            n_st = stg.n
            wsl.free_sem, wsl.free_val = stg, n_st

def emit_stage_cast(ctx, src_dram, R, C, out_dram, b, slab_range=None, slab_range2=None):
    V, Y, G = ctx.V, ctx.Y, ctx.G
    ve = ctx.sem("ve")
    rng = slab_range if slab_range is not None else range(R // P)
    for ii, s in enumerate(rng):
        s2 = s if slab_range2 is None else slab_range2[ii]
        par = ii % 2
        stg = b["stgc"][par]
        fsl = b["f32_slot"][par]
        src_in = (src_dram[s * P:(s + 1) * P, :] if slab_range2 is None else
                  (src_dram[s * P:(s + 1) * P, :], src_dram[s2 * P:(s2 + 1) * P, :]))
        fsl.load(Y, [(b["f32_sb"][par][:, 0:C], src_in)])
        fsl.wait(V)
        csl = b["cast_slot"][par]
        if csl.free_sem is not None:
            V.wait_ge(csl.free_sem.h, csl.free_val)
        nV = inc(V.tensor_copy(b["cast_sb"][par][:, 0:C], b["f32_sb"][par][:, 0:C]), ve)
        fsl.set_free(ve, nV)
        G.wait_ge(ve.h, nV)
        if slab_range2 is None:
            inc(G.dma_start(out=out_dram[s * P:(s + 1) * P, :],
                            in_=b["cast_sb"][par][:, 0:C]), stg, 16)
        else:
            store_x(G, stg, (out_dram[s * P:(s + 1) * P, :], out_dram[s2 * P:(s2 + 1) * P, :]),
                    b["cast_sb"][par][:, 0:C])
        n_st = stg.n
        csl.free_sem, csl.free_val = stg, n_st

def alloc_stage_bufs(ctx, pool=None):
    b = {}
    b["idx_sb"] = [ctx.asb(f"st_idx{j}", [P, 5504], I32) for j in range(2)]
    b["idx_slot"] = [Slot(ctx, f"stidx{j}") for j in range(2)]
    b["idxb"] = ctx.asb("st_idxb", [P, 5504], BF16)
    b["acc"] = ctx.asb("st_acc", [P, 5504], BF16)
    b["tmp"] = ctx.asb("st_tmp", [P, 5504], BF16)
    b["wt_sb"] = [ctx.asb(f"st_wt{j}", [P, 2048], BF16) for j in range(2)]
    b["wt_slot"] = [Slot(ctx, f"stwt{j}") for j in range(2)]
    b["f32_sb"] = [ctx.asb(f"st_f32{j}", [P, 2816], F32) for j in range(2)]
    b["f32_slot"] = [Slot(ctx, f"stf32{j}") for j in range(2)]
    b["cast_sb"] = [ctx.asb(f"st_cast{j}", [P, 2816], BF16) for j in range(2)]
    b["cast_slot"] = [Slot(ctx, f"stcast{j}") for j in range(2)]
    b["stg"] = [ctx.sem("stgA"), ctx.sem("stgB")]
    b["stgc"] = [ctx.sem("stgcA"), ctx.sem("stgcB")]
    b["psbf_busy"] = 0
    return b

def build_stage(nc):
    import contextlib
    ctx = Ctx(nc)
    ctx._es = contextlib.ExitStack()
    dr = lambda nm, sh, dt, kind: nc.dram_tensor(nm, sh, dt, kind=kind).ap()
    I, O = "ExternalInput", "ExternalOutput"
    gidx = dr("gidx", [5504, 2048], I32, I)
    uidx = dr("uidx", [5504, 2048], I32, I)
    didx = dr("didx", [2048, 5504], I32, I)
    lut = dr("lut", [16], F32, I)
    wk = dr("wk", [2048, 2048], F32, I)
    wv = dr("wv", [2048, 2048], F32, I)
    wr = dr("wr", [2048, 2048], F32, I)
    wo = dr("wo", [2048, 2048], F32, I)
    qw = dr("qw", [2048, 2048], F32, I)
    kva = dr("kva", [576, 2048], F32, I)
    kvb = dr("kvb", [4096, 512], F32, I)
    ow = dr("ow", [2048, 2048], F32, I)
    mg = dr("mg", [8, 2048, 1408], F32, I)
    mu = dr("mu", [8, 2048, 1408], F32, I)
    md = dr("md", [8, 1408, 2048], F32, I)
    swg = dr("swg", [2048, 2816], F32, I)
    swu = dr("swu", [2048, 2816], F32, I)
    swd = dr("swd", [2816, 2048], F32, I)
    identb = dr("identb", [P, P], BF16, I)
    onesf = dr("onesf", [1, P], F32, I)

    wgt = dr("wgt", [43, P, 2048], BF16, O)
    wut = dr("wut", [43, P, 2048], BF16, O)
    wdt = dr("wdt", [43, P, 2048], BF16, O)
    wkT = dr("wkT", [16, P, 2048], BF16, O)
    wvT = dr("wvT", [16, P, 2048], BF16, O)
    wrT = dr("wrT", [16, P, 2048], BF16, O)
    woT = dr("woT", [16, P, 2048], BF16, O)
    qwT = dr("qwT", [16, P, 2048], BF16, O)
    kvaT = dr("kvaT", [16, P, 512], BF16, O)
    kvbT = dr("kvbT", [32, P, 512], BF16, O)
    owT = dr("owT", [16, P, 2048], BF16, O)
    mgb = dr("mgb", [8, 2048, 1408], BF16, O)
    mub = dr("mub", [8, 2048, 1408], BF16, O)
    mdb = dr("mdb", [8, 1408, 2048], BF16, O)
    swgb = dr("swgb", [2048, 2816], BF16, O)
    swub = dr("swub", [2048, 2816], BF16, O)
    swdb = dr("swdb", [2816, 2048], BF16, O)

    b = alloc_stage_bufs(ctx)
    b["ident_bf"] = ctx.sb("identb_sb", [P, P], BF16)
    b["psbf"] = ctx.ps("psbf", [P, 2048], BF16)
    pbank = ctx.ps("pb0", [P, 512], F32)
    ones_sb = ctx.sb("ones_sb", [1, P], F32)
    lut_sb = ctx.sb("lut_sb", [P, 16], F32)

    idl = Slot(ctx, "idl")
    idl.load(ctx.Y, [(b["ident_bf"][:], identb), (ones_sb[:], onesf)])
    idl.wait(ctx.T)
    nV = emit_lut_bcast(ctx, lut, lut_sb, ones_sb, pbank)
    ctx.V.wait_ge(ctx.sem("ve").h, nV)

    emit_stage_ghost_lhsT(ctx, gidx, wgt, lut_sb, b, range(43))
    emit_stage_ghost_lhsT(ctx, uidx, wut, lut_sb, b, range(43))
    emit_stage_ghost_rhs(ctx, didx, wdt, lut_sb, b, range(16))
    emit_stage_transpose(ctx, wk, 2048, 2048, wkT, "blk", b)
    emit_stage_transpose(ctx, wv, 2048, 2048, wvT, "blk", b)
    emit_stage_transpose(ctx, wr, 2048, 2048, wrT, "blk", b)
    emit_stage_transpose(ctx, wo, 2048, 2048, woT, "rowmajor", b)
    emit_stage_transpose(ctx, qw, 2048, 2048, qwT, "blk", b)
    emit_stage_transpose(ctx, kva, 512, 2048, kvaT, "rowmajor", b, src_row0=64)
    emit_stage_transpose(ctx, kvb, 4096, 512, kvbT, "blk", b)
    emit_stage_transpose(ctx, ow, 2048, 2048, owT, "rowmajor", b)
    for e in range(8):
        emit_stage_cast(ctx, mg[e], 2048, 1408, mgb[e], b)
        emit_stage_cast(ctx, mu[e], 2048, 1408, mub[e], b)
        emit_stage_cast(ctx, md[e], 1408, 2048, mdb[e], b)
    emit_stage_cast(ctx, swg, 2048, 2816, swgb, b)
    emit_stage_cast(ctx, swu, 2048, 2816, swub, b)
    emit_stage_cast(ctx, swd, 2816, 2048, swdb, b)

    for s in b["stg"] + b["stgc"]:
        ctx.G.wait_ge(s.h, s.n)
    finalize(ctx)
    ctx._es.close()
    return nc


def emit_rwkv_body(ctx, xv_ext, xv_out, wkT, wvT, wrT, woT, lnb, b, after_chunk0_ln=None):
    T, V, S, Y, G = ctx.T, ctx.V, ctx.S, ctx.Y, ctx.G
    ve = ctx.sem("ve"); se = ctx.sem("se"); mm = ctx.sem("mm"); frp = ctx.sem("frp")
    Tc = 512
    nt = 4

    def vch(op):
        n = inc(op, ve)
        V.wait_ge(ve.h, n)
        return n

    for c in range(5):
        warm = (c == 0)
        ln_done = emit_ln_transpose(ctx, lnb, xv_ext, nt, b["xnT"], Tc, row_base=c * Tc)
        if c == 0 and after_chunk0_ln is not None:
            after_chunk0_ln()
        V.wait_ge(frp.h, ln_done)
        for kt in range(16):
            xns = b["xnT"][:, kt * Tc:(kt + 1) * Tc]
            xsh = b["xsh"]
            vch(V.tensor_copy(xsh[:, 1:Tc], xns[:, 0:Tc - 1]))
            vch(V.tensor_copy(xsh[:, 0:1], b["pcol"][:, kt:kt + 1]))
            vch(V.tensor_tensor(b["tdiff"][:], xns, xsh[:], ALU.subtract))
            vch(V.scalar_tensor_tensor(b["xkT"][:, kt * Tc:(kt + 1) * Tc], b["tdiff"][:],
                                       b["mk_sb"][:, kt:kt + 1], xsh[:], ALU.mult, ALU.add))
            vch(V.scalar_tensor_tensor(b["xvT"][:, kt * Tc:(kt + 1) * Tc], b["tdiff"][:],
                                       b["mv_sb"][:, kt:kt + 1], xsh[:], ALU.mult, ALU.add))
            if not warm:
                vch(V.scalar_tensor_tensor(b["xrT"][:, kt * Tc:(kt + 1) * Tc], b["tdiff"][:],
                                           b["mr_sb"][:, kt:kt + 1], xsh[:], ALU.mult, ALU.add))
            vch(V.tensor_copy(b["pcol"][:, kt:kt + 1], xns[:, Tc - 1:Tc]))
        mix_done = ve.n
        free_at = b.setdefault("pkv_free", {})
        for m in range(16):
            par = m % 2
            wsl = b["wkvr_slot"][par]
            pairs = [(b["wkb"][par][:], wkT[m, :, :]), (b["wvb"][par][:], wvT[m, :, :])]
            if not warm:
                pairs.append((b["wrb"][par][:], wrT[m, :, :]))
            wsl.load(Y, pairs)
            wsl.wait(T)
            if m == 0:
                T.wait_ge(ve.h, mix_done)
            if par in free_at:
                for s_, v_ in free_at[par]:
                    T.wait_ge(s_.h, v_)
            pk = b["pk"][par]; pv = b["pv"][par]; pr = b["pr"][par]
            for kt in range(16):
                T.matmul(pk[:], b["wkb"][par][:, kt * P:(kt + 1) * P],
                         b["xkT"][:, kt * Tc:(kt + 1) * Tc], start=(kt == 0), stop=(kt == 15))
            for kt in range(16):
                ins = T.matmul(pv[:], b["wvb"][par][:, kt * P:(kt + 1) * P],
                               b["xvT"][:, kt * Tc:(kt + 1) * Tc], start=(kt == 0), stop=(kt == 15))
            if not warm:
                for kt in range(16):
                    ins = T.matmul(pr[:], b["wrb"][par][:, kt * P:(kt + 1) * P],
                                   b["xrT"][:, kt * Tc:(kt + 1) * Tc], start=(kt == 0), stop=(kt == 15))
            nT = inc(ins, mm)
            wsl.set_free(mm, nT)
            S.wait_ge(mm.h, nT)
            if b.get("scan_ve_prev"):
                S.wait_ge(ve.h, b["scan_ve_prev"])
            nS = inc(S.activation(b["e_sb"][:], pk[:], AF.Exp,
                                  bias=(b["warmb"][:] if warm else 0.0)), se)
            if not warm:
                nS2 = inc(S.activation(b["sr_sb"][:], pr[:], AF.Sigmoid), se)
            else:
                nS2 = nS
            V.wait_ge(se.h, nS)
            vch(V.tensor_tensor(b["p_sb"][:], b["e_sb"][:], pv[:], ALU.mult))
            lam_bc = b["lam_sb"][:, m:m + 1].broadcast_to((P, Tc))
            vch(V.tensor_tensor_scan(b["s_sb"][:], lam_bc, b["p_sb"][:],
                                     b["stS"][:, m:m + 1], ALU.mult, ALU.add))
            vch(V.tensor_tensor_scan(b["z_sb"][:], lam_bc, b["e_sb"][:],
                                     b["stZ"][:, m:m + 1], ALU.mult, ALU.add))
            nfree_v = ve.n
            if not warm:
                vch(V.tensor_copy(b["sh_sb"][:, 1:Tc], b["s_sb"][:, 0:Tc - 1]))
                vch(V.tensor_copy(b["sh_sb"][:, 0:1], b["stS"][:, m:m + 1]))
                vch(V.tensor_copy(b["zh_sb"][:, 1:Tc], b["z_sb"][:, 0:Tc - 1]))
                vch(V.tensor_copy(b["zh_sb"][:, 0:1], b["stZ"][:, m:m + 1]))
            vch(V.tensor_copy(b["stS"][:, m:m + 1], b["s_sb"][:, Tc - 1:Tc]))
            vch(V.tensor_copy(b["stZ"][:, m:m + 1], b["z_sb"][:, Tc - 1:Tc]))
            if not warm:
                vch(V.scalar_tensor_tensor(b["num_sb"][:], b["p_sb"][:], b["eu_sb"][:, m:m + 1],
                                           b["sh_sb"][:], ALU.mult, ALU.add))
                vch(V.scalar_tensor_tensor(b["den_sb"][:], b["e_sb"][:], b["eu_sb"][:, m:m + 1],
                                           b["zh_sb"][:], ALU.mult, ALU.add))
                vch(V.reciprocal(b["den_sb"][:], b["den_sb"][:]))
                V.wait_ge(se.h, nS2)
                vch(V.tensor_tensor(b["num_sb"][:], b["num_sb"][:], b["den_sb"][:], ALU.mult))
                vch(V.tensor_tensor(b["wkvsr"][:, m * Tc:(m + 1) * Tc], b["num_sb"][:],
                                    b["sr_sb"][:], ALU.mult))
            free_at[par] = [(ctx.sem("ve"), ve.n), (ctx.sem("se"), nS2)]
            b["scan_ve_prev"] = ve.n
        scan_done = ve.n
        if warm:
            continue
        for dq in range(4):
            par = 0
            wsl = b["wo_slot"][par]
            wsl.load(Y, [(b["wob"][par][:],
                          woT[:, :, dq * 512:(dq + 1) * 512].rearrange("k p f -> p k f"))])
            wsl.wait(T)
            if dq == 0:
                T.wait_ge(ve.h, scan_done)
            if dq >= 1:
                T.wait_ge(ve.h, b["wo_ev"][dq - 1])
            for kt in range(16):
                for tt in range(nt):
                    ins = T.matmul(b["po"][tt][:], b["wkvsr"][:, kt * Tc + tt * P: kt * Tc + (tt + 1) * P],
                                   b["wob"][par][:, kt * 512:(kt + 1) * 512],
                                   start=(kt == 0), stop=(kt == 15))
            nT = inc(ins, mm)
            wsl.set_free(mm, nT)
            xsl = b["xr_slot"][par]
            xsl.load(Y, [(b["xrm"][par][tt][:],
                          xv_ext.sl(c * Tc + tt * P, c * Tc + (tt + 1) * P, dq * 512, (dq + 1) * 512))
                         for tt in range(nt)])
            V.wait_ge(mm.h, nT)
            xsl.wait(V)
            stx_p = ctx.sem("rwst0")
            if stx_p.n > 0:
                V.wait_ge(stx_p.h, stx_p.n)
            for tt in range(nt):
                ins = V.tensor_tensor(b["xrw"][par][tt][:], b["po"][tt][:], b["xrm"][par][tt][:],
                                      ALU.add)
            nV = inc(ins, ve)
            xsl.set_free(ve, nV)
            b["wo_ev"][dq] = nV
            G.wait_ge(ve.h, nV)
            for tt in range(nt):
                store_x(G, stx_p, xv_out.sl((c - 1) * Tc + tt * P, (c - 1) * Tc + (tt + 1) * P,
                                            dq * 512, (dq + 1) * 512), b["xrw"][par][tt][:])
        b["pkv_free"] = {0: [(ve, b["wo_ev"][3])], 1: [(ve, b["wo_ev"][3])]}


def alloc_rwkv_bufs(ctx, pool=None):
    Tc = 512
    b = {}
    b["xnT"] = ctx.asb("rw_xnT", [P, 16 * Tc], BF16)
    b["xkT"] = ctx.asb("rw_xkT", [P, 16 * Tc], BF16)
    b["xvT"] = ctx.asb("rw_xvT", [P, 16 * Tc], BF16)
    b["xrT"] = ctx.asb("rw_xrT", [P, 16 * Tc], BF16)
    b["wkvsr"] = ctx.asb("rw_wkvsr", [P, 16 * Tc], BF16)
    b["xsh"] = ctx.asb("rw_xsh", [P, Tc], BF16)
    b["tdiff"] = ctx.asb("rw_tdiff", [P, Tc], BF16)
    b["pcol"] = ctx.asb("rw_pcol", [P, 16], BF16)
    for nm in ["e_sb", "p_sb", "s_sb", "z_sb", "sh_sb", "zh_sb", "num_sb", "den_sb", "sr_sb"]:
        b[nm] = ctx.asb("rw_" + nm, [P, Tc], F32)
    b["stS"] = ctx.asb("rw_stS", [P, 16], F32)
    b["stZ"] = ctx.asb("rw_stZ", [P, 16], F32)
    for nm in ["lam_sb", "eu_sb", "mk_sb", "mv_sb", "mr_sb"]:
        b[nm] = ctx.asb("rw_" + nm, [P, 16], F32)
    b["warmb"] = ctx.asb("rw_warmb", [P, 1], F32)
    b["td_sb"] = ctx.asb("rw_td", [P, 16], F32)
    b["tf_sb"] = ctx.asb("rw_tf", [P, 16], F32)
    b["tmp16"] = ctx.asb("rw_tmp16", [P, 16], F32)
    b["wkb"] = [ctx.asb(f"rw_wkb{j}", [P, 2048], BF16) for j in range(2)]
    b["wvb"] = [ctx.asb(f"rw_wvb{j}", [P, 2048], BF16) for j in range(2)]
    b["wrb"] = [ctx.asb(f"rw_wrb{j}", [P, 2048], BF16) for j in range(2)]
    b["wob"] = [ctx.asb("rw_wob0", [P, 16 * 512], BF16)]
    b["wkvr_slot"] = [Slot(ctx, f"rwkvr{j}") for j in range(2)]
    b["wo_slot"] = [Slot(ctx, "rwwo0")]
    b["xr_slot"] = [Slot(ctx, "rwxr0")]
    b["xrm"] = [[ctx.asb(f"rw_xrm0_{t}", [P, 512], F32) for t in range(4)]]
    b["xrw"] = [[ctx.asb(f"rw_xrw0_{t}", [P, 512], F32) for t in range(4)]]
    if pool is None:
        b["pk"] = [ctx.ps(f"rw_pk{j}", [P, 512], F32) for j in range(2)]
        b["pv"] = [ctx.ps(f"rw_pv{j}", [P, 512], F32) for j in range(2)]
        b["pr"] = [ctx.ps(f"rw_pr{j}", [P, 512], F32) for j in range(2)]
    else:
        b["pk"] = pool[0:2]
        b["pv"] = pool[2:4]
        b["pr"] = pool[4:6]
    b["po"] = b["pk"] + b["pv"]
    b["wo_ev"] = {}
    return b


def emit_rwkv_consts(ctx, b, td_in, tf_in, mk_in, mv_in, mr_in, warmb_in):
    V, S, Y = ctx.V, ctx.S, ctx.Y
    ve = ctx.sem("ve"); se = ctx.sem("se")
    sl = Slot(ctx, "rwconst")
    sl.load(Y, [(b["td_sb"][:], td_in), (b["tf_sb"][:], tf_in), (b["mk_sb"][:], mk_in),
                (b["mv_sb"][:], mv_in), (b["mr_sb"][:], mr_in), (b["warmb"][:], warmb_in)])
    sl.wait(S)
    nS = inc(S.activation(b["tmp16"][:], b["td_sb"][:], AF.Exp), se)
    S.wait_ge(se.h, nS)
    nS = inc(S.activation(b["lam_sb"][:], b["tmp16"][:], AF.Exp, scale=-1.0), se)
    S.wait_ge(se.h, nS)
    nS = inc(S.activation(b["eu_sb"][:], b["tf_sb"][:], AF.Exp), se)
    V.wait_ge(se.h, nS)
    nV = inc(V.memset(b["pcol"][:], 0.0), ve)
    nV = inc(V.memset(b["stS"][:], 0.0), ve)
    nV = inc(V.memset(b["stZ"][:], 0.0), ve)
    V.wait_ge(ve.h, nV)
    return nS, nV


def build_rwkv(nc):
    import contextlib
    ctx = Ctx(nc)
    ctx._es = contextlib.ExitStack()
    dr = lambda nm, sh, dt, kind: nc.dram_tensor(nm, sh, dt, kind=kind).ap()
    I, O = "ExternalInput", "ExternalOutput"
    x_ext = dr("x_ext", [2560, 2048], F32, I)
    wkT = dr("wkT", [16, P, 2048], BF16, I)
    wvT = dr("wvT", [16, P, 2048], BF16, I)
    wrT = dr("wrT", [16, P, 2048], BF16, I)
    woT = dr("woT", [16, P, 2048], BF16, I)
    td = dr("td", [P, 16], F32, I)
    tf = dr("tf", [P, 16], F32, I)
    mkc = dr("mkc", [P, 16], F32, I)
    mvc = dr("mvc", [P, 16], F32, I)
    mrc = dr("mrc", [P, 16], F32, I)
    warmb = dr("warmb", [P, 1], F32, I)
    identb = dr("identb", [P, P], BF16, I)
    xo = dr("xo", [2048, 2048], F32, O)

    lnb = alloc_ln_bufs(ctx)
    lnb["ident_bf"] = ctx.sb("identb_sb", [P, P], BF16)
    lnb["psbf"] = ctx.ps("psbf", [P, 2048], BF16)
    b = alloc_rwkv_bufs(ctx)

    idl = Slot(ctx, "idl")
    idl.load(ctx.Y, [(lnb["ident_bf"][:], identb)])
    idl.wait(ctx.T)
    emit_rwkv_consts(ctx, b, td, tf, mkc, mvc, mrc, warmb)
    emit_rwkv_body(ctx, XV(ap=x_ext), XV(ap=xo), wkT, wvT, wrT, woT, lnb, b)
    s = ctx.sem("rwst0")
    ctx.G.wait_ge(s.h, s.n)
    finalize(ctx)
    ctx._es.close()
    return nc



def emit_moe_ln_gating(ctx, lnb, b, xv, row_base, gwb, gwlo, n_tiles, Tc):
    T, V, S, Y = ctx.T, ctx.V, ctx.S, ctx.Y
    ve = ctx.sem("ve"); se = ctx.sem("se"); mm = ctx.sem("mm"); frp = ctx.sem("frp")
    D = 2048

    def vch(op):
        n = inc(op, ve)
        V.wait_ge(ve.h, n)
        return n

    for i in range(n_tiles):
        par = i % len(lnb["xa"])
        xa = lnb["xa"][par]; xn = lnb["xn"][par]
        sl = lnb["xa_slot"][par]
        sl.load(Y, [(xa[:], xv.sl(row_base + i * P, row_base + (i + 1) * P))])
        sl.wait(V)
        nV = inc(V.reduce_sum(lnb["s1"][:], xa[:], axis=AX.X), ve)
        sl.wait(S)
        nS = inc(S.activation(lnb["sq"][:], xa[:], AF.Square, accum_out=lnb["ss"][:]), se)
        V.wait_ge(se.h, nS)
        V.wait_ge(ve.h, nV)
        vch(V.tensor_scalar(lnb["va"][:], lnb["ss"][:], 1.0 / D, None, ALU.mult))
        vch(V.tensor_scalar(lnb["vb"][:], lnb["s1"][:], 1.0 / D, None, ALU.mult))
        vch(V.tensor_tensor(lnb["vc"][:], lnb["vb"][:], lnb["vb"][:], ALU.mult))
        vch(V.tensor_tensor(lnb["vc"][:], lnb["va"][:], lnb["vc"][:], ALU.subtract))
        nV = vch(V.tensor_scalar(lnb["vc"][:], lnb["vc"][:], 1e-5, None, ALU.add))
        S.wait_ge(ve.h, nV)
        nS = inc(S.activation(lnb["lnv"][:], lnb["vc"][:], AF.Ln), se)
        S.wait_ge(se.h, nS)
        nS = inc(S.activation(lnb["rstd"][:], lnb["lnv"][:], AF.Exp, scale=-0.5), se)
        V.wait_ge(se.h, nS)
        vch(V.tensor_tensor(lnb["nmr"][:], lnb["vb"][:], lnb["rstd"][:], ALU.mult))
        nV = vch(V.tensor_scalar(lnb["nmr"][:], lnb["nmr"][:], -1.0, None, ALU.mult))
        S.wait_ge(ve.h, nV)
        if b.get("sq_free"):
            S.wait_ge(ve.h, b["sq_free"])
        nS = inc(S.activation(lnb["sq"][:], xa[:], AF.Identity, bias=lnb["nmr"][:],
                              scale=lnb["rstd"][:]), se)
        sl.set_free(se, nS)
        V.wait_ge(se.h, nS)
        nV = vch(V.tensor_copy(xn[:], lnb["sq"][:]))
        nV = vch(V.tensor_tensor(b["lo_t"][:], lnb["sq"][:], xn[:], ALU.subtract))
        b["sq_free"] = nV
        T.wait_ge(ve.h, nV)
        if lnb["psbf_busy"]:
            T.wait_ge(frp.h, lnb["psbf_busy"])
        for kt in range(16):
            ins = T.transpose(lnb["psbf"][:, kt * P:(kt + 1) * P], xn[:, kt * P:(kt + 1) * P],
                              lnb["ident_bf"][:])
        nT = inc(ins, mm)
        V.wait_ge(mm.h, nT)
        for kt in range(16):
            ins = V.tensor_copy(b["xnT"][:, kt * Tc + i * P: kt * Tc + (i + 1) * P],
                                lnb["psbf"][:, kt * P:(kt + 1) * P])
        nfrp = inc(ins, frp)
        lnb["psbf_busy"] = nfrp
        T.wait_ge(frp.h, nfrp)
        for kt in range(16):
            ins = T.transpose(lnb["psbf"][:, kt * P:(kt + 1) * P], b["lo_t"][:, kt * P:(kt + 1) * P],
                              lnb["ident_bf"][:])
        nT = inc(ins, mm)
        V.wait_ge(mm.h, nT)
        for kt in range(16):
            ins = V.tensor_copy(b["loT_t"][:, kt * P:(kt + 1) * P], lnb["psbf"][:, kt * P:(kt + 1) * P])
        nfrp = inc(ins, frp)
        lnb["psbf_busy"] = nfrp
        nloT = inc(V.engine_nop(), ve)
        T.wait_ge(ve.h, nloT)
        if b.get("gt_ev"):
            T.wait_ge(ve.h, b["gt_ev"])
        first = True
        for kt in range(16):
            T.matmul(b["pgt"][:, 0:8], b["xnT"][:, kt * Tc + i * P: kt * Tc + (i + 1) * P],
                     gwb[:, kt * 8:(kt + 1) * 8], start=first, stop=False)
            first = False
            T.matmul(b["pgt"][:, 0:8], b["xnT"][:, kt * Tc + i * P: kt * Tc + (i + 1) * P],
                     gwlo[:, kt * 8:(kt + 1) * 8], start=False, stop=False)
            ins = T.matmul(b["pgt"][:, 0:8], b["loT_t"][:, kt * P:(kt + 1) * P],
                           gwb[:, kt * 8:(kt + 1) * 8], start=False, stop=(kt == 15))
        nT = inc(ins, mm)
        V.wait_ge(mm.h, nT)
        tt = i
        lg = b["lg"]; mx = b["mx"]; es_ = b["es"]
        vch(V.tensor_copy(lg[:], b["pgt"][:, 0:8]))
        b["gt_ev"] = ve.n
        vch(V.reduce_max(mx[:], lg[:], axis=AX.X))
        nV = vch(V.tensor_scalar(mx[:], mx[:], -1.0, None, ALU.mult))
        S.wait_ge(ve.h, nV)
        nS = inc(S.activation(es_[:], lg[:], AF.Exp, bias=mx[:], accum_out=b["den8"][:]), se)
        V.wait_ge(se.h, nS)
        vch(V.reciprocal(b["rden8"][:], b["den8"][:]))
        sc = b["sc_sb"][:, tt * 8:(tt + 1) * 8]
        vch(V.tensor_scalar(sc, es_[:], b["rden8"][:], None, ALU.mult))
        vch(V.reduce_max(b["mx1"][:], sc, axis=AX.X))
        vch(V.tensor_scalar(b["is1"][:], sc, b["mx1"][:], None, ALU.is_equal))
        vch(V.scalar_tensor_tensor(b["tmsk"][:], b["is1"][:], -1e30, sc, ALU.mult, ALU.add))
        vch(V.reduce_max(b["mx2"][:], b["tmsk"][:], axis=AX.X))
        vch(V.tensor_scalar(b["is2"][:], b["tmsk"][:], b["mx2"][:], None, ALU.is_equal))
        vch(V.tensor_tensor(b["is1"][:], b["is1"][:], b["is2"][:], ALU.add))
        vch(V.tensor_tensor(b["comb"][:, tt * 8:(tt + 1) * 8], sc, b["is1"][:], ALU.mult))
    return ve.n

def emit_moe_body(ctx, xv_in, xv_out, mgb, mub, mdb, swgb, swub, swdb, gwb_pair, lnb, b):
    import os
    """x, xo [2048, 2048] f32. mgb/mub [8, 2048, 1408] bf16, mdb [8, 1408, 2048] bf16,
    swgb/swub [2048, 2816] bf16, swdb [2816, 2048] bf16. gwb_sb [128, 16*8] bf16 resident."""
    T, V, S, Y, G = ctx.T, ctx.V, ctx.S, ctx.Y, ctx.G
    ve = ctx.sem("ve"); se = ctx.sem("se"); mm = ctx.sem("mm"); frp = ctx.sem("frp")
    Tc = 1024
    nt = 8

    def vch(op):
        n = inc(op, ve)
        V.wait_ge(ve.h, n)
        return n

    gwb_sb, gwlo_sb = gwb_pair
    for c in range(int(os.environ.get('MOE_NCHUNK', 2))):
        comb_done = emit_moe_ln_gating(ctx, lnb, b, xv_in, c * Tc, gwb_sb, gwlo_sb, nt, Tc)
        if "dbg_comb" in b:
            G.wait_ge(ve.h, comb_done)
            dbs = ctx.sem("dbs")
            inc(G.dma_start(out=b["dbg_comb"][c], in_=b["comb"][:]), dbs, 16)
            inc(G.dma_start(out=b["dbg_sc"][c], in_=b["sc_sb"][:]), dbs, 16)
            if c == 1:
                inc(G.dma_start(out=b["dbg_xnT"], in_=b["xnT"][:]), dbs, 16)
            G.wait_ge(dbs.h, dbs.n)
        T.wait_ge(ve.h, comb_done)
        for tt in range(nt):
            ins = T.transpose(b["pct"][tt // 4][0:8, (tt % 4) * P:(tt % 4 + 1) * P],
                              b["comb"][:, tt * 8:(tt + 1) * 8], b["ident_f"][:])
        nT = inc(ins, mm)
        V.wait_ge(mm.h, nT)
        vch(V.tensor_copy(b["combT"][0:8, 0:512], b["pct"][0][0:8, :]))
        vch(V.tensor_copy(b["combT"][0:8, 512:1024], b["pct"][1][0:8, :]))
        ct_done = ve.n

        glist = [4] if os.environ.get('MOE_ONLYSHARED') else list(range(int(os.environ.get('MOE_MAXG', 5))))
        for g in glist:
            shared = (g == 4)
            n_m_half = [(0, 6), (6, 5)] if not shared else [(0, 6), (6, 5), (11, 6), (17, 5)]
            NKT_D = 22
            nexp = 1 if shared else 2
            src_xv = xv_in if (g == 0 or (os.environ.get('MOE_ONLYSHARED') and g == 4)) else xv_out
            h_cols = 0
            if shared and 7 in b["dn_ev"]:
                b["gu_ev0"] = max(b.get("gu_ev0", 0), b["dn_ev"][7])
                b["gu_ev1"] = max(b.get("gu_ev1", 0), b["dn_ev"][7])
            for ei in range(nexp):
                e = 2 * g + ei if not shared else 0
                if not shared:
                    T.wait_ge(ve.h, ct_done)
                    if b.get("cbc_ev"):
                        T.wait_ge(ve.h, b["cbc_ev"])
                    if b.get("gu_ev"):
                        T.wait_ge(ve.h, b["gu_ev"])
                    for nb in range(2):
                        if nb == 1:
                            T.wait_ge(ve.h, ve.n)
                        ins = T.matmul(b["pcb"][:], b["sel8"][:, e * P:(e + 1) * P],
                                       b["combT"][0:8, nb * 512:(nb + 1) * 512],
                                       start=True, stop=True)
                        nT = inc(ins, mm)
                        V.wait_ge(mm.h, nT)
                        vch(V.tensor_copy(b["cbc"][:, nb * 512:(nb + 1) * 512], b["pcb"][:]))
                    b["cbc_ev"] = ve.n
                    if 7 in b["dn_ev"]:
                        b["gu_ev0"] = max(b.get("gu_ev0", 0), b["dn_ev"][7])
                        b["gu_ev1"] = max(b.get("gu_ev1", 0), b["dn_ev"][7])
                for (o0, no) in n_m_half:
                    wsl = b["wgu_slot"]
                    if shared:
                        gsrc, usrc = swgb, swub
                        pairs = []
                        for kt in range(16):
                            pairs.append((b["wg_sb"][:, kt * no * P:(kt + 1) * no * P],
                                          gsrc[kt * P:(kt + 1) * P, o0 * P:(o0 + no) * P]))
                            pairs.append((b["wu_sb"][:, kt * no * P:(kt + 1) * no * P],
                                          usrc[kt * P:(kt + 1) * P, o0 * P:(o0 + no) * P]))
                    else:
                        pairs = []
                        for kt in range(16):
                            pairs.append((b["wg_sb"][:, kt * no * P:(kt + 1) * no * P],
                                          mgb[e, kt * P:(kt + 1) * P, o0 * P:(o0 + no) * P]))
                            pairs.append((b["wu_sb"][:, kt * no * P:(kt + 1) * no * P],
                                          mub[e, kt * P:(kt + 1) * P, o0 * P:(o0 + no) * P]))
                    wsl.load(Y, pairs)
                    wsl.wait(T)
                    for m in range(no):
                        hc = (ei * 11 + o0 + m) * Tc
                        for th in range(2):
                            gpar = b.get("gu_par", 0)
                            b["gu_par"] = 1 - gpar
                            if b.get("gu_ev%d" % gpar):
                                T.wait_ge(ve.h, b["gu_ev%d" % gpar])
                            pg = b["pgu4"][2 * gpar]; pu = b["pgu4"][2 * gpar + 1]
                            for kt in range(16):
                                T.matmul(pg[:], b["wg_sb"][:, (kt * no + m) * P:(kt * no + m + 1) * P],
                                         b["xnT"][:, kt * Tc + th * 512: kt * Tc + th * 512 + 512],
                                         start=(kt == 0), stop=(kt == 15))
                            for kt in range(16):
                                ins = T.matmul(pu[:], b["wu_sb"][:, (kt * no + m) * P:(kt * no + m + 1) * P],
                                               b["xnT"][:, kt * Tc + th * 512: kt * Tc + th * 512 + 512],
                                               start=(kt == 0), stop=(kt == 15))
                            nT = inc(ins, mm)
                            if m == no - 1 and th == 1:
                                wsl.set_free(mm, nT)
                            S.wait_ge(mm.h, nT)
                            if b.get("sg_ev%d" % gpar):
                                S.wait_ge(ve.h, b["sg_ev%d" % gpar])
                            nS = inc(S.activation(b["sg%d" % gpar][:], pg[:], AF.Silu), se)
                            V.wait_ge(se.h, nS)
                            vch(V.tensor_tensor(b["hf"][:, 0:512], b["sg%d" % gpar][:], pu[:], ALU.mult))
                            b["sg_ev%d" % gpar] = ve.n
                            if shared:
                                ins = V.tensor_copy(b["h_sb"][:, hc + th * 512: hc + (th + 1) * 512],
                                                    b["hf"][:, 0:512])
                            else:
                                ins = V.tensor_tensor(b["h_sb"][:, hc + th * 512: hc + (th + 1) * 512],
                                                      b["hf"][:, 0:512],
                                                      b["cbc"][:, th * 512:(th + 1) * 512], ALU.mult)
                            nV = inc(ins, ve)
                            V.wait_ge(ve.h, nV)
                            b["gu_ev%d" % gpar] = nV
            h_done = ve.n
            if "dbg_h" in b and c == 1 and shared:
                dbs = ctx.sem("dbs")
                G.wait_ge(ve.h, h_done)
                inc(G.dma_start(out=b["dbg_h"], in_=b["h_sb"][:]), dbs, 16)
                G.wait_ge(dbs.h, dbs.n)
            for dc in range(8):
                par = dc % 2
                wsl = b["wd_slot"][0]
                if shared:
                    wd_pairs = [(b["wd_sb"][0][:, 0:22 * 256],
                                 swdb[:, dc * 256:(dc + 1) * 256].rearrange("(k p) f -> p k f", p=P))]
                else:
                    wd_pairs = [(b["wd_sb"][0][:, ei * 11 * 256:(ei + 1) * 11 * 256],
                                 mdb[2 * g + ei][:, dc * 256:(dc + 1) * 256].rearrange("(k p) f -> p k f", p=P))
                                for ei in range(nexp)]
                wsl.load(Y, wd_pairs)
                xsl = b["xrm_slot"][par]
                for pp in (0, 1):
                    stq = ctx.sem("moestp" + str(pp))
                    if g > 0 and stq.n > 0:
                        Y.wait_ge(stq.h, stq.n)
                xsl.load(Y, [(b["xrm"][par][tt][:],
                              src_xv.sl(c * Tc + tt * P, c * Tc + (tt + 1) * P, dc * 256, (dc + 1) * 256))
                             for tt in range(nt)])
                wsl.wait(T)
                if dc == 0:
                    T.wait_ge(ve.h, h_done)
                    for gp in (0, 1):
                        if b.get("gu_ev%d" % gp):
                            T.wait_ge(ve.h, b["gu_ev%d" % gp])
                    if 7 in b["dn_ev"]:
                        T.wait_ge(ve.h, b["dn_ev"][7])
                if dc >= 1:
                    T.wait_ge(ve.h, b["dn_ev"][dc - 1])
                for ks in range(NKT_D):
                    for tt in range(nt):
                        psl = b["pdn"][tt // 2][:, (tt % 2) * 256:(tt % 2) * 256 + 256]
                        ins = T.matmul(psl, b["h_sb"][:, ks * Tc + tt * P: ks * Tc + (tt + 1) * P],
                                       b["wd_sb"][0][:, ks * 256:(ks + 1) * 256],
                                       start=(ks == 0 and tt % 2 == 0),
                                       stop=(ks == NKT_D - 1 and tt % 2 == 1))
                nT = inc(ins, mm)
                wsl.set_free(mm, nT)
                V.wait_ge(mm.h, nT)
                xsl.wait(V)
                stx_p = ctx.sem("moestp0")
                if stx_p.n > 0:
                    V.wait_ge(stx_p.h, stx_p.n)
                for tt in range(nt):
                    psl = b["pdn"][tt // 2][:, (tt % 2) * 256:(tt % 2) * 256 + 256]
                    ins = V.scalar_tensor_tensor(b["xrw"][par][tt][:], psl, 0.5,
                                                 b["xrm"][par][tt][:], ALU.mult, ALU.add)
                nV = inc(ins, ve)
                xsl.set_free(ve, nV)
                b["dn_ev"][dc] = nV
                G.wait_ge(ve.h, nV)
                for tt in range(nt):
                    store_x(G, stx_p, xv_out.sl(c * Tc + tt * P, c * Tc + (tt + 1) * P,
                                                dc * 256, (dc + 1) * 256), b["xrw"][par][tt][:])


def alloc_moe_bufs(ctx, pool=None):
    Tc = 1024
    b = {}
    b["xnT"] = ctx.asb("mo_xnT", [P, 16 * Tc], BF16)
    b["h_sb"] = ctx.asb("mo_h", [P, 22 * Tc], BF16)
    b["wg_sb"] = ctx.asb("mo_wg", [P, 16 * 768], BF16)
    b["wu_sb"] = ctx.asb("mo_wu", [P, 16 * 768], BF16)
    b["wd_sb"] = [ctx.asb("mo_wd0", [P, 22 * 256], BF16)]
    b["sg0"] = ctx.asb("mo_sg0", [P, 512], F32)
    b["lo_t"] = ctx.asb("mo_lo_t", [P, 2048], BF16)
    b["sg1"] = ctx.nc.alloc_sbuf_tensor_at("mo_sg1", [P, 512], F32,
                                           offset=ctx.nc.lookup_mloc(b["lo_t"]).addr)
    b["loT_t"] = ctx.asb("mo_loT_t", [P, 2048], BF16)
    b["hf"] = ctx.asb("mo_hf", [P, 512], F32)
    b["cbc"] = ctx.asb("mo_cbc", [P, Tc], F32)
    b["combT"] = ctx.asb("mo_combT", [8, Tc], F32)
    b["sc_sb"] = ctx.asb("mo_sc", [P, 64], F32)
    b["comb"] = ctx.asb("mo_comb", [P, 64], F32)
    for nm in ["lg", "es", "is1", "is2", "tmsk"]:
        b[nm] = ctx.asb("mo_" + nm, [P, 8], F32)
    for nm in ["mx", "mx1", "mx2", "den8", "rden8"]:
        b[nm] = ctx.asb("mo_" + nm, [P, 1], F32)
    b["ones_f"] = ctx.asb("mo_ones", [1, P], F32)
    b["sel8"] = ctx.asb("mo_sel8", [8, 8 * P], F32)
    b["ident_f"] = ctx.asb("mo_identf", [P, P], F32)
    b["xrm"] = [[ctx.asb(f"mo_xrm0_{t}", [P, 256], F32) for t in range(8)]]
    _x1 = [ctx.asb(f"mo_xrm1_{t}", [P, 256], F32) for t in range(4)]
    _x1 += [ctx.nc.alloc_sbuf_tensor_at(f"mo_xrm1_{t}", [P, 256], F32,
                                        offset=ctx.nc.lookup_mloc(b["loT_t"]).addr + (t - 4) * 1024)
            for t in range(4, 8)]
    b["xrm"].append(_x1)
    _xrw = [ctx.asb(f"mo_xrw0_{t}", [P, 256], F32) for t in range(8)]
    b["xrw"] = [_xrw, _xrw]
    b["wgu_slot"] = Slot(ctx, "mowgu")
    b["wd_slot"] = [Slot(ctx, "mowd0")]
    b["xrm_slot"] = [Slot(ctx, "moxrm0"), Slot(ctx, "moxrm1")]
    if pool is None:
        pool = [ctx.ps(f"mo_pl{j}", [P, 512], F32) for j in range(6)]
    b["pgu4"] = pool[0:4]
    b["pdn"] = pool[0:4]
    b["pgt"] = pool[4]
    b["pcb"] = pool[5]
    b["pct"] = [pool[4], pool[5]]
    b["st_mark"] = {}
    b["dn_ev"] = {}
    return b


def build_moe(nc):
    import contextlib
    ctx = Ctx(nc)
    ctx._es = contextlib.ExitStack()
    dr = lambda nm, sh, dt, kind: nc.dram_tensor(nm, sh, dt, kind=kind).ap()
    I, O = "ExternalInput", "ExternalOutput"
    x = dr("x", [2048, 2048], F32, I)
    mgb = dr("mgb", [8, 2048, 1408], BF16, I)
    mub = dr("mub", [8, 2048, 1408], BF16, I)
    mdb = dr("mdb", [8, 1408, 2048], BF16, I)
    swgb = dr("swgb", [2048, 2816], BF16, I)
    swub = dr("swub", [2048, 2816], BF16, I)
    swdb = dr("swdb", [2816, 2048], BF16, I)
    gw = dr("gw", [2048, 8], F32, I)
    identb = dr("identb", [P, P], BF16, I)
    identf = dr("identf", [P, P], F32, I)
    onesf = dr("onesf", [1, P], F32, I)
    sel8 = dr("sel8", [8, 8 * P], F32, I)
    xo = dr("xo", [2048, 2048], F32, O)

    lnb = alloc_ln_bufs(ctx, nx=1)
    lnb["ident_bf"] = ctx.sb("identb_sb", [P, P], BF16)
    lnb["psbf"] = ctx.ps("psbf", [P, 2048], BF16)
    b = alloc_moe_bufs(ctx)

    gw_f = ctx.sb("mo_gwf", [P, 16 * 8], F32)
    gwb = ctx.sb("mo_gwb", [P, 16 * 8], BF16)
    gwlo = ctx.sb("mo_gwlo", [P, 16 * 8], BF16)
    idl = Slot(ctx, "idl")
    idl.load(ctx.Y, [(lnb["ident_bf"][:], identb), (b["ident_f"][:], identf),
                     (b["ones_f"][:], onesf), (b["sel8"][:], sel8),
                     (gw_f[:], gw.rearrange("(k p) e -> p k e", p=P))])
    idl.wait(ctx.V)
    ve = ctx.sem("ve")
    nV = inc(ctx.V.tensor_copy(gwb[:], gw_f[:]), ve)
    ctx.V.wait_ge(ve.h, nV)
    nV = inc(ctx.V.tensor_tensor(gwlo[:], gw_f[:], gwb[:], ALU.subtract), ve)
    ctx.V.wait_ge(ve.h, nV)
    idl.wait(ctx.T)

    import os
    if os.environ.get("MOE_DEBUG"):
        b["dbg_comb"] = dr("dbg_comb", [2, P, 64], F32, O)
        b["dbg_sc"] = dr("dbg_sc", [2, P, 64], F32, O)
        b["dbg_xnT"] = dr("dbg_xnT", [P, 16 * 1024], BF16, O)
        b["dbg_h"] = dr("dbg_h", [P, 22 * 1024], BF16, O)
    emit_moe_body(ctx, XV(ap=x), XV(ap=xo), mgb, mub, mdb, swgb, swub, swdb, (gwb, gwlo), lnb, b)
    for pp in (0, 1):
        s = ctx.sem("moestp" + str(pp))
        ctx.G.wait_ge(s.h, s.n)
    finalize(ctx)
    ctx._es.close()
    return nc


def emit_mla_body(ctx, xv_full, xv_own, xv_out, kvaT, kvbT, qwT, owT, lnb, b):
    T, V, S, Y, G = ctx.T, ctx.V, ctx.S, ctx.Y, ctx.G
    ve = ctx.sem("ve"); se = ctx.sem("se"); mm = ctx.sem("mm"); frp = ctx.sem("frp")
    EPS = 1e-6
    SCALE = 1.0 / math.sqrt(128.0)

    def vch(op):
        n = inc(op, ve)
        V.wait_ge(ve.h, n)
        return n

    for i in range(32):
        par = i % len(lnb["xa"])
        xa = lnb["xa"][par]; xn = lnb["xn"][par]
        sl = lnb["xa_slot"][par]
        sl.load(Y, [(xa[:], xv_full.sl(i * P, (i + 1) * P))])
        sl.wait(V)
        nV = inc(V.reduce_sum(lnb["s1"][:], xa[:], axis=AX.X), ve)
        sl.wait(S)
        nS = inc(S.activation(lnb["sq"][:], xa[:], AF.Square, accum_out=lnb["ss"][:]), se)
        V.wait_ge(se.h, nS)
        V.wait_ge(ve.h, nV)
        vch(V.tensor_scalar(lnb["va"][:], lnb["ss"][:], 1.0 / 2048, None, ALU.mult))
        vch(V.tensor_scalar(lnb["vb"][:], lnb["s1"][:], 1.0 / 2048, None, ALU.mult))
        vch(V.tensor_tensor(lnb["vc"][:], lnb["vb"][:], lnb["vb"][:], ALU.mult))
        vch(V.tensor_tensor(lnb["vc"][:], lnb["va"][:], lnb["vc"][:], ALU.subtract))
        nV = vch(V.tensor_scalar(lnb["vc"][:], lnb["vc"][:], 1e-5, None, ALU.add))
        S.wait_ge(ve.h, nV)
        nS = inc(S.activation(lnb["lnv"][:], lnb["vc"][:], AF.Ln), se)
        S.wait_ge(se.h, nS)
        nS = inc(S.activation(lnb["rstd"][:], lnb["lnv"][:], AF.Exp, scale=-0.5), se)
        V.wait_ge(se.h, nS)
        vch(V.tensor_tensor(lnb["nmr"][:], lnb["vb"][:], lnb["rstd"][:], ALU.mult))
        nV = vch(V.tensor_scalar(lnb["nmr"][:], lnb["nmr"][:], -1.0, None, ALU.mult))
        S.wait_ge(ve.h, nV)
        nS = inc(S.activation(xn[:], xa[:], AF.Identity, bias=lnb["nmr"][:], scale=lnb["rstd"][:]), se)
        sl.set_free(se, nS)
        T.wait_ge(se.h, nS)
        if lnb["psbf_busy"]:
            T.wait_ge(frp.h, lnb["psbf_busy"])
        for kt in range(16):
            ins = T.transpose(lnb["psbf"][:, kt * P:(kt + 1) * P], xn[:, kt * P:(kt + 1) * P],
                              lnb["ident_bf"][:])
        nT = inc(ins, mm)
        V.wait_ge(mm.h, nT)
        if b.get("xnt_ev"):
            V.wait_ge(mm.h, b["xnt_ev"])
        for kt in range(16):
            ins = V.tensor_copy(b["xnT_t"][:, kt * P:(kt + 1) * P], lnb["psbf"][:, kt * P:(kt + 1) * P])
        nfrp = inc(ins, frp)
        lnb["psbf_busy"] = nfrp
        T.wait_ge(frp.h, nfrp)
        if b.get("kvc_ev"):
            T.wait_ge(ve.h, b["kvc_ev"])
        for kt in range(16):
            ins = T.matmul(b["pkv"][:], b["xnT_t"][:, kt * P:(kt + 1) * P],
                           b["kvaT_sb"][:, kt * 512:(kt + 1) * 512], start=(kt == 0), stop=(kt == 15))
        nT = inc(ins, mm)
        b["xnt_ev"] = nT
        S.wait_ge(mm.h, nT)
        if b.get("lat_sq_ev"):
            S.wait_ge(ve.h, b["lat_sq_ev"])
        nS = inc(S.activation(b["sq512"][:], b["pkv"][:], AF.Square, accum_out=b["ss2"][:]), se)
        V.wait_ge(se.h, nS)
        vch(V.tensor_scalar(b["va2"][:], b["ss2"][:], 1.0 / 512, None, ALU.mult))
        nV = vch(V.tensor_scalar(b["va2"][:], b["va2"][:], EPS, None, ALU.add))
        S.wait_ge(ve.h, nV)
        nS = inc(S.activation(b["lnv2"][:], b["va2"][:], AF.Ln), se)
        S.wait_ge(se.h, nS)
        nS = inc(S.activation(b["rstd2"][:], b["lnv2"][:], AF.Exp, scale=-0.5), se)
        V.wait_ge(se.h, nS)
        vch(V.tensor_scalar(b["lat32"][:], b["pkv"][:], b["rstd2"][:], None, ALU.mult))
        b["kvc_ev"] = ve.n
        nV = vch(V.tensor_tensor(b["lat_t"][:], b["lat32"][:], b["wkv_bc"][:], ALU.mult))
        b["lat_sq_ev"] = nV
        T.wait_ge(ve.h, nV)
        T.wait_ge(frp.h, lnb["psbf_busy"])
        for kt in range(4):
            ins = T.transpose(lnb["psbf"][:, kt * P:(kt + 1) * P], b["lat_t"][:, kt * P:(kt + 1) * P],
                              lnb["ident_bf"][:])
        nT = inc(ins, mm)
        V.wait_ge(mm.h, nT)
        for kt in range(4):
            ins = V.tensor_copy(b["latT"][:, kt * 4096 + i * P: kt * 4096 + (i + 1) * P],
                                lnb["psbf"][:, kt * P:(kt + 1) * P])
        nfrp = inc(ins, frp)
        lnb["psbf_busy"] = nfrp
    latT_done = ctx.sem("frp").n

    def rope_inplace(buf, cosb, sinb, F):
        rsem = ctx.sem("rope")
        G.wait_ge(ve.h, ve.n)
        if b.get("rope_free"):
            G.wait_ge(ve.h, b["rope_free"])
        inc(G.dma_start(out=b["rot"][0:32, 0:F], in_=buf[32:64, 0:F]), rsem, 16)
        inc(G.dma_start(out=b["rot"][32:64, 0:F], in_=buf[0:32, 0:F]), rsem, 16)
        V.wait_ge(rsem.h, rsem.n)
        vch(V.tensor_scalar(b["rot"][0:32, 0:F], b["rot"][0:32, 0:F], -1.0, None, ALU.mult))
        vch(V.tensor_tensor(b["rt1"][0:64, 0:F], buf[0:64, 0:F], cosb[0:64, 0:F], ALU.mult))
        vch(V.tensor_tensor(b["rot"][0:64, 0:F], b["rot"][0:64, 0:F], sinb[0:64, 0:F], ALU.mult))
        nV = vch(V.tensor_tensor(buf[0:64, 0:F], b["rt1"][0:64, 0:F], b["rot"][0:64, 0:F], ALU.add))
        b["rope_free"] = nV
        return nV

    for toh in range(2):
        own_done = emit_ln_transpose(ctx, lnb, xv_own, 8, b["xnT_own"], 1024,
                                      row_base=toh * 1024)
        for h in range(16):
            qsl = b["qw_slot"]
            qsl.load(Y, [(b["qwb"][:], qwT[h, :, :])])
            ksl = b["kvb_slot"]
            ksl.load(Y, [(b["kvbb"][:, 0:512], kvbT[2 * h, :, :]),
                         (b["kvbb"][:, 512:1024], kvbT[2 * h + 1, :, :])])
            ksl.wait(T)
            if toh == 0 and h == 0:
                T.wait_ge(frp.h, latT_done)
            T.wait_ge(frp.h, own_done)
            bev = b.setdefault("bankev", {})
            for nb in range(8):
                pk = b["psc"][nb % 2]
                if (nb % 2) in bev:
                    eng_, val_ = bev[nb % 2]
                    T.wait_ge(eng_.h, val_)
                if nb >= 2:
                    T.wait_ge(ve.h, b["kevac"][nb - 2])
                for kt in range(4):
                    ins = T.matmul(pk[:], b["kvbb"][:, kt * P:(kt + 1) * P],
                                   b["latT"][:, kt * 4096 + nb * 512: kt * 4096 + (nb + 1) * 512],
                                   start=(kt == 0), stop=(kt == 3))
                nT = inc(ins, mm)
                V.wait_ge(mm.h, nT)
                if nb == 0 and b.get("kb_war"):
                    V.wait_ge(mm.h, b["kb_war"])
                ins = V.tensor_copy(b["kb"][:, nb * 512:(nb + 1) * 512], pk[:])
                b["kevac"][nb] = inc(ins, ve)
                bev[nb % 2] = (ctx.sem("ve"), b["kevac"][nb])
            rope_inplace(b["kb"], b["kcos"], b["ksin"], 4096)
            kb_done = ctx.sem("ve").n
            for tk in range(32):
                pv = b["pav"]
                if b.get("v_ev"):
                    T.wait_ge(ve.h, b["v_ev"])
                if tk == 0 and b.get("av_ev"):
                    T.wait_ge(ve.h, b["av_ev"])
                for kt in range(4):
                    ins = T.matmul(pv[:, 0:P], b["latT"][:, kt * 4096 + tk * P: kt * 4096 + (tk + 1) * P],
                                   b["kvbb"][:, 512 + kt * P: 512 + (kt + 1) * P],
                                   start=(kt == 0), stop=(kt == 3))
                nT = inc(ins, mm)
                ksl.set_free(mm, nT)
                V.wait_ge(mm.h, nT)
                ins = V.tensor_copy(b["v_sb"][:, tk * P:(tk + 1) * P], pv[:, 0:P])
                b["v_ev"] = inc(ins, ve)
            v_done = ctx.sem("ve").n
            qsl.wait(T)
            for nb in range(2):
                pq = b["psc"][nb % 2]
                if (nb % 2) in bev:
                    eng_, val_ = bev[nb % 2]
                    T.wait_ge(eng_.h, val_)
                if b.get("q_ev"):
                    T.wait_ge(ve.h, b["q_ev"])
                for kt in range(16):
                    ins = T.matmul(pq[:], b["qwb"][:, kt * P:(kt + 1) * P],
                                   b["xnT_own"][:, kt * 1024 + nb * 512: kt * 1024 + (nb + 1) * 512],
                                   start=(kt == 0), stop=(kt == 15))
                nT = inc(ins, mm)
                qsl.set_free(mm, nT)
                V.wait_ge(mm.h, nT)
                ins = V.tensor_copy(b["qb"][:, nb * 512:(nb + 1) * 512], pq[:])
                nqe = inc(ins, ve)
                bev[nb % 2] = (ctx.sem("ve"), nqe)
            q_done = rope_inplace(b["qb"], b["qcs"][toh][0], b["qcs"][toh][1], 1024)
            b["q_ev"] = q_done
            scb = b["scb"]
            bev = b.setdefault("bankev", {})
            for tq in range(8):
                pmx = b["pmax"]
                for nb in range(8):
                    bk = nb % 4
                    ps = scb[bk]
                    if bk in bev:
                        eng_, val_ = bev[bk]
                        T.wait_ge(eng_.h, val_)
                    if nb == 0 and tq == 0:
                        T.wait_ge(ve.h, max(kb_done, q_done, v_done))
                    ins = T.matmul(ps[:], b["qb"][:, tq * P:(tq + 1) * P],
                                   b["kb"][:, nb * 512:(nb + 1) * 512], start=True, stop=True)
                    nT = inc(ins, mm)
                    dst = b["sc_sb"][:, nb * 512:(nb + 1) * 512]
                    if nb == 0 and b.get("sm_done"):
                        S.wait_ge(ve.h, b["sm_done"])
                        V.wait_ge(ve.h, b["sm_done"])
                    if nb % 2 == 0:
                        S.wait_ge(mm.h, nT)
                        nS = inc(S.activation(dst, ps[:], AF.Identity, scale=SCALE), se)
                        bev[bk] = (ctx.sem("se"), nS)
                        V.wait_ge(se.h, nS)
                    else:
                        V.wait_ge(mm.h, nT)
                        nVe = inc(V.tensor_scalar(dst, ps[:], SCALE, None, ALU.mult), ve)
                        bev[bk] = (ctx.sem("ve"), nVe)
                        V.wait_ge(ve.h, nVe)
                    nVp = inc(V.reduce_max(pmx[:, nb:nb + 1], dst, axis=AX.X), ve)
                    V.wait_ge(ve.h, nVp)
                vch(V.reduce_max(b["smx"][:], pmx[:, 0:8], axis=AX.X))
                nV = vch(V.tensor_scalar(b["smx"][:], b["smx"][:], -1.0, None, ALU.mult))
                S.wait_ge(ve.h, nV)
                for half in range(2):
                    if b.get("ab_free"):
                        S.wait_ge(mm.h, b["ab_free"])
                    nS = inc(S.activation(b["ab"][:, half * 2048:(half + 1) * 2048],
                                          b["sc_sb"][:, half * 2048:(half + 1) * 2048], AF.Exp,
                                          bias=b["smx"][:],
                                          accum_out=(b["sden"][:] if half == 0 else b["sden2"][:])), se)
                    T.wait_ge(se.h, nS)
                    T.wait_ge(frp.h, lnb["psbf_busy"])
                    for j in range(16):
                        ins = T.transpose(lnb["psbf"][:, j * P:(j + 1) * P],
                                          b["ab"][:, (half * 16 + j) * P:(half * 16 + j + 1) * P],
                                          lnb["ident_bf"][:])
                    nT = inc(ins, mm)
                    V.wait_ge(mm.h, nT)
                    if b.get("abT_war"):
                        V.wait_ge(mm.h, b["abT_war"])
                    for j in range(16):
                        ins = V.tensor_copy(b["abT"][:, (half * 16 + j) * P:(half * 16 + j + 1) * P],
                                            lnb["psbf"][:, j * P:(j + 1) * P])
                    nfrp = inc(ins, frp)
                    lnb["psbf_busy"] = nfrp
                b["ab_free"] = 0
                V.wait_ge(se.h, nS)
                vch(V.tensor_tensor(b["sden"][:], b["sden"][:], b["sden2"][:], ALU.add))
                vch(V.reciprocal(b["srden"][:], b["sden"][:]))
                b["sm_done"] = ctx.sem("ve").n
                abT_done = inc(V.engine_nop(), ve)
                T.wait_ge(ve.h, abT_done)
                if b.get("av_ev"):
                    T.wait_ge(ve.h, b["av_ev"])
                for kt in range(32):
                    ins = T.matmul(b["pav"][:, 0:P], b["abT"][:, kt * P:(kt + 1) * P],
                                   b["v_sb"][:, kt * P:(kt + 1) * P], start=(kt == 0), stop=(kt == 31))
                nT = inc(ins, mm)
                b["ab_free"] = nT
                b["abT_war"] = nT
                V.wait_ge(mm.h, nT)
                if b.get("ao_st"):
                    V.wait_ge(ctx.sem("aost").h, b["ao_st"])
                ins = V.tensor_scalar(b["ao_t"][:], b["pav"][:, 0:P], b["srden"][:], None, ALU.mult)
                nV = inc(ins, ve)
                b["av_ev"] = nV
                G.wait_ge(ve.h, nV)
                aost = ctx.sem("aost")
                b["ao_st"] = inc(G.dma_start(out=b["attn_out"][h, toh * 1024 + tq * P: toh * 1024 + (tq + 1) * P, :],
                                             in_=b["ao_t"][:]), aost, 16)
            b["kb_war"] = ctx.sem("mm").n
    head_done_st = ctx.sem("aost").n

    for tt in range(16):
        asl = b["ao_slot"]
        if tt == 0:
            Y.wait_ge(ctx.sem("aost").h, head_done_st)
        asl.load(Y, [(b["aon"][:, hh * P:(hh + 1) * P], b["attn_out"][hh, tt * P:(tt + 1) * P, :])
                     for hh in range(16)])
        asl.wait(T)
        if lnb["psbf_busy"]:
            T.wait_ge(frp.h, lnb["psbf_busy"])
        for hh in range(16):
            ins = T.transpose(lnb["psbf"][:, hh * P:(hh + 1) * P], b["aon"][:, hh * P:(hh + 1) * P],
                              lnb["ident_bf"][:])
        nT = inc(ins, mm)
        asl.set_free(mm, nT)
        V.wait_ge(mm.h, nT)
        if b.get("aoT_ev"):
            V.wait_ge(mm.h, b["aoT_ev"])
        for hh in range(16):
            ins = V.tensor_copy(b["aoT"][:, hh * P:(hh + 1) * P], lnb["psbf"][:, hh * P:(hh + 1) * P])
        nfrp = inc(ins, frp)
        lnb["psbf_busy"] = nfrp
        aoT_done = inc(V.engine_nop(), ve)
        T.wait_ge(ve.h, aoT_done)
        if b.get("o_ev"):
            T.wait_ge(ve.h, b["o_ev"])
        for kt in range(16):
            osl = b["ow_slot"]
            osl.load(Y, [(b["owb"][:], owT[kt, :, :])])
            osl.wait(T)
            for dc in range(4):
                ins = T.matmul(b["po"][dc][:], b["aoT"][:, kt * P:(kt + 1) * P],
                               b["owb"][:, dc * 512:(dc + 1) * 512],
                               start=(kt == 0), stop=(kt == 15))
            nT = inc(ins, mm)
            osl.set_free(mm, nT)
        b["aoT_ev"] = nT
        xsl = b["xr_slot"]
        xsl.load(Y, [(b["xrm"][dc][:], xv_own.sl(tt * P, (tt + 1) * P, dc * 512, (dc + 1) * 512))
                     for dc in range(4)])
        V.wait_ge(mm.h, nT)
        xsl.wait(V)
        stq = ctx.sem("mlast")
        if stq.n > 0:
            V.wait_ge(stq.h, stq.n)
        for dc in range(4):
            ins = V.tensor_tensor(b["xrw"][dc][:], b["po"][dc][:], b["xrm"][dc][:], ALU.add)
        nV = inc(ins, ve)
        xsl.set_free(ve, nV)
        b["o_ev"] = nV
        G.wait_ge(ve.h, nV)
        for dc in range(4):
            store_x(G, stq, xv_out.sl(tt * P, (tt + 1) * P, dc * 512, (dc + 1) * 512),
                    b["xrw"][dc][:])


def alloc_mla_bufs(ctx, attn_out_dram, pool=None):
    b = {}
    b["kvaT_sb"] = ctx.asb("ml_kvaT", [P, 16 * 512], BF16)
    b["latT"] = ctx.asb("ml_latT", [P, 4 * 4096], BF16)
    b["xnT_own"] = ctx.asb("ml_xnTo", [P, 16 * 1024], BF16)
    b["xnT_t"] = ctx.nc.alloc_sbuf_tensor_at("ml_xnTt", [P, 16 * P], BF16,
                                             offset=ctx.nc.lookup_mloc(b["xnT_own"]).addr)
    b["sq512"] = ctx.asb("ml_sq512", [P, 512], F32)
    b["lat32"] = ctx.asb("ml_lat32", [P, 512], F32)
    b["lat_t"] = ctx.asb("ml_latt", [P, 512], BF16)
    b["wkv_bc"] = ctx.asb("ml_wkvbc", [P, 512], F32)
    for nm in ["ss2", "va2", "lnv2", "rstd2", "smx", "sden", "sden2", "srden"]:
        b[nm] = ctx.asb("ml_" + nm, [P, 1], F32)
    b["pmax"] = ctx.asb("ml_pmax", [P, 8], F32)
    b["kb"] = ctx.asb("ml_kb", [P, 4096], BF16)
    b["qb"] = ctx.asb("ml_qb", [P, 1024], BF16)
    b["v_sb"] = ctx.asb("ml_v", [P, 32 * P], BF16)
    b["ab"] = ctx.asb("ml_ab", [P, 4096], BF16)
    b["abT"] = ctx.asb("ml_abT", [P, 32 * P], BF16)
    b["sc_sb"] = ctx.asb("ml_sc", [P, 4096], F32)
    b["rot"] = ctx.nc.alloc_sbuf_tensor_at("ml_rot", [64, 4096], BF16,
                                           offset=ctx.nc.lookup_mloc(b["kvaT_sb"]).addr + 256)
    b["rt1"] = ctx.asb("ml_rt1", [64, 4096], BF16)
    b["kcos"] = ctx.asb("ml_kcos", [64, 4096], BF16)
    b["ksin"] = ctx.asb("ml_ksin", [64, 4096], BF16)
    b["qcs"] = [[ctx.asb(f"ml_qcos{t}", [64, 1024], BF16), ctx.asb(f"ml_qsin{t}", [64, 1024], BF16)]
                for t in range(2)]
    b["qwb"] = ctx.asb("ml_qwb", [P, 2048], BF16)
    b["kvbb"] = ctx.asb("ml_kvbb", [P, 1024], BF16)
    nc = ctx.nc
    def sb_at(name, shape, dt, donor, off):
        base = nc.lookup_mloc(donor).addr
        return nc.alloc_sbuf_tensor_at(name, shape, dt, offset=base + off)
    b["owb"] = sb_at("ml_owb", [P, 2048], BF16, b["kb"], 0)
    b["aon"] = sb_at("ml_aon", [P, 16 * P], BF16, b["kb"], 4096)
    b["aoT"] = sb_at("ml_aoT", [P, 16 * P], BF16, b["ab"], 0)
    b["ao_t"] = sb_at("ml_aot", [P, P], BF16, b["kvaT_sb"], 0)
    b["xrm"] = [sb_at(f"ml_xrm{t}", [P, 512], F32, b["sc_sb"], t * 2048) for t in range(4)]
    b["xrw"] = [sb_at(f"ml_xrw{t}", [P, 512], F32, b["sc_sb"], 8192 + t * 2048) for t in range(4)]
    b["qw_slot"] = Slot(ctx, "mlqw")
    b["kvb_slot"] = Slot(ctx, "mlkvb")
    b["ow_slot"] = Slot(ctx, "mlow")
    b["ao_slot"] = Slot(ctx, "mlao")
    b["xr_slot"] = Slot(ctx, "mlxr")
    if pool is None:
        b["pkv"] = ctx.ps("ml_pkv", [P, 512], F32)
        b["psc"] = [ctx.ps(f"ml_psc{j}", [P, 512], F32) for j in range(2)]
        b["pav"] = ctx.ps("ml_pav", [P, 512], F32)
        b["po2"] = [ctx.ps(f"ml_po{j}", [P, 512], F32) for j in range(2)]
    else:
        b["pkv"] = pool[0]
        b["psc"] = pool[1:3]
        b["pav"] = pool[3]
        b["po2"] = pool[4:6]
    b["scb"] = [b["psc"][0], b["psc"][1], b["pkv"], b["po2"][0]]
    b["po"] = b["po2"] + [b["pkv"], b["pav"]]
    b["attn_out"] = attn_out_dram
    b["kevac"] = {}
    return b


def build_mla(nc):
    import contextlib
    ctx = Ctx(nc)
    ctx._es = contextlib.ExitStack()
    dr = lambda nm, sh, dt, kind: nc.dram_tensor(nm, sh, dt, kind=kind).ap()
    I, O = "ExternalInput", "ExternalOutput"
    x_full = dr("x_full", [4096, 2048], F32, I)
    x_own = dr("x_own", [2048, 2048], F32, I)
    kvaT = dr("kvaT", [16, P, 512], BF16, I)
    kvbT = dr("kvbT", [32, P, 512], BF16, I)
    qwT = dr("qwT", [16, P, 2048], BF16, I)
    owT = dr("owT", [16, P, 2048], BF16, I)
    kvnw = dr("kvnw", [1, 512], F32, I)
    kcos = dr("kcos", [64, 4096], BF16, I)
    ksin = dr("ksin", [64, 4096], BF16, I)
    qcos = dr("qcos", [64, 2048], BF16, I)
    qsin = dr("qsin", [64, 2048], BF16, I)
    identb = dr("identb", [P, P], BF16, I)
    onesf = dr("onesf", [1, P], F32, I)
    xo = dr("xo", [2048, 2048], F32, O)
    attn_out = nc.dram_tensor("attn_sc", [16, 2048, P], BF16).ap()

    lnb = alloc_ln_bufs(ctx, nx=1)
    lnb["ident_bf"] = ctx.sb("identb_sb", [P, P], BF16)
    lnb["psbf"] = ctx.ps("psbf", [P, 2048], BF16)
    b = alloc_mla_bufs(ctx, attn_out)
    def _sb_at(name, shape, dt, donor, off):
        return nc.alloc_sbuf_tensor_at(name, shape, dt, offset=nc.lookup_mloc(donor).addr + off)
    ones_sb = _sb_at("ml_ones", [1, P], F32, b["aoT"], 0)
    kvnw_sb = _sb_at("ml_kvnw", [1, 512], F32, b["aoT"], 512)

    idl = Slot(ctx, "idl")
    idl.load(ctx.Y, [(lnb["ident_bf"][:], identb), (ones_sb[:], onesf),
                     (kvnw_sb[:], kvnw), (b["kcos"][:], kcos), (b["ksin"][:], ksin),
                     (b["qcs"][0][0][:], qcos[:, 0:1024]), (b["qcs"][0][1][:], qsin[:, 0:1024]),
                     (b["qcs"][1][0][:], qcos[:, 1024:2048]), (b["qcs"][1][1][:], qsin[:, 1024:2048]),
                     (b["kvaT_sb"][:], kvaT.rearrange("k p f -> p k f"))])
    idl.wait(ctx.T)
    mm = ctx.sem("mm"); ve = ctx.sem("ve")
    nT = inc(ctx.T.matmul(b["pkv"][:], ones_sb[0:1, :], kvnw_sb[:], start=True, stop=True), mm)
    ctx.V.wait_ge(mm.h, nT)
    nV = inc(ctx.V.tensor_copy(b["wkv_bc"][:], b["pkv"][:]), ve)
    ctx.V.wait_ge(ve.h, nV)

    emit_mla_body(ctx, XV(ap=x_full), XV(ap=x_own), XV(ap=xo), kvaT, kvbT, qwT, owT, lnb, b)
    s = ctx.sem("mlast")
    ctx.G.wait_ge(s.h, s.n)
    finalize(ctx)
    ctx._es.close()
    return nc


def build_fused(nc):
    import contextlib, os
    STOP = int(os.environ.get("FUSED_STOP_AFTER", "6"))
    ctx = Ctx(nc)
    ctx._es = contextlib.ExitStack()
    dr = lambda nm, sh, dt, kind, **kw: nc.dram_tensor(nm, sh, dt, kind=kind, **kw).ap()
    I, O = "ExternalInput", "ExternalOutput"
    x_in = dr("x_in", [2048, 2048], F32, I)
    slot = dr("slot", [1, 1], I32, I)
    kconst = dr("kconst", [1, 16], I32, I)
    gidx = dr("gidx", [5504, 2048], I32, I)
    uidx = dr("uidx", [5504, 2048], I32, I)
    didx = dr("didx", [2048, 5504], I32, I)
    lut = dr("lut", [16], F32, I)
    wk = dr("wk", [2048, 2048], F32, I); wv = dr("wv", [2048, 2048], F32, I)
    wr = dr("wr", [2048, 2048], F32, I); wo = dr("wo", [2048, 2048], F32, I)
    qw = dr("qw", [2048, 2048], F32, I); kva = dr("kva", [576, 2048], F32, I)
    kvb = dr("kvb", [4096, 512], F32, I); ow = dr("ow", [2048, 2048], F32, I)
    mg = dr("mg", [8, 2048, 1408], F32, I); mu = dr("mu", [8, 2048, 1408], F32, I)
    md = dr("md", [8, 1408, 2048], F32, I)
    swg = dr("swg", [2048, 2816], F32, I); swu = dr("swu", [2048, 2816], F32, I)
    swd = dr("swd", [2816, 2048], F32, I)
    gw = dr("gw", [2048, 8], F32, I)
    identb = dr("identb", [P, P], BF16, I)
    identf = dr("identf", [P, P], F32, I)
    onesf = dr("onesf", [1, P], F32, I)
    sel8 = dr("sel8", [8, 8 * P], F32, I)
    td = dr("td", [P, 16], F32, I); tf = dr("tf", [P, 16], F32, I)
    mkc = dr("mkc", [P, 16], F32, I); mvc = dr("mvc", [P, 16], F32, I)
    mrc = dr("mrc", [P, 16], F32, I)
    warmb = dr("warmb", [P, 1], F32, I)
    kvnw = dr("kvnw", [1, 512], F32, I)
    kcos = dr("kcos", [64, 4096], BF16, I); ksin = dr("ksin", [64, 4096], BF16, I)
    qcos = dr("qcos", [64, 2048], BF16, I); qsin = dr("qsin", [64, 2048], BF16, I)
    xo_ext = dr("xo", [2048, 2048], F32, O)
    SH = {"addr_space": "Shared"}
    wgt = dr("wgt", [43, P, 2048], BF16, "Internal", **SH)
    wut = dr("wut", [43, P, 2048], BF16, "Internal", **SH)
    wdt = dr("wdt", [43, P, 2048], BF16, "Internal", **SH)
    wkT = dr("wkT", [16, P, 2048], BF16, "Internal", **SH)
    wvT = dr("wvT", [16, P, 2048], BF16, "Internal", **SH)
    wrT = dr("wrT", [16, P, 2048], BF16, "Internal", **SH)
    woT = dr("woT", [16, P, 2048], BF16, "Internal", **SH)
    qwT = dr("qwT", [16, P, 2048], BF16, "Internal", **SH)
    kvaT = dr("kvaT", [16, P, 512], BF16, "Internal", **SH)
    kvbT = dr("kvbT", [32, P, 512], BF16, "Internal", **SH)
    owT = dr("owT", [16, P, 2048], BF16, "Internal", **SH)
    mgb = dr("mgb", [8, 2048, 1408], BF16, "Internal", **SH)
    mub = dr("mub", [8, 2048, 1408], BF16, "Internal", **SH)
    mdb = dr("mdb", [8, 1408, 2048], BF16, "Internal", **SH)
    swgb = dr("swgb", [2048, 2816], BF16, "Internal", **SH)
    swub = dr("swub", [2048, 2816], BF16, "Internal", **SH)
    swdb = dr("swdb", [2816, 2048], BF16, "Internal", **SH)
    attn_out = dr("attn_sc", [16, 2048, P], BF16, "Internal")
    xs0 = dr("xs0", [4608, 2048], F32, "Internal", addr_space="Shared")
    xs1 = dr("xs1", [4608, 2048], F32, "Internal", addr_space="Shared")
    flags = dr("flags", [2, 16], I32, "Internal", addr_space="Shared")

    T, V, S, G, Y = ctx.T, ctx.V, ctx.S, ctx.G, ctx.Y

    lnb = alloc_ln_bufs(ctx, nx=1)
    lnb["ident_bf"] = ctx.sb("identb_sb", [P, P], BF16)
    lnb["psbf"] = ctx.ps("psbf", [P, 2048], BF16)
    pool = [ctx.ps(f"pf{j}", [P, 512], F32) for j in range(6)]
    ident_f = ctx.sb("identf_sb", [P, P], F32)
    ones_sb = ctx.sb("ones_sb", [1, P], F32)
    sel8_sb = ctx.sb("sel8_sb", [8, 8 * P], F32)
    kconst_sb = ctx.sb("kconst_sb", [1, 16], I32)
    pollbuf = ctx.sb("pollbuf", [1, 2], I32)
    gw_f = ctx.sb("gw_f", [P, 16 * 8], F32)
    gwb = ctx.sb("gwb", [P, 16 * 8], BF16)
    gwlo = ctx.sb("gwlo", [P, 16 * 8], BF16)
    lut_sb = ctx.sb("lut_sb", [P, 16], F32)
    emit_slot_regs(ctx, slot)
    avail = nc.sbuf_top - nc.sbuf_base - 64
    avail = (avail // 32) * 32
    arena = Arena(ctx, avail)
    ctx.arena = arena
    ve = ctx.sem("ve"); se = ctx.sem("se"); mm = ctx.sem("mm")

    idl = Slot(ctx, "idl")
    idl.load(Y, [(lnb["ident_bf"][:], identb), (ident_f[:], identf), (ones_sb[:], onesf),
                 (sel8_sb[:], sel8), (kconst_sb[:], kconst),
                 (gw_f[:], gw.rearrange("(k p) e -> p k e", p=P))])
    idl.wait(V)
    nV = inc(V.tensor_copy(gwb[:], gw_f[:]), ve)
    V.wait_ge(ve.h, nV)
    nV = inc(V.tensor_tensor(gwlo[:], gw_f[:], gwb[:], ALU.subtract), ve)
    V.wait_ge(ve.h, nV)
    idl.wait(T)
    idl.wait(S)

    zinit = ctx.sem("zinit")
    nV = inc(V.memset(lnb["sq"][:], 0.0), ve)
    G.wait_ge(ve.h, nV)
    for i in range(4):
        inc(G.dma_start(out=xs0[i * P:(i + 1) * P, :], in_=lnb["sq"][:]), zinit, 16)
    G.wait_ge(zinit.h, zinit.n)

    sb_ = alloc_stage_bufs(ctx)
    sb_["ident_bf"] = lnb["ident_bf"]
    sb_["psbf"] = lnb["psbf"]
    nV = emit_lut_bcast(ctx, lut, lut_sb, ones_sb, pool[0])
    V.wait_ge(ctx.sem("ve").h, nV)
    def _spl(n):
        a = (n + 1) // 2
        return list(range(a)), [min(a + i, n - 1) for i in range(a)]
    gA, gB = _spl(43)
    emit_stage_ghost_lhsT(ctx, gidx, wgt, lut_sb, sb_, gA, gB)
    emit_stage_ghost_lhsT(ctx, uidx, wut, lut_sb, sb_, gA, gB)
    dA, dB = _spl(16)
    emit_stage_ghost_rhs(ctx, didx, wdt, lut_sb, sb_, dA, dB)
    tA, tB = _spl(16)
    emit_stage_transpose(ctx, wk, 2048, 2048, wkT, "blk", sb_, tA, slab_range2=tB)
    emit_stage_transpose(ctx, wv, 2048, 2048, wvT, "blk", sb_, tA, slab_range2=tB)
    emit_stage_transpose(ctx, wr, 2048, 2048, wrT, "blk", sb_, tA, slab_range2=tB)
    emit_stage_transpose(ctx, wo, 2048, 2048, woT, "rowmajor", sb_, tA, slab_range2=tB)
    emit_stage_transpose(ctx, qw, 2048, 2048, qwT, "blk", sb_, tA, slab_range2=tB)
    kA, kB = _spl(4)
    emit_stage_transpose(ctx, kva, 512, 2048, kvaT, "rowmajor", sb_, kA, src_row0=64, slab_range2=kB)
    vA, vB = _spl(32)
    emit_stage_transpose(ctx, kvb, 4096, 512, kvbT, "blk", sb_, vA, slab_range2=vB)
    emit_stage_transpose(ctx, ow, 2048, 2048, owT, "rowmajor", sb_, tA, slab_range2=tB)
    cA16, cB16 = _spl(16)
    cA11, cB11 = _spl(11)
    cA22, cB22 = _spl(22)
    for e in range(8):
        emit_stage_cast(ctx, mg[e], 2048, 1408, mgb[e], sb_, cA16, cB16)
        emit_stage_cast(ctx, mu[e], 2048, 1408, mub[e], sb_, cA16, cB16)
        emit_stage_cast(ctx, md[e], 1408, 2048, mdb[e], sb_, cA11, cB11)
    emit_stage_cast(ctx, swg, 2048, 2816, swgb, sb_, cA16, cB16)
    emit_stage_cast(ctx, swu, 2048, 2816, swub, sb_, cA16, cB16)
    emit_stage_cast(ctx, swd, 2816, 2048, swdb, sb_, cA22, cB22)
    for s in sb_["stg"] + sb_["stgc"]:
        G.wait_ge(s.h, s.n)
    emit_flag_write(ctx, flags, kconst_sb, 3)
    v3 = emit_flag_poll(ctx, flags, 3, pollbuf, tag="st")
    Y.wait_ge(ctx.sems["barok"].h, v3)
    emit_join(ctx)
    arena.reset()

    ffn_b = alloc_ffn0_bufs(ctx, pool=pool)
    emit_ffn0_body(ctx, XV(ap=x_in), XV(shared_ap=xs0), wgt, wut, wdt, 2048, 512, 0.5,
                   lnb, b=ffn_b)
    for pn in ("stx0", "stx1"):
        s = ctx.sem(pn)
        G.wait_ge(s.h, s.n)
    emit_flag_write(ctx, flags, kconst_sb, 1)
    emit_join(ctx)
    arena.reset()

    def dump_and_stop(src_xv):
        dsem = ctx.sem("dumpst")
        dsl = Slot(ctx, "dumpld")
        for tt in range(16):
            dsl.load(Y, [(lnb["xa"][0][:], src_xv.sl(tt * P, (tt + 1) * P))])
            dsl.wait(V)
            if dsem.n > 0:
                V.wait_ge(dsem.h, dsem.n)
            nV = inc(V.tensor_copy(lnb["sq"][:], lnb["xa"][0][:]), ctx.sem("ve"))
            dsl.set_free(ctx.sem("ve"), nV)
            G.wait_ge(ctx.sem("ve").h, nV)
            ins = G.dma_start(out=xo_ext[tt * P:(tt + 1) * P, :], in_=lnb["sq"][:])
            inc(ins, dsem, 16)
            G.wait_ge(dsem.h, dsem.n)
        finalize(ctx)
        ctx._es.close()
        return ctx

    if STOP == 1:
        return dump_and_stop(XV(shared_ap=xs0))

    rw_b = alloc_rwkv_bufs(ctx, pool=pool)
    v1 = emit_flag_poll(ctx, flags, 1, pollbuf, tag="a")
    Y.wait_ge(ctx.sems["barok"].h, v1)
    emit_rwkv_consts(ctx, rw_b, td, tf, mkc, mvc, mrc, warmb)

    def rflag_cb():
        for j in range(len(lnb["xa_slot"])):
            sl = lnb["xa_slot"][j]
            G.wait_ge(sl.ld.h, sl.ld.n)
        emit_flag_write(ctx, flags, kconst_sb, 2)
    emit_rwkv_body(ctx, XV(shared_ap=xs0, ext=True), XV(shared_ap=xs1), wkT, wvT, wrT, woT,
                   lnb, rw_b, after_chunk0_ln=rflag_cb)
    s = ctx.sem("rwst0")
    G.wait_ge(s.h, s.n)
    emit_join(ctx)
    arena.reset()

    if STOP == 2:
        return dump_and_stop(XV(shared_ap=xs1))

    v2 = emit_flag_poll(ctx, flags, 2, pollbuf, tag="b")
    G.wait_ge(ctx.sems["barok"].h, v2)
    ffn_b2 = alloc_ffn0_bufs(ctx, pool=pool)
    emit_ffn0_body(ctx, XV(shared_ap=xs1), XV(shared_ap=xs0), wgt, wut, wdt, 2048, 512, 0.5,
                   lnb, b=ffn_b2)
    for pn in ("stx0", "stx1"):
        s = ctx.sem(pn)
        G.wait_ge(s.h, s.n)
    emit_join(ctx)
    arena.reset()

    if STOP == 3:
        return dump_and_stop(XV(shared_ap=xs0))

    moe_b = alloc_moe_bufs(ctx, pool=pool)
    moe_b["ident_f"] = ident_f
    moe_b["ones_f"] = ones_sb
    moe_b["sel8"] = sel8_sb
    emit_moe_body(ctx, XV(shared_ap=xs0), XV(shared_ap=xs1), mgb, mub, mdb, swgb, swub, swdb,
                  (gwb, gwlo), lnb, moe_b)
    for pp in (0, 1):
        s = ctx.sem("moestp" + str(pp))
        G.wait_ge(s.h, s.n)
    emit_flag_write(ctx, flags, kconst_sb, 4)
    emit_join(ctx)
    arena.reset()

    if STOP == 4:
        return dump_and_stop(XV(shared_ap=xs1))

    v4 = emit_flag_poll(ctx, flags, 4, pollbuf, tag="c")
    Y.wait_ge(ctx.sems["barok"].h, v4)
    mla_b = alloc_mla_bufs(ctx, attn_out, pool=pool)
    kvnw_sb = nc.alloc_sbuf_tensor_at("ml_kvnw", [1, 512], F32,
                                      offset=nc.lookup_mloc(mla_b["aoT"]).addr)
    mla_sl = Slot(ctx, "mlconst")
    mla_sl.load(Y, [(kvnw_sb[:], kvnw), (mla_b["kcos"][:], kcos), (mla_b["ksin"][:], ksin),
                    (mla_b["qcs"][0][0][:], qcos[:, 0:1024]), (mla_b["qcs"][0][1][:], qsin[:, 0:1024]),
                    (mla_b["qcs"][1][0][:], qcos[:, 1024:2048]), (mla_b["qcs"][1][1][:], qsin[:, 1024:2048]),
                    (mla_b["kvaT_sb"][:], kvaT.rearrange("k p f -> p k f"))])
    mla_sl.wait(T)
    nT = inc(T.matmul(mla_b["pkv"][:], ones_sb[0:1, :], kvnw_sb[:], start=True, stop=True), mm)
    V.wait_ge(mm.h, nT)
    nV = inc(V.tensor_copy(mla_b["wkv_bc"][:], mla_b["pkv"][:]), ve)
    V.wait_ge(ve.h, nV)
    emit_mla_body(ctx, XV(ap=xs1, row0=512), XV(shared_ap=xs1), XV(shared_ap=xs0),
                  kvaT, kvbT, qwT, owT, lnb, mla_b)
    s = ctx.sem("mlast")
    G.wait_ge(s.h, s.n)
    emit_join(ctx)
    arena.reset()

    if STOP == 5:
        return dump_and_stop(XV(shared_ap=xs0))

    moe_b2 = alloc_moe_bufs(ctx, pool=pool)
    moe_b2["ident_f"] = ident_f
    moe_b2["ones_f"] = ones_sb
    moe_b2["sel8"] = sel8_sb
    emit_moe_body(ctx, XV(shared_ap=xs0), XV(ap=xo_ext), mgb, mub, mdb, swgb, swub, swdb,
                  (gwb, gwlo), lnb, moe_b2)
    for pp in (0, 1):
        s = ctx.sem("moestp" + str(pp))
        G.wait_ge(s.h, s.n)
    finalize(ctx)
    ctx._es.close()
    return ctx



import math as _math

def _yarn_tables():
    RD = 64
    MAX_POS, ORIG_POS, BASE = 131072, 8192, 500000.0
    scaling = MAX_POS / ORIG_POS
    pos_freqs = BASE ** (np.arange(0, RD, 2, dtype=np.float64) / RD)
    extrap = 1.0 / pos_freqs
    interp = 1.0 / (scaling * pos_freqs)
    low = max(_math.floor(RD * _math.log(ORIG_POS / (32.0 * 2 * _math.pi)) / (2 * _math.log(BASE))), 0)
    high = min(_math.ceil(RD * _math.log(ORIG_POS / (1.0 * 2 * _math.pi)) / (2 * _math.log(BASE))), RD - 1)
    i = np.arange(RD // 2, dtype=np.float64)
    smooth = np.clip((i - low) / max(high - low, 1), 0.0, 1.0)
    inv_freq = (1.0 - smooth) * interp + smooth * extrap
    pos = np.arange(4096, dtype=np.float64)
    freqs = pos[:, None] * inv_freq[None, :]
    emb = np.concatenate([freqs, freqs], axis=-1)
    mscale = 0.1 * _math.log(scaling) + 1.0
    return np.cos(emb) * mscale, np.sin(emb) * mscale


def kernel(**inputs):
    from concourse.bass_utils import run_bass_kernel_spmd
    bfc = lambda a: np.ascontiguousarray(a).astype(ml_dtypes.bfloat16)
    f32 = lambda a: np.ascontiguousarray(np.asarray(a), dtype=np.float32)
    identb = np.eye(P, dtype=ml_dtypes.bfloat16)
    identf = np.eye(P, dtype=np.float32)
    onesf = np.ones((1, P), np.float32)
    sel8 = np.concatenate([(np.arange(8)[:, None] == e).astype(np.float32) * np.ones((8, P), np.float32)
                           for e in range(8)], axis=1)
    pt = lambda a: np.ascontiguousarray(np.asarray(a, dtype=np.float32).reshape(16, P).T)
    cos_t, sin_t = _yarn_tables()

    x = f32(inputs["L0_attn_x"]).reshape(4, 4096, 2048)
    nc = bass.Bass("TRN2", target_bir_lowering=False, debug=False)
    build_fused(nc)
    common = {
        "kconst": flag_const_array(),
        "gidx": np.ascontiguousarray(inputs["L0_ffn_gate_idx"], dtype=np.int32),
        "uidx": np.ascontiguousarray(inputs["L0_ffn_up_idx"], dtype=np.int32),
        "didx": np.ascontiguousarray(inputs["L0_ffn_down_idx"], dtype=np.int32),
        "lut": f32(inputs["L0_ffn_lut"]),
        "wk": f32(inputs["L0_attn_Wk"]), "wv": f32(inputs["L0_attn_Wv"]),
        "wr": f32(inputs["L0_attn_Wr"]), "wo": f32(inputs["L0_attn_Wo"]),
        "qw": f32(inputs["L1_attn_q_w"]), "kva": f32(inputs["L1_attn_kv_a_w"]),
        "kvb": f32(inputs["L1_attn_kv_b_w"]), "ow": f32(inputs["L1_attn_o_w"]),
        "mg": f32(inputs["L1_ffn_w_gate"]), "mu": f32(inputs["L1_ffn_w_up"]),
        "md": f32(inputs["L1_ffn_w_down"]),
        "swg": f32(inputs["L1_ffn_sw_gate"]), "swu": f32(inputs["L1_ffn_sw_up"]),
        "swd": f32(inputs["L1_ffn_sw_down"]), "gw": f32(inputs["L1_ffn_gate_w"]),
        "identb": identb, "identf": identf, "onesf": onesf, "sel8": sel8,
        "td": pt(inputs["L0_attn_time_decay"]), "tf": pt(inputs["L0_attn_time_first"]),
        "mkc": pt(np.asarray(inputs["L0_attn_time_mix_k"]).reshape(2048)),
        "mvc": pt(np.asarray(inputs["L0_attn_time_mix_v"]).reshape(2048)),
        "mrc": pt(np.asarray(inputs["L0_attn_time_mix_r"]).reshape(2048)),
        "kvnw": f32(inputs["L1_attn_kv_norm_w"]).reshape(1, 512),
        "kcos": bfc(cos_t.T), "ksin": bfc(sin_t.T),
    }
    ims = []
    for c in range(8):
        bidx, h = c // 2, c % 2
        im = dict(common)
        im["x_in"] = np.ascontiguousarray(x[bidx, h * 2048:(h + 1) * 2048])
        im["slot"] = np.array([[h]], np.int32)
        im["warmb"] = (np.full((P, 1), -1e30, np.float32) if h == 0
                       else np.zeros((P, 1), np.float32))
        im["qcos"] = bfc(cos_t[h * 2048:(h + 1) * 2048].T)
        im["qsin"] = bfc(sin_t[h * 2048:(h + 1) * 2048].T)
        ims.append(im)
    res = run_bass_kernel_spmd(nc, ims, core_ids=list(range(8)))
    out = np.empty((4, 4096, 2048), np.float32)
    for c in range(8):
        out[c // 2, (c % 2) * 2048:(c % 2 + 1) * 2048] = \
            np.asarray(res.results[c]["xo"]).reshape(2048, 2048)
    return out
